# version 04
# baseline (speedup 1.0000x reference)
import numpy as np
import ml_dtypes
from contextlib import ExitStack

import concourse.bass as bass
import concourse.mybir as mybir
from concourse.bass_utils import run_bass_kernel_spmd

F32 = mybir.dt.float32
BF16 = mybir.dt.bfloat16
AF = mybir.ActivationFunctionType
ALU = mybir.AluOpType
AX = mybir.AxisListType

D = 1024
T = 2048
HID = 4096
TT = 512
NTT = T // TT
NIC = D // 128
EPS = 1e-5

RN_H, RN_DK, RN_DV, CHUNK = 4, 256, 512, 128
W_H, W_DK, W_WIN = 16, 64, 64


def _hilo(x64):
    hi = x64.astype(np.float32)
    lo = (x64 - hi.astype(np.float64)).astype(np.float32)
    return hi, lo


def build_consts():
    c = {}
    c["ident_f32"] = np.eye(128, dtype=np.float32)

    j = np.arange(0, RN_DK, 2, dtype=np.float64) / RN_DK
    inv_freq = 1.0 / (10000.0 ** j)
    ang = inv_freq[:, None] * np.arange(T, dtype=np.float64)[None, :]
    c["rope_cos_hi"], c["rope_cos_lo"] = _hilo(np.cos(ang))
    c["rope_sin_hi"], c["rope_sin_lo"] = _hilo(np.sin(ang))

    r = np.arange(CHUNK, dtype=np.float64)
    maskt = np.zeros((RN_H * 128, CHUNK))
    qsc = np.zeros((RN_H * 128, CHUNK))
    ksc = np.zeros((RN_H * 128, 1))
    gC = np.zeros((RN_H * 128, 1))
    for h in range(RN_H):
        lg = float(np.log1p(-np.exp2(-5.0 - h)))
        diff = r[None, :] - r[:, None]
        maskt[h * 128:(h + 1) * 128, :] = np.where(diff >= 0, np.exp(lg * diff) / 16.0, 0.0)
        qsc[h * 128:(h + 1) * 128, :] = np.exp(lg * (r[None, :] + 1.0)) / 16.0
        ksc[h * 128:(h + 1) * 128, 0] = np.exp(lg * (CHUNK - 1.0 - r))
        gC[h * 128:(h + 1) * 128, 0] = np.exp(lg * CHUNK)
    c["rn_maskt_hi"], c["rn_maskt_lo"] = _hilo(maskt)
    c["rn_qsc_hi"], c["rn_qsc_lo"] = _hilo(qsc)
    c["rn_ksc_hi"], c["rn_ksc_lo"] = _hilo(ksc)
    c["rn_gc_hi"], c["rn_gc_lo"] = _hilo(gC)

    NEG = -1e30
    mi = np.full((128, 384), NEG)
    for rr in range(128):
        mi[rr, 96 + rr:160 + rr] = 0.0
    c["w_mask_int_hi"], c["w_mask_int_lo"] = _hilo(mi)
    mf = np.full((128, 256), NEG)
    for rr in range(128):
        lo_k, hi_k = rr - 32, min(rr + 31, 255)
        if lo_k < 0:
            mf[rr, 0] = np.log(float(1 - lo_k))
            mf[rr, 1:hi_k + 1] = 0.0
        else:
            mf[rr, lo_k:hi_k + 1] = 0.0
    c["w_mask_first_hi"], c["w_mask_first_lo"] = _hilo(mf)
    ml = np.full((128, 256), NEG)
    for rr in range(128):
        t = 1920 + rr
        lo_k, hi_k = t - 32 - 1792, t + 31 - 1792
        if hi_k > 255:
            ml[rr, 255] = np.log(float(hi_k - 255 + 1))
            ml[rr, lo_k:255] = 0.0
        else:
            ml[rr, lo_k:hi_k + 1] = 0.0
    c["w_mask_last_hi"], c["w_mask_last_lo"] = _hilo(ml)
    return c


_CONSTS = build_consts()


class Eng:
    def __init__(self, name, sem):
        self.name = name
        self.sem = sem
        self.val = 0


class Ctx:
    def __init__(self, nc, es: ExitStack):
        self.nc = nc
        self.es = es
        E = es.enter_context
        self.pe = Eng("pe", E(nc.semaphore("pe_s")))
        self.dve = Eng("dve", E(nc.semaphore("dve_s")))
        self.act = Eng("act", E(nc.semaphore("act_s")))
        self.pool = Eng("pool", E(nc.semaphore("pool_s")))
        self.ld_sp = Eng("ld_sp", E(nc.semaphore("ld_sp_s")))
        self.ld_act = Eng("ld_act", E(nc.semaphore("ld_act_s")))
        self.ldg = Eng("ldg", E(nc.semaphore("ldg_s")))
        self.st = Eng("st", E(nc.semaphore("st_s")))
        self.block = E(nc.Block())
        self._waited = {}

    def wait(self, q, qname, eng, val):
        if val <= 0:
            return
        assert val <= eng.val, (
            f"wait for {eng.name}>={val} but only {eng.val} scheduled (hang)")
        key = (qname, eng.name)
        if self._waited.get(key, 0) >= val:
            return
        q.wait_ge(eng.sem, val)
        self._waited[key] = val

    def inc(self, eng, instr, n=1):
        instr.then_inc(eng.sem, n)
        eng.val += n
        return eng.val

    def new_dma_sem(self, name):
        return Eng(name, self.es.enter_context(self.nc.semaphore(name)))

    def wait_list(self, q, qname, deps):
        for (eng, val) in deps:
            self.wait(q, qname, eng, val)


def sb(cx, es, name, shape, dtype):
    return es.enter_context(cx.nc.sbuf_tensor(name, shape, dtype))


def ps(cx, es, name, shape, dtype=F32):
    return es.enter_context(cx.nc.psum_tensor(name, shape, dtype))


def alloc_common(cx, es):
    r = {}
    r["x"] = sb(cx, es, "x_res", [128, NIC * T], F32)
    r["scrA"] = sb(cx, es, "scrA", [128, NIC * TT], BF16)
    r["scrB"] = sb(cx, es, "scrB", [128, NIC * TT], BF16)
    r["ln_rows"] = sb(cx, es, "ln_rows", [1, 2 * TT], F32)
    r["ones_bf"] = sb(cx, es, "ones_bf", [128, 1], BF16)
    r["ones_f1"] = sb(cx, es, "ones_f1", [1, 128], F32)
    r["eps_t"] = sb(cx, es, "eps_t", [1, 1], F32)
    r["id32"] = sb(cx, es, "id32", [128, 128], F32)
    r["id_bf"] = sb(cx, es, "id_bf", [128, 128], BF16)
    r["x_stg"] = sb(cx, es, "x_stg", [128, D], F32)
    r["ln_tmp"] = r["x_stg"]
    r["ps_mm"] = [ps(cx, es, f"psb{i}", [128, 512]) for i in range(8)]
    r["bank_free"] = [(cx.pe, 0)] * 8
    r["ld_pro"] = cx.new_dma_sem("ld_pro")
    r["ld_id"] = cx.new_dma_sem("ld_id")
    r["ld_x"] = cx.new_dma_sem("ld_x")
    r["_w1_act"] = {}
    r["_ln_pe_prev"] = 0

    @cx.block.gpsimd
    def _(gp):
        cx.inc(cx.pool, gp.memset(r["ones_bf"][:, :], 1.0))
        cx.inc(cx.pool, gp.memset(r["ones_f1"][:, :], 1.0))
        cx.inc(cx.pool, gp.memset(r["eps_t"][:, :], EPS))
    r["ones_dep"] = cx.pool.val
    return r


class PrepassStream:

    def __init__(self, cx, es, r, plan, rings):
        self.cx = cx
        self.r = r
        self.rings = rings
        self.nring = len(rings)
        self.ring_cols = rings[0].shape[1]
        self.ppl = [cx.new_dma_sem(f"ppl{i}") for i in range(self.nring)]
        self.pps = [cx.new_dma_sem(f"pps{i}") for i in range(self.nring)]
        self.ring_free = [0] * self.nring
        self.steps = []
        self.cast_done = {}
        self.cast_info = {}
        self.worder = []
        self.done = {}
        self.idx = 0
        self.ring_i = 0
        for (name, w, scr1, scr2) in plan:
            O, I = w.shape
            self.cast_info[name] = (w, scr1, O, I)
            self.worder.append(name)
            for ic in range(I // 128):
                for o0 in range(0, O, self.ring_cols):
                    o1 = min(O, o0 + self.ring_cols)
                    self.steps.append((name, scr1, scr2, ic, o0, o1))

    def _ensure_cast(self, name):
        if name in self.cast_done:
            return
        cx = self.cx
        w, scr1, O, I = self.cast_info[name]
        ldg_w = cx.new_dma_sem("ldg_" + name)

        @cx.block.gpsimd
        def _(gp, w=w, scr1=scr1, O=O, I=I, ldg_w=ldg_w):
            rows_per = max(128, min(O, (8 << 20) // (I * 2)))
            r0 = 0
            while r0 < O:
                r1 = min(O, r0 + rows_per)
                cx.inc(ldg_w, gp.dma_start(out=scr1[r0:r1, :], in_=w[r0:r1, :]), 16)
                r0 = r1
        self.cast_done[name] = (ldg_w, ldg_w.val)

    def step(self, k=1):
        cx = self.cx
        for _ in range(k):
            if self.idx >= len(self.steps):
                return
            name, scr1, scr2, ic, o0, o1 = self.steps[self.idx]
            self.idx += 1
            self._ensure_cast(name)
            wi = self.worder.index(name)
            if wi + 1 < len(self.worder):
                self._ensure_cast(self.worder[wi + 1])
            b = self.ring_i % self.nring
            self.ring_i += 1
            ldg_w, cval = self.cast_done[name]

            @cx.block.sync
            def _(q, scr1=scr1, ic=ic, b=b, o0=o0, o1=o1, ldg_w=ldg_w, cval=cval):
                cx.wait(q, "sync", ldg_w, cval)
                cx.wait(q, "sync", self.pps[b], self.ring_free[b])
                for r0 in range(o0, o1, 1024):
                    r1 = min(o1, r0 + 1024)
                    cx.inc(self.ppl[b], q.dma_start(
                        out=self.rings[b][:, r0 - o0:r1 - o0],
                        in_=scr1[r0:r1, ic * 128:(ic + 1) * 128],
                        transpose=True), 16)
            ldv = self.ppl[b].val

            @cx.block.gpsimd
            def _(q, scr2=scr2, ic=ic, b=b, ldv=ldv, o0=o0, o1=o1):
                cx.wait(q, "gpsimd", self.ppl[b], ldv)
                cx.inc(self.pps[b], q.dma_start(out=scr2[ic * 128:(ic + 1) * 128, o0:o1],
                                                in_=self.rings[b][:, 0:o1 - o0]), 16)
                self.ring_free[b] = self.pps[b].val
            if self.idx == len(self.steps) or self.steps[self.idx][0] != name:
                self.done[name] = [(sem, sem.val) for sem in self.pps]

    def drain(self, name):
        while name not in self.done and self.idx < len(self.steps):
            self.step()

    def drain_all(self):
        while self.idx < len(self.steps):
            self.step()

    def wdone(self, name):
        assert name in self.done, f"prepass for {name} not drained"
        return self.done[name]


def emit_prepass(cx, es, r, plan, rings=None):
    if rings is None:
        rings = [sb(cx, es, "pp_wt0", [128, 1024], BF16)[:],
                 sb(cx, es, "pp_wt1", [128, 1024], BF16)[:],
                 sb(cx, es, "pp_wt2", [128, 1024], BF16)[:],
                 sb(cx, es, "pp_wt3", [128, 1024], BF16)[:]]
    pp = PrepassStream(cx, es, r, plan, rings)
    pp.drain_all()
    r["_pp_all"] = [(sem, sem.val) for sem in pp.pps]
    return {name: pp.wdone(name) for (name, _, _, _) in plan}


def emit_load_scale_fm(cx, es, r, name, s_ap, n, out=None):
    g = n // 128
    if "scale_nat" not in r:
        r["scale_nat"] = sb(cx, es, "scale_nat", [32, 128], F32)
    nat = r["scale_nat"][0:g, :]
    if out is None:
        out = sb(cx, es, name + "_fm", [128, g], F32)
    pt = r["ps_mm"][4]
    ld_pro = r["ld_pro"]

    @cx.block.sync
    def _(q):
        cx.wait(q, "sync", cx.pe, r.get("_pro_pe_prev", 0))
        cx.wait(q, "sync", cx.dve, r.get("_hl_dve", 0))
        cx.inc(ld_pro, q.dma_start(out=nat, in_=s_ap.rearrange("(a b) -> a b", b=128)), 16)
    ldv = ld_pro.val

    @cx.block.tensor
    def _(t):
        cx.wait(t, "tensor", ld_pro, ldv)
        cx.wait(t, "tensor", r["ld_id"], 16)
        fe, fv = r["bank_free"][4]
        cx.wait(t, "tensor", fe, fv)
        cx.inc(cx.pe, t.transpose(pt[0:128, 0:g], nat, r["id32"][0:g, 0:g]))
    pev = cx.pe.val
    r["_pro_pe_prev"] = pev

    @cx.block.vector
    def _(v):
        cx.wait(v, "vector", cx.pe, pev)
        cx.inc(cx.dve, v.tensor_copy(out[:, :], pt[0:128, 0:g]))
    r["bank_free"][4] = (cx.dve, cx.dve.val)
    return out, (cx.dve, cx.dve.val)


class LNRes:
    pass


def emit_ln(cx, r, t0, ncol, x_dep, xn_dst=None):
    x = r["x"]
    if xn_dst is None:
        xn_dst = lambda ic: r["xn_t"][:, ic * ncol:(ic + 1) * ncol]
    xbf, xsq = r["scrA"], r["scrB"]
    rows, tmp = r["ln_rows"], r["ln_tmp"]
    banks = r["ps_mm"]
    bank_free = r["bank_free"]
    s1 = banks[0][0:1, :]
    s2 = banks[3][0:1, :]
    bcr = banks[1]
    bcm = banks[2]
    pe_prev = r["_ln_pe_prev"]

    act_per_chunk = []
    for ic in range(NIC):
        xc = x[:, ic * T + t0: ic * T + t0 + ncol]

        @cx.block.scalar
        def _(sc, ic=ic, xc=xc):
            cx.wait(sc, "scalar", x_dep[0], x_dep[1])
            cx.wait(sc, "scalar", cx.dve, r.get("_hl_dve", 0))
            cx.wait_list(sc, "scalar", r.get("_ln_gate", []))
            cx.wait_list(sc, "scalar", r.get("_stage_gate", []))
            cx.wait(sc, "scalar", cx.pe, pe_prev)
            cx.inc(cx.act, sc.activation(xbf[:, ic * ncol:(ic + 1) * ncol], xc, AF.Copy))
            cx.inc(cx.act, sc.activation(xsq[:, ic * ncol:(ic + 1) * ncol], xc, AF.Square))
        act_per_chunk.append(cx.act.val)

    @cx.block.tensor
    def _(t):
        cx.wait(t, "tensor", cx.pool, r["ones_dep"])
        fe, fv = bank_free[0]
        cx.wait(t, "tensor", fe, fv)
        fe, fv = bank_free[3]
        cx.wait(t, "tensor", fe, fv)
        for ic in range(NIC):
            cx.wait(t, "tensor", cx.act, act_per_chunk[ic])
            cx.inc(cx.pe, t.matmul(s1[:, 0:ncol], r["ones_bf"][:, :],
                                   xbf[:, ic * ncol:(ic + 1) * ncol],
                                   start=(ic == 0), stop=(ic == NIC - 1)))
            cx.inc(cx.pe, t.matmul(s2[:, 0:ncol], r["ones_bf"][:, :],
                                   xsq[:, ic * ncol:(ic + 1) * ncol],
                                   start=(ic == 0), stop=(ic == NIC - 1)))
    pe_stats = cx.pe.val
    r["_ln_pe_prev"] = pe_stats

    r_m = s1[:, 0:ncol]
    r_e2 = s2[:, 0:ncol]
    r_rs = rows[:, 0:ncol]
    r_mr = rows[:, TT:TT + ncol]

    @cx.block.vector
    def _(v):
        cx.wait(v, "vector", cx.pe, pe_stats)
        cx.inc(cx.dve, v.tensor_scalar(r_m, s1[:, 0:ncol], 1.0 / D, None, ALU.mult))
        cx.inc(cx.dve, v.tensor_scalar(r_e2, s2[:, 0:ncol], 1.0 / D, None, ALU.mult))
    dve_ms = cx.dve.val

    @cx.block.scalar
    def _(sc):
        cx.wait(sc, "scalar", cx.dve, dve_ms)
        cx.wait(sc, "scalar", cx.pe, r.get("_ln_rows_pe", 0))
        cx.inc(cx.act, sc.activation(r_rs, r_m, AF.Square))
    act_m2 = cx.act.val

    @cx.block.vector
    def _(v):
        cx.wait(v, "vector", cx.act, act_m2)
        cx.inc(cx.dve, v.tensor_tensor(r_rs, r_e2, r_rs, ALU.subtract))
    dve_rows = cx.dve.val

    @cx.block.scalar
    def _(sc):
        cx.wait(sc, "scalar", cx.dve, dve_rows)
        cx.wait(sc, "scalar", cx.pool, r["ones_dep"])
        cx.inc(cx.act, sc.activation(r_rs, r_rs, AF.Ln, bias=r["eps_t"][:, :]))
        cx.wait(sc, "scalar", cx.act, cx.act.val)
        cx.inc(cx.act, sc.activation(r_rs, r_rs, AF.Exp, scale=-0.5))
    act_rs = cx.act.val

    @cx.block.vector
    def _(v):
        cx.wait(v, "vector", cx.act, act_rs)
        cx.wait(v, "vector", cx.pe, r.get("_ln_rows_pe", 0))
        cx.inc(cx.dve, v.tensor_tensor(r_mr, r_m, r_rs, ALU.mult))
    dve_mr = cx.dve.val
    bank_free[0] = (cx.dve, dve_mr)
    bank_free[3] = (cx.dve, dve_mr)

    @cx.block.tensor
    def _(t):
        cx.wait(t, "tensor", cx.act, act_rs)
        cx.wait(t, "tensor", cx.dve, dve_mr)
        for bidx, row in ((1, r_rs), (2, r_mr)):
            fe, fv = bank_free[bidx]
            cx.wait(t, "tensor", fe, fv)
        cx.inc(cx.pe, t.matmul(bcr[:, 0:ncol], r["ones_f1"][:, :], r_rs, start=True, stop=True))
        cx.inc(cx.pe, t.matmul(bcm[:, 0:ncol], r["ones_f1"][:, :], r_mr, start=True, stop=True))
    pe_bc = cx.pe.val
    r["_ln_rows_pe"] = pe_bc

    for ic in range(NIC):
        xc = x[:, ic * T + t0: ic * T + t0 + ncol]

        @cx.block.vector
        def _(v, ic=ic, xc=xc):
            cx.wait(v, "vector", cx.pe, pe_bc)
            tb = tmp[:, (ic % 2) * TT:(ic % 2) * TT + ncol]
            cx.inc(cx.dve, v.tensor_tensor(tb, xc, bcr[:, 0:ncol], ALU.mult))
            cx.wait(v, "vector", cx.dve, cx.dve.val)
            cx.inc(cx.dve, v.tensor_tensor(xn_dst(ic), tb,
                                           bcm[:, 0:ncol], ALU.subtract))
    res = LNRes()
    res.dve_done = cx.dve.val
    bank_free[1] = (cx.dve, res.dve_done)
    bank_free[2] = (cx.dve, res.dve_done)
    return res


def emit_mm_stage(cx, r, scr2, O, I, rhs_fn, rhs_dep, evac_fn, w_gate,
                  groups=None, qrot=(0, 1), ncol=TT):
    n_oc, n_ic = O // 128, I // 128
    if groups is None:
        groups = [(g * 8, min(8, n_oc - g * 8)) for g in range((n_oc + 7) // 8)]
    banks, bank_free = r["ps_mm"], r["bank_free"]
    wring, wfree = r["wslc"], r["wslc_free"]
    wsem = r["wslc_sem"]
    nslot = len(wring)
    items = [(oc0, now, ic) for (oc0, now) in groups for ic in range(n_ic)]
    LA = nslot - 1
    loads = [None] * len(items)

    def emit_load(j):
        oc0, now, ic = items[j]
        slot = r["wslc_next"][0] % nslot
        r["wslc_next"][0] += 1
        qi = qrot[j % len(qrot)]
        qname = "sync" if qi == 0 else "scalar"

        def body(q):
            cx.wait_list(q, qname, w_gate)
            cx.wait_list(q, qname, r.get("_stage_gate", []))
            cx.wait(q, qname, cx.pe, wfree[slot])
            cx.inc(wsem[slot], q.dma_start(out=wring[slot][:, 0:now * 128],
                                           in_=scr2[ic * 128:(ic + 1) * 128,
                                                    oc0 * 128:(oc0 + now) * 128]), 16)
            loads[j] = (slot, wsem[slot], wsem[slot].val)
        if qi == 0:
            @cx.block.sync
            def _(q):
                body(q)
        else:
            @cx.block.scalar
            def _(q):
                body(q)

    pp_step = r.get("_pp_step")
    for j in range(min(LA, len(items))):
        emit_load(j)
    gi = 0
    for (oc0, now) in groups:
        for ic in range(n_ic):
            j = gi
            gi += 1
            if pp_step is not None and (j % 2 == 0):
                pp_step()
            if j + LA < len(items):
                emit_load(j + LA)
            slot, ldeng, ldval = loads[j]

            @cx.block.tensor
            def _(t, oc0=oc0, now=now, ic=ic, slot=slot, ldeng=ldeng, ldval=ldval):
                cx.wait(t, "tensor", ldeng, ldval)
                cx.wait(t, "tensor", rhs_dep[0], rhs_dep[1])
                rhs = rhs_fn(ic)
                for oj in range(now):
                    if ic == 0:
                        fe, fv = bank_free[oj]
                        cx.wait(t, "tensor", fe, fv)
                    cx.inc(cx.pe, t.matmul(banks[oj][:, 0:ncol],
                                           wring[slot][:, oj * 128:(oj + 1) * 128],
                                           rhs, start=(ic == 0), stop=(ic == n_ic - 1)))
                wfree[slot] = cx.pe.val
        pev = cx.pe.val
        for oj in range(now):
            bank_free[oj] = evac_fn(oc0 + oj, banks[oj], pev)


def emit_ffn(cx, es, r, scr2, scales, wdone, x_dep):
    xn = r["xn_t"]
    h_sil, h_b, h_a, z1 = r["scrA"], r["scrB"], r["h_a"], r["z1"]
    ytmp = r["ytmp"]
    w1s, w2s = scales["w1_s"][0], scales["w2_s"][0]
    h1s, h2s = scales["h1_s"][0], scales["h2_s"][0]
    w3s = scales["w3_s"][0]

    for tt in range(NTT):
        t0 = tt * TT
        ln = emit_ln(cx, r, t0, TT, x_dep)
        xn_dep = (cx.dve, ln.dve_done)

        def rhs_xn(ic):
            return xn[:, ic * TT:(ic + 1) * TT]

        for g in range(HID // 128 // 8):
            def evac_w1(oc, bank, pev, g=g):
                @cx.block.scalar
                def _(sc):
                    cx.wait(sc, "scalar", cx.pe, pev)
                    oj = oc - g * 8
                    cx.inc(cx.act, sc.activation(h_sil[:, oj * TT:(oj + 1) * TT],
                                                 bank[:, 0:TT], AF.Silu,
                                                 scale=w1s[:, oc:oc + 1]))
                r["_w1_act"][oc] = cx.act.val
                return (cx.act, cx.act.val)

            emit_mm_stage(cx, r, scr2["w1"], HID, D, rhs_xn, xn_dep, evac_w1,
                          wdone["w1"], groups=[(g * 8, 8)], qrot=(0,))

            def evac_w2(oc, bank, pev, g=g):
                oj = oc - g * 8

                @cx.block.vector
                def _(v):
                    cx.wait(v, "vector", cx.pe, pev)
                    cx.inc(cx.dve, v.tensor_scalar(h_b[:, oj * TT:(oj + 1) * TT],
                                                   bank[:, 0:TT], w2s[:, oc:oc + 1],
                                                   None, ALU.mult))
                    cx.wait(v, "vector", cx.dve, cx.dve.val)
                    cx.wait(v, "vector", cx.act, r["_w1_act"][oc])
                    cx.inc(cx.dve, v.tensor_tensor(h_a[:, oc * TT:(oc + 1) * TT],
                                                   h_sil[:, oj * TT:(oj + 1) * TT],
                                                   h_b[:, oj * TT:(oj + 1) * TT],
                                                   ALU.mult))
                return (cx.dve, cx.dve.val)

            emit_mm_stage(cx, r, scr2["w2"], HID, D, rhs_xn, xn_dep, evac_w2,
                          wdone["w2"], groups=[(g * 8, 8)], qrot=(1,))
        h_a_dep = (cx.dve, cx.dve.val)

        def rhs_ha(ic):
            return h_a[:, ic * TT:(ic + 1) * TT]

        def evac_h1(oc, bank, pev):
            @cx.block.scalar
            def _(sc):
                cx.wait(sc, "scalar", cx.pe, pev)
                cx.inc(cx.act, sc.activation(z1[:, oc * TT:(oc + 1) * TT],
                                             bank[:, 0:TT], AF.Gelu,
                                             scale=h1s[:, oc:oc + 1]))
            return (cx.act, cx.act.val)

        emit_mm_stage(cx, r, scr2["h1"], HID, HID, rhs_ha, h_a_dep, evac_h1, wdone["h1"])
        z1_dep = (cx.act, cx.act.val)

        def rhs_z1(ic):
            return z1[:, ic * TT:(ic + 1) * TT]

        def evac_h2(oc, bank, pev):
            @cx.block.scalar
            def _(sc):
                cx.wait(sc, "scalar", cx.pe, pev)
                cx.inc(cx.act, sc.activation(h_a[:, oc * TT:(oc + 1) * TT],
                                             bank[:, 0:TT], AF.Gelu,
                                             scale=h2s[:, oc:oc + 1]))
            return (cx.act, cx.act.val)

        emit_mm_stage(cx, r, scr2["h2"], HID, HID, rhs_z1, z1_dep, evac_h2, wdone["h2"])
        z2_dep = (cx.act, cx.act.val)

        def evac_w3(oc, bank, pev, t0=t0):
            @cx.block.vector
            def _(v):
                cx.wait(v, "vector", cx.pe, pev)
                yb = ytmp[:, (oc % 2) * TT:(oc % 2) * TT + TT]
                cx.inc(cx.dve, v.tensor_scalar(yb, bank[:, 0:TT], w3s[:, oc:oc + 1],
                                               0.5, ALU.mult, ALU.mult))
                cx.wait(v, "vector", cx.dve, cx.dve.val)
                xs = r["x"][:, oc * T + t0: oc * T + t0 + TT]
                cx.inc(cx.dve, v.tensor_tensor(xs, xs, yb, ALU.add))
            return (cx.dve, cx.dve.val)

        emit_mm_stage(cx, r, scr2["w3"], D, HID, rhs_ha, z2_dep, evac_w3, wdone["w3"])
        x_dep = (cx.dve, cx.dve.val)
    return x_dep


def alloc_ffn(cx, es):
    r = {}
    r["h_a"] = sb(cx, es, "h_a", [128, 32 * TT], BF16)
    r["z1"] = sb(cx, es, "z1", [128, 32 * TT], BF16)
    r["xn_t"] = sb(cx, es, "xn_t", [128, NIC * TT], BF16)
    r["ytmp"] = sb(cx, es, "ytmp", [128, 2 * TT], F32)
    r["wslc"] = [sb(cx, es, f"wslc{i}", [128, 1024], BF16) for i in range(4)]
    r["wslc_sem"] = [cx.new_dma_sem(f"wsl{i}") for i in range(4)]
    r["wslc_free"] = [0, 0, 0, 0]
    r["wslc_next"] = [0]
    return r


def emit_load_x(cx, es, r, x_in, in_tm):
    x = r["x"]
    ld_x = r["ld_x"]
    if not in_tm:
        @cx.block.sync
        def _(q):
            cx.inc(ld_x, q.dma_start(
                out=x[:].rearrange("p (ic t) -> p ic t", ic=NIC),
                in_=x_in.rearrange("(ic p) t -> p ic t", p=128)), 16)
        return (ld_x, ld_x.val)

    stg = r["x_stg"]
    banks = r["ps_mm"]
    prev_dve = [0]
    for ti in range(T // 128):
        @cx.block.sync
        def _(q, ti=ti):
            cx.wait(q, "sync", cx.dve, prev_dve[0])
            cx.inc(ld_x, q.dma_start(out=stg[:, :],
                                     in_=x_in[ti * 128:(ti + 1) * 128, :]), 16)
        ldv = ld_x.val
        for half in range(2):
            bank = banks[4 + half]

            @cx.block.tensor
            def _(t, ti=ti, half=half, ldv=ldv, bank=bank):
                cx.wait(t, "tensor", ld_x, ldv)
                cx.wait(t, "tensor", r["ld_id"], 16)
                fe, fv = r["bank_free"][4 + half]
                cx.wait(t, "tensor", fe, fv)
                for q4 in range(4):
                    ic = half * 4 + q4
                    cx.inc(cx.pe, t.transpose(bank[:, q4 * 128:(q4 + 1) * 128],
                                              stg[:, ic * 128:(ic + 1) * 128],
                                              r["id32"][:, :]))
            pev = cx.pe.val

            @cx.block.vector
            def _(v, ti=ti, half=half, pev=pev, bank=bank):
                cx.wait(v, "vector", cx.pe, pev)
                for q4 in range(4):
                    ic = half * 4 + q4
                    cx.inc(cx.dve, v.tensor_copy(
                        x[:, ic * T + ti * 128: ic * T + (ti + 1) * 128],
                        bank[:, q4 * 128:(q4 + 1) * 128]))
            r["bank_free"][4 + half] = (cx.dve, cx.dve.val)
        prev_dve[0] = cx.dve.val
    return (cx.dve, cx.dve.val)


def emit_store_x(cx, es, r, x_out, out_tm, x_dep):
    x = r["x"]
    if not out_tm:
        @cx.block.sync
        def _(q):
            cx.wait(q, "sync", x_dep[0], x_dep[1])
            cx.inc(cx.st, q.dma_start(
                out=x_out.rearrange("(ic p) t -> p ic t", p=128),
                in_=x[:].rearrange("p (ic t) -> p ic t", ic=NIC)), 16)
            q.wait_ge(cx.st.sem, cx.st.val)
        return

    stg = r["x_stg"]
    banks = r["ps_mm"]
    st_prev = [cx.st.val]
    for ti in range(T // 128):
        dvev = 0
        for half in range(2):
            bank = banks[4 + half]

            @cx.block.tensor
            def _(t, ti=ti, half=half, bank=bank):
                cx.wait(t, "tensor", x_dep[0], x_dep[1])
                fe, fv = r["bank_free"][4 + half]
                cx.wait(t, "tensor", fe, fv)
                for q4 in range(4):
                    ic = half * 4 + q4
                    cx.inc(cx.pe, t.transpose(
                        bank[:, q4 * 128:(q4 + 1) * 128],
                        x[:, ic * T + ti * 128: ic * T + (ti + 1) * 128],
                        r["id32"][:, :]))
            pev = cx.pe.val

            @cx.block.vector
            def _(v, ti=ti, half=half, pev=pev, bank=bank):
                cx.wait(v, "vector", cx.pe, pev)
                cx.wait(v, "vector", cx.st, st_prev[0])
                for q4 in range(4):
                    ic = half * 4 + q4
                    cx.inc(cx.dve, v.tensor_copy(stg[:, ic * 128:(ic + 1) * 128],
                                                 bank[:, q4 * 128:(q4 + 1) * 128]))
            r["bank_free"][4 + half] = (cx.dve, cx.dve.val)
            dvev = cx.dve.val

        @cx.block.sync
        def _(q, ti=ti, dvev=dvev):
            cx.wait(q, "sync", cx.dve, dvev)
            cx.inc(cx.st, q.dma_start(out=x_out[ti * 128:(ti + 1) * 128, :],
                                      in_=stg[:, :]), 16)
        st_prev[0] = cx.st.val

    @cx.block.sync
    def _(q):
        q.wait_ge(cx.st.sem, cx.st.val)


def emit_hilo(cx, es, r, name, hi_ap, lo_ap, shape, rearr=None, out_rearr=None,
              out=None, dtype=F32):
    if out is None:
        out = sb(cx, es, name, shape, dtype)[:]
    scr = r["hl_scr"]
    F = int(np.prod(shape[1:]))

    @cx.block.sync
    def _(q):
        ha = rearr(hi_ap) if rearr else hi_ap[:, :]
        la = rearr(lo_ap) if rearr else lo_ap[:, :]
        cx.wait(q, "sync", cx.dve, r.get("_hl_dve", 0))
        cx.wait(q, "sync", cx.pe, r.get("_pro_pe_prev", 0))
        cx.wait_list(q, "sync", r.get("_hl_gate", []))
        sa = out_rearr(scr[:, 0:F]) if out_rearr else scr[:, 0:F]
        sb_ = out_rearr(scr[:, F:2 * F]) if out_rearr else scr[:, F:2 * F]
        with cx.nc.allow_non_contiguous_dma(reason="tiny one-time const load"):
            cx.inc(r["ld_pro"], q.dma_start(out=sa, in_=ha), 16)
            cx.inc(r["ld_pro"], q.dma_start(out=sb_, in_=la), 16)
    ldv = r["ld_pro"].val

    @cx.block.vector
    def _(v):
        cx.wait(v, "vector", r["ld_pro"], ldv)
        cx.wait_list(v, "vector", r.get("_hl_gate", []))
        cx.inc(cx.dve, v.tensor_tensor(out, scr[:, 0:F], scr[:, F:2 * F], ALU.add))
    r["_hl_dve"] = cx.dve.val
    return out, (cx.dve, cx.dve.val)


def alloc_retnet(cx, es):
    a = {}
    a["ld_hw"] = cx.new_dma_sem("ld_hw")
    a["ld_hw2"] = cx.new_dma_sem("ld_hw2")
    a["xn_full"] = sb(cx, es, "xn_full", [128, NIC * T], BF16)
    a["H"] = sb(cx, es, "rn_H", [128, RN_H * 2 * RN_DV], F32)
    a["H_bf"] = sb(cx, es, "rn_Hbf", [128, 2 * RN_DV], BF16)
    a["wq_t"] = sb(cx, es, "rn_wq", [128, NIC * 256], BF16)
    a["wk_t"] = sb(cx, es, "rn_wk", [128, NIC * 256], BF16)
    a["wv_t"] = sb(cx, es, "rn_wv", [128, NIC * 512], BF16)
    a["wg_t"] = sb(cx, es, "rn_wg", [128, NIC * 512], BF16)
    a["wo_t"] = sb(cx, es, "rn_wo", [128, 4 * 1024], BF16)
    a["q_bf"] = sb(cx, es, "rn_qbf", [128, 256], BF16)
    a["qt_bf"] = sb(cx, es, "rn_qtbf", [128, 256], BF16)
    a["k_bf"] = sb(cx, es, "rn_kbf", [128, 256], BF16)
    a["k_f32"] = sb(cx, es, "rn_kf32", [128, 256], F32)
    a["k_tm"] = sb(cx, es, "rn_ktm", [128, 256], BF16)
    a["v_bf"] = sb(cx, es, "rn_vbf", [128, 512], BF16)
    a["g_si"] = sb(cx, es, "rn_gsi", [128, 512], BF16)
    a["s_bf"] = sb(cx, es, "rn_sbf", [128, 128], BF16)
    a["o_bf"] = sb(cx, es, "rn_obf", [128, 512], BF16)
    a["osq"] = sb(cx, es, "rn_osq", [128, 512], BF16)
    a["o_g"] = sb(cx, es, "rn_og", [128, 512], BF16)
    a["t128"] = sb(cx, es, "rn_t128", [128, 4 * 128], F32)
    a["rrow"] = sb(cx, es, "rn_rrow", [1, 256], F32)
    return a


def emit_retnet(cx, es, r, a, scr2, gnw_fm, consts, wdone, x_dep, bias_fm=None):
    nc = cx.nc
    x = r["x"]
    xn = a["xn_full"]
    banks = r["ps_mm"]

    for tt in range(NTT):
        t0 = tt * TT
        ln = emit_ln(cx, r, t0, TT, x_dep,
                     xn_dst=lambda ic, t0=t0: xn[:, ic * T + t0: ic * T + t0 + TT])
    xn_dep = (cx.dve, cx.dve.val)

    @cx.block.gpsimd
    def _(gp):
        cx.wait_list(gp, "gpsimd", r.get("_pp_all", []))
        cx.wait(gp, "gpsimd", cx.dve, r.get("_hl_dve", 0))
        cx.inc(cx.pool, gp.memset(a["H"][:, :], 0.0))
    H_dep = cx.pool.val

    cos_f, sin_f = consts["cos"], consts["sin"]
    maskt_f, qsc_f = consts["maskt"], consts["qsc"]
    ksc_f, gc_f = consts["ksc"], consts["gc"]
    cdep = consts["dep"]

    bfree = list(r["bank_free"])

    for h in range(RN_H):
        @cx.block.sync
        def _(q, h=h):
            cx.wait_list(q, "sync", wdone["Wq"])
            cx.wait_list(q, "sync", wdone["Wk"])
            cx.wait_list(q, "sync", wdone["Wv"])
            cx.wait_list(q, "sync", r.get("_stage_gate", []))
            cx.wait(q, "sync", cx.pe, a.get("_wt_pe_free", 0))
            cx.inc(a["ld_hw"], q.dma_start(
                out=a["wq_t"][:].rearrange("p (ic o) -> p ic o", ic=NIC),
                in_=scr2["Wq"][:, h * 256:(h + 1) * 256].rearrange("(ic p) o -> p ic o", p=128)), 16)
            cx.inc(a["ld_hw"], q.dma_start(
                out=a["wk_t"][:].rearrange("p (ic o) -> p ic o", ic=NIC),
                in_=scr2["Wk"][:, h * 256:(h + 1) * 256].rearrange("(ic p) o -> p ic o", p=128)), 16)
            cx.inc(a["ld_hw"], q.dma_start(
                out=a["wv_t"][:].rearrange("p (ic o) -> p ic o", ic=NIC),
                in_=scr2["Wv"][:, h * 512:(h + 1) * 512].rearrange("(ic p) o -> p ic o", p=128)), 16)
        ld_qkv = a["ld_hw"].val

        @cx.block.scalar
        def _(q, h=h):
            cx.wait_list(q, "scalar", wdone["Wg"])
            cx.wait_list(q, "scalar", wdone["Wo"])
            cx.wait_list(q, "scalar", r.get("_stage_gate", []))
            cx.wait(q, "scalar", cx.pe, a.get("_wt_pe_free", 0))
            cx.inc(a["ld_hw2"], q.dma_start(
                out=a["wg_t"][:].rearrange("p (ic o) -> p ic o", ic=NIC),
                in_=scr2["Wg"][:, h * 512:(h + 1) * 512].rearrange("(ic p) o -> p ic o", p=128)), 16)
            cx.inc(a["ld_hw2"], q.dma_start(
                out=a["wo_t"][:].rearrange("p (kc o) -> p kc o", kc=4),
                in_=scr2["Wo"][h * 512:(h + 1) * 512, :].rearrange("(kc p) o -> p kc o", p=128)), 16)
        ld_go = a["ld_hw2"].val

        for c in range(T // CHUNK):
            cols = slice(c * CHUNK, (c + 1) * CHUNK)
            emit_retnet_chunk(cx, r, a, h, c, ld_qkv, ld_go, xn_dep, H_dep,
                              cdep, cos_f, sin_f, maskt_f, qsc_f, ksc_f, gc_f,
                              gnw_fm, bfree, bias_fm)
        a["_wt_pe_free"] = cx.pe.val

    r["bank_free"] = list(bfree)
    return (cx.dve, cx.dve.val)


def emit_retnet_chunk(cx, r, a, h, c, ld_qkv, ld_go, xn_dep, H_dep, cdep,
                      cos_f, sin_f, maskt_f, qsc_f, ksc_f, gc_f, gnw_fm,
                      bfree, bias_fm):
    x = r["x"]
    xn = a["xn_full"]
    banks = r["ps_mm"]
    b_q, b_k, b_v, b_g, b_s, b_t, b_o, b_r = banks

    def xn_sl(ic):
        return xn[:, ic * T + c * CHUNK: ic * T + (c + 1) * CHUNK]

    @cx.block.tensor
    def _(t):
        cx.wait(t, "tensor", a["ld_hw"], ld_qkv)
        cx.wait(t, "tensor", a["ld_hw2"], ld_go)
        cx.wait(t, "tensor", xn_dep[0], xn_dep[1])
        for bi, dep in ((0, bfree[0]), (1, bfree[1]), (2, bfree[2]), (3, bfree[3])):
            cx.wait(t, "tensor", dep[0], dep[1])
        for sl in range(2):
            for ic in range(NIC):
                cx.inc(cx.pe, t.matmul(b_q[:, sl * 128:(sl + 1) * 128],
                                       a["wq_t"][:, ic * 256 + sl * 128: ic * 256 + (sl + 1) * 128],
                                       xn_sl(ic), start=(ic == 0), stop=(ic == NIC - 1)))
        for sl in range(2):
            for ic in range(NIC):
                cx.inc(cx.pe, t.matmul(b_k[:, sl * 128:(sl + 1) * 128],
                                       a["wk_t"][:, ic * 256 + sl * 128: ic * 256 + (sl + 1) * 128],
                                       xn_sl(ic), start=(ic == 0), stop=(ic == NIC - 1)))
        for ic in range(NIC):
            cx.inc(cx.pe, t.matmul(b_v[:, 0:512], xn_sl(ic),
                                   a["wv_t"][:, ic * 512:(ic + 1) * 512],
                                   start=(ic == 0), stop=(ic == NIC - 1)))
        for sl in range(4):
            for ic in range(NIC):
                cx.inc(cx.pe, t.matmul(b_g[:, sl * 128:(sl + 1) * 128],
                                       a["wg_t"][:, ic * 512 + sl * 128: ic * 512 + (sl + 1) * 128],
                                       xn_sl(ic), start=(ic == 0), stop=(ic == NIC - 1)))
    pe_proj = cx.pe.val

    t128 = a["t128"]
    ccol = cos_f[:, c * CHUNK:(c + 1) * CHUNK]
    scol = sin_f[:, c * CHUNK:(c + 1) * CHUNK]

    @cx.block.vector
    def _(v):
        cx.wait(v, "vector", cx.pe, pe_proj)
        cx.wait(v, "vector", cdep[0], cdep[1])
        for (bank, dst) in ((b_q, a["q_bf"]), (b_k, a["k_f32"])):
            cx.wait(v, "vector", cx.dve, cx.dve.val)
            ta = t128[:, 0:128]
            tb = t128[:, 128:256]
            tc2 = t128[:, 256:384]
            td = t128[:, 384:512]
            cx.inc(cx.dve, v.tensor_tensor(ta, bank[:, 0:128], ccol, ALU.mult))
            cx.inc(cx.dve, v.tensor_tensor(tb, bank[:, 128:256], scol, ALU.mult))
            cx.inc(cx.dve, v.tensor_tensor(tc2, bank[:, 0:128], scol, ALU.mult))
            cx.inc(cx.dve, v.tensor_tensor(td, bank[:, 128:256], ccol, ALU.mult))
            cx.wait(v, "vector", cx.dve, cx.dve.val)
            cx.inc(cx.dve, v.tensor_tensor(dst[:, 0:128], ta, tb, ALU.subtract))
            cx.inc(cx.dve, v.tensor_tensor(dst[:, 128:256], tc2, td, ALU.add))
        cx.wait(v, "vector", cx.dve, cx.dve.val)
        cx.inc(cx.dve, v.tensor_copy(a["k_bf"][:, :], a["k_f32"][:, :]))
        cx.wait(v, "vector", cx.dve, cx.dve.val)
        for sl in range(2):
            cx.inc(cx.dve, v.tensor_tensor(a["qt_bf"][:, sl * 128:(sl + 1) * 128],
                                           a["q_bf"][:, sl * 128:(sl + 1) * 128],
                                           qsc_f[:, h * 128:(h + 1) * 128], ALU.mult))
    dve_rope = cx.dve.val
    bfree[0] = (cx.dve, dve_rope)

    @cx.block.tensor
    def _(t):
        cx.wait(t, "tensor", cx.dve, dve_rope)
        fe, fv = bfree[4]
        cx.wait(t, "tensor", fe, fv)
        for sl in range(2):
            cx.inc(cx.pe, t.matmul(b_s[:, 0:128],
                                   a["k_bf"][:, sl * 128:(sl + 1) * 128],
                                   a["q_bf"][:, sl * 128:(sl + 1) * 128],
                                   start=(sl == 0), stop=(sl == 1)))
        fe, fv = bfree[5]
        cx.wait(t, "tensor", fe, fv)
        for sl in range(2):
            cx.inc(cx.pe, t.transpose(b_t[:, sl * 128:(sl + 1) * 128],
                                      a["k_f32"][:, sl * 128:(sl + 1) * 128],
                                      r["id32"][:, :]))
    pe_sc = cx.pe.val

    @cx.block.vector
    def _(v):
        cx.wait(v, "vector", cx.pe, pe_sc)
        cx.inc(cx.dve, v.tensor_tensor(a["s_bf"][:, :], b_s[:, 0:128],
                                       maskt_f[:, h * 128:(h + 1) * 128], ALU.mult))
        cx.inc(cx.dve, v.tensor_scalar(a["k_tm"][:, :], b_t[:, 0:256],
                                       ksc_f[:, h:h + 1], None, ALU.mult))
    dve_sc = cx.dve.val
    bfree[1] = (cx.dve, dve_sc)
    bfree[4] = (cx.dve, dve_sc)

    @cx.block.scalar
    def _(sc):
        cx.wait(sc, "scalar", cx.pe, pe_proj)
        cx.inc(cx.act, sc.activation(a["v_bf"][:, :], b_v[:, 0:512], AF.Copy))
        cx.inc(cx.act, sc.activation(a["g_si"][:, :], b_g[:, 0:512], AF.Silu))
    act_vg = cx.act.val
    bfree[2] = (cx.act, act_vg)
    bfree[3] = (cx.act, act_vg)

    @cx.block.tensor
    def _(t):
        cx.wait(t, "tensor", cx.dve, dve_sc)
        cx.wait(t, "tensor", cx.act, act_vg)
        fe, fv = bfree[6]
        cx.wait(t, "tensor", fe, fv)
        if c > 0:
            cx.wait(t, "tensor", cx.dve, a["_Hbf_dve"])
        for dvs in range(4):
            cx.inc(cx.pe, t.matmul(b_o[:, dvs * 128:(dvs + 1) * 128],
                                   a["v_bf"][:, dvs * 128:(dvs + 1) * 128],
                                   a["s_bf"][:, :], start=True, stop=(c == 0)))
            if c > 0:
                for kc in range(2):
                    cx.inc(cx.pe, t.matmul(
                        b_o[:, dvs * 128:(dvs + 1) * 128],
                        a["H_bf"][:, kc * 512 + dvs * 128: kc * 512 + (dvs + 1) * 128],
                        a["qt_bf"][:, kc * 128:(kc + 1) * 128],
                        start=False, stop=(kc == 1)))
    pe_o = cx.pe.val

    st_banks = [b_t, b_s]

    @cx.block.tensor
    def _(t):
        cx.wait(t, "tensor", cx.dve, dve_sc)
        cx.wait(t, "tensor", cx.act, act_vg)
        if c > 0:
            cx.wait(t, "tensor", cx.dve, a["_Hbf_dve"])
        for kc in range(2):
            cx.inc(cx.pe, t.matmul(st_banks[kc][:, 0:512],
                                   a["k_tm"][:, kc * 128:(kc + 1) * 128],
                                   a["v_bf"][:, :], start=True, stop=True))
    pe_st = cx.pe.val

    @cx.block.vector
    def _(v):
        cx.wait(v, "vector", cx.pe, pe_st)
        cx.wait(v, "vector", cx.pool, H_dep)
        for kc in range(2):
            H_sl = a["H"][:, (h * 2 + kc) * RN_DV:(h * 2 + kc + 1) * RN_DV]
            cx.inc(cx.dve, v.tensor_scalar(H_sl, H_sl, gc_f[:, h:h + 1], None, ALU.mult))
            cx.wait(v, "vector", cx.dve, cx.dve.val)
            cx.inc(cx.dve, v.tensor_tensor(H_sl, H_sl, st_banks[kc][:, 0:512], ALU.add))
            cx.wait(v, "vector", cx.dve, cx.dve.val)
    a["_H_dve"] = cx.dve.val

    @cx.block.vector
    def _(v):
        cx.wait(v, "vector", cx.pe, pe_o)
        for kc in range(2):
            H_sl = a["H"][:, (h * 2 + kc) * RN_DV:(h * 2 + kc + 1) * RN_DV]
            cx.inc(cx.dve, v.tensor_copy(a["H_bf"][:, kc * RN_DV:(kc + 1) * RN_DV], H_sl))
    a["_Hbf_dve"] = cx.dve.val
    bfree[4] = (cx.dve, a["_H_dve"])
    bfree[5] = (cx.dve, a["_H_dve"])

    @cx.block.scalar
    def _(sc):
        cx.wait(sc, "scalar", cx.pe, pe_o)
        cx.inc(cx.act, sc.activation(a["o_bf"][:, :], b_o[:, 0:512], AF.Copy))
        cx.wait(sc, "scalar", cx.act, cx.act.val)
        cx.inc(cx.act, sc.activation(a["osq"][:, :], a["o_bf"][:, :], AF.Square))
    act_o = cx.act.val
    bfree[6] = (cx.act, act_o)

    @cx.block.tensor
    def _(t):
        cx.wait(t, "tensor", cx.act, act_o)
        fe, fv = bfree[7]
        cx.wait(t, "tensor", fe, fv)
        for dvs in range(4):
            cx.inc(cx.pe, t.matmul(b_r[0:1, 0:128], r["ones_bf"][:, :],
                                   a["osq"][:, dvs * 128:(dvs + 1) * 128],
                                   start=(dvs == 0), stop=(dvs == 3)))
    pe_rms = cx.pe.val

    r_ms = a["rrow"][:, 0:128]
    r_rstd = a["rrow"][:, 128:256]

    @cx.block.vector
    def _(v):
        cx.wait(v, "vector", cx.pe, pe_rms)
        cx.inc(cx.dve, v.tensor_scalar(r_ms, b_r[0:1, 0:128], 1.0 / RN_DV, None, ALU.mult))
    dve_ms = cx.dve.val

    @cx.block.scalar
    def _(sc):
        cx.wait(sc, "scalar", cx.dve, dve_ms)
        cx.inc(cx.act, sc.activation(r_rstd, r_ms, AF.Ln, bias=r["eps_t"][:, :]))
        cx.wait(sc, "scalar", cx.act, cx.act.val)
        cx.inc(cx.act, sc.activation(r_rstd, r_rstd, AF.Exp, scale=-0.5))
    act_rstd = cx.act.val

    @cx.block.tensor
    def _(t):
        cx.wait(t, "tensor", cx.act, act_rstd)
        cx.inc(cx.pe, t.matmul(b_r[:, 128:256], r["ones_f1"][:, :], r_rstd,
                               start=True, stop=True))
    pe_bc = cx.pe.val

    @cx.block.vector
    def _(v):
        cx.wait(v, "vector", cx.pe, pe_bc)
        cx.wait(v, "vector", cx.act, act_o)
        for dvs in range(4):
            tmp = a["t128"][:, dvs * 128:(dvs + 1) * 128]
            cx.inc(cx.dve, v.tensor_tensor(tmp, a["o_bf"][:, dvs * 128:(dvs + 1) * 128],
                                           b_r[:, 128:256], ALU.mult))
            cx.wait(v, "vector", cx.dve, cx.dve.val)
            cx.inc(cx.dve, v.tensor_scalar(tmp, tmp, gnw_fm[:, dvs:dvs + 1], None, ALU.mult))
            cx.wait(v, "vector", cx.dve, cx.dve.val)
            cx.inc(cx.dve, v.tensor_tensor(a["o_g"][:, dvs * 128:(dvs + 1) * 128],
                                           tmp, a["g_si"][:, dvs * 128:(dvs + 1) * 128],
                                           ALU.mult))
    dve_gate = cx.dve.val
    bfree[7] = (cx.dve, dve_gate)

    y_banks = [b_q, b_k]
    yfree = [bfree[0], bfree[1]]
    for oc in range(NIC):
        bk = y_banks[oc % 2]

        @cx.block.tensor
        def _(t, oc=oc, bk=bk):
            cx.wait(t, "tensor", cx.dve, dve_gate)
            fe, fv = yfree[oc % 2]
            cx.wait(t, "tensor", fe, fv)
            for kc2 in range(4):
                cx.inc(cx.pe, t.matmul(bk[:, 0:128],
                                       a["wo_t"][:, kc2 * 1024 + oc * 128: kc2 * 1024 + (oc + 1) * 128],
                                       a["o_g"][:, kc2 * 128:(kc2 + 1) * 128],
                                       start=(kc2 == 0), stop=(kc2 == 3)))
        pev = cx.pe.val

        @cx.block.vector
        def _(v, oc=oc, bk=bk, pev=pev):
            cx.wait(v, "vector", cx.pe, pev)
            xs = x[:, oc * T + c * CHUNK: oc * T + (c + 1) * CHUNK]
            cx.inc(cx.dve, v.tensor_tensor(xs, xs, bk[:, 0:128], ALU.add))
        yfree[oc % 2] = (cx.dve, cx.dve.val)
    bfree[0] = yfree[0]
    bfree[1] = yfree[1]



FFN_W_SHAPES = [("w1", (HID, D)), ("w2", (HID, D)), ("h1", (HID, HID)),
                ("h2", (HID, HID)), ("w3", (D, HID))]


def build_ffn_prog(in_tm, out_tm):
    nc = bass.Bass("TRN2", target_bir_lowering=False, debug=False)
    x_in = nc.dram_tensor("x_in", [T, D] if in_tm else [D, T], F32,
                          kind="ExternalInput").ap()
    x_out = nc.dram_tensor("x_out", [T, D] if out_tm else [D, T], F32,
                           kind="ExternalOutput").ap()
    w, sv, scr1, scr2 = {}, {}, {}, {}
    for nme, shape in FFN_W_SHAPES:
        w[nme] = nc.dram_tensor(nme, list(shape), F32, kind="ExternalInput").ap()
        scr1[nme] = nc.dram_tensor("scr1_" + nme, list(shape), BF16, kind="Internal").ap()
        scr2[nme] = nc.dram_tensor("scr2_" + nme, [shape[1], shape[0]], BF16,
                                   kind="Internal").ap()
        sv[nme + "_s"] = nc.dram_tensor(nme + "_s", [shape[0]], F32,
                                        kind="ExternalInput").ap()
    id32 = nc.dram_tensor("ident_f32", [128, 128], F32, kind="ExternalInput").ap()

    with ExitStack() as es:
        cx = Ctx(nc, es)
        r = alloc_common(cx, es)
        r.update(alloc_ffn(cx, es))

        @cx.block.sync
        def _(q):
            cx.inc(r["ld_id"], q.dma_start(out=r["id32"][:, :], in_=id32[:, :]), 16)

        x_dep = emit_load_x(cx, es, r, x_in, in_tm)

        wdone = emit_prepass(cx, es, r, [(n, w[n], scr1[n], scr2[n])
                                         for n, _ in FFN_W_SHAPES])
        scales = {}
        for nme, shape in FFN_W_SHAPES:
            scales[nme + "_s"] = emit_load_scale_fm(cx, es, r, nme + "_s",
                                                    sv[nme + "_s"], shape[0])

        x_dep = emit_ffn(cx, es, r, scr2, scales, wdone, x_dep)
        emit_store_x(cx, es, r, x_out, out_tm, x_dep)
    return nc


RN_W_SHAPES = [("Wq", (D, D)), ("Wk", (D, D)), ("Wv", (2 * D, D)),
               ("Wg", (2 * D, D)), ("Wo", (D, 2 * D))]

RN_CONST_NAMES = ["rope_cos", "rope_sin", "rn_maskt", "rn_qsc", "rn_ksc", "rn_gc"]


def build_retnet_prog(in_tm=False, out_tm=False):
    nc = bass.Bass("TRN2", target_bir_lowering=False, debug=False)
    x_in = nc.dram_tensor("x_in", [T, D] if in_tm else [D, T], F32,
                          kind="ExternalInput").ap()
    x_out = nc.dram_tensor("x_out", [T, D] if out_tm else [D, T], F32,
                           kind="ExternalOutput").ap()
    w, scr1, scr2 = {}, {}, {}
    for nme, shape in RN_W_SHAPES:
        w[nme] = nc.dram_tensor(nme, list(shape), F32, kind="ExternalInput").ap()
        scr1[nme] = nc.dram_tensor("scr1_" + nme, list(shape), BF16, kind="Internal").ap()
        scr2[nme] = nc.dram_tensor("scr2_" + nme, [shape[1], shape[0]], BF16,
                                   kind="Internal").ap()
    gnw = nc.dram_tensor("g_norm_w", [RN_DV], F32, kind="ExternalInput").ap()
    id32 = nc.dram_tensor("ident_f32", [128, 128], F32, kind="ExternalInput").ap()
    cst = {}
    for nme in RN_CONST_NAMES:
        for p in ("_hi", "_lo"):
            shp = {"rope_cos": [128, T], "rope_sin": [128, T],
                   "rn_maskt": [RN_H * 128, 128], "rn_qsc": [RN_H * 128, 128],
                   "rn_ksc": [RN_H * 128, 1], "rn_gc": [RN_H * 128, 1]}[nme]
            cst[nme + p] = nc.dram_tensor(nme + p, shp, F32, kind="ExternalInput").ap()

    with ExitStack() as es:
        cx = Ctx(nc, es)
        r = alloc_common(cx, es)
        a = alloc_retnet(cx, es)
        r["hl_scr"] = a["H"][:, 0:4096]

        @cx.block.sync
        def _(q):
            cx.inc(r["ld_id"], q.dma_start(out=r["id32"][:, :], in_=id32[:, :]), 16)

        @cx.block.vector
        def _(v):
            cx.wait(v, "vector", r["ld_id"], 16)
            cx.inc(cx.dve, v.tensor_copy(r["id_bf"][:, :], r["id32"][:, :]))

        x_dep = emit_load_x(cx, es, r, x_in, in_tm)
        pp_rings = [a["xn_full"][:, 0:4096], a["xn_full"][:, 4096:8192]]
        wdone = emit_prepass(cx, es, r, [(n, w[n], scr1[n], scr2[n])
                                         for n, _ in RN_W_SHAPES], rings=pp_rings)
        r["_ln_gate"] = r["_pp_all"]
        gnw_fm, gnw_dep = emit_load_scale_fm(cx, es, r, "gnw", gnw, RN_DV)

        hp = lambda ap: ap.rearrange("(h p) c -> p h c", p=128)
        hv = lambda ap: ap.rearrange("p (h c) -> p h c", h=RN_H)
        consts = {}
        consts["cos"], _ = emit_hilo(cx, es, r, "cos_f", cst["rope_cos_hi"],
                                     cst["rope_cos_lo"], [128, T])
        consts["sin"], _ = emit_hilo(cx, es, r, "sin_f", cst["rope_sin_hi"],
                                     cst["rope_sin_lo"], [128, T])
        consts["maskt"], _ = emit_hilo(cx, es, r, "maskt_f", cst["rn_maskt_hi"],
                                       cst["rn_maskt_lo"], [128, RN_H * 128],
                                       rearr=hp, out_rearr=hv)
        consts["qsc"], _ = emit_hilo(cx, es, r, "qsc_f", cst["rn_qsc_hi"],
                                     cst["rn_qsc_lo"], [128, RN_H * 128],
                                     rearr=hp, out_rearr=hv)
        consts["ksc"], _ = emit_hilo(cx, es, r, "ksc_f", cst["rn_ksc_hi"],
                                     cst["rn_ksc_lo"], [128, RN_H],
                                     rearr=hp, out_rearr=hv)
        consts["gc"], dep = emit_hilo(cx, es, r, "gc_f", cst["rn_gc_hi"],
                                      cst["rn_gc_lo"], [128, RN_H],
                                      rearr=hp, out_rearr=hv)
        consts["dep"] = (cx.dve, max(dep[1], gnw_dep[1]))

        x_dep = emit_retnet(cx, es, r, a, scr2, gnw_fm, consts, wdone, x_dep)
        emit_store_x(cx, es, r, x_out, out_tm, x_dep)
    return nc


WA_W_SHAPES = [("Wq", (D, D)), ("Wk", (D, D)), ("Wv", (D, D)), ("Wo", (D, D))]
WA_CONST_NAMES = ["w_mask_int", "w_mask_first", "w_mask_last"]
NST = T // 128


def alloc_window(cx, es):
    a = {}
    a["ld_hw"] = cx.new_dma_sem("wa_ld_hw")
    a["xn_full"] = sb(cx, es, "xn_full", [128, NIC * T], BF16)
    a["k_fm"] = sb(cx, es, "wa_kfm", [128, T], BF16)
    a["v_tm"] = sb(cx, es, "wa_vtm", [128, T], BF16)
    a["o_fm"] = sb(cx, es, "wa_ofm", [128, T], BF16)
    a["q_bf"] = sb(cx, es, "wa_qbf", [128, 128], BF16)
    a["s_m"] = sb(cx, es, "wa_sm", [128, 384], F32)
    a["p_f"] = sb(cx, es, "wa_pf", [128, 384], F32)
    a["p_n"] = sb(cx, es, "wa_pn", [128, 384], F32)
    a["pt_bf"] = sb(cx, es, "wa_ptbf", [128, 384], BF16)
    a["rows"] = sb(cx, es, "wa_rows", [128, 3], F32)
    a["wq_t"] = sb(cx, es, "wa_wq", [128, NIC * 128], BF16)
    a["wk_t"] = sb(cx, es, "wa_wk", [128, NIC * 128], BF16)
    a["wv_t"] = sb(cx, es, "wa_wv", [128, NIC * 128], BF16)
    a["wo_t"] = sb(cx, es, "wa_wo", [128, D], BF16)
    return a


def emit_window(cx, es, r, a, scr2, masks, bo_fm, wdone, x_dep):
    x = r["x"]
    xn = a["xn_full"]
    banks = r["ps_mm"]

    for tt in range(NTT):
        t0 = tt * TT
        ln = emit_ln(cx, r, t0, TT, x_dep,
                     xn_dst=lambda ic, t0=t0: xn[:, ic * T + t0: ic * T + t0 + TT])
    xn_dep = (cx.dve, cx.dve.val)
    bfree = list(r["bank_free"])
    b_p0, b_p1, b_s, b_t, b_pv, b_y0, b_y1, b_r = banks

    @cx.block.vector
    def _(v):
        cx.wait(v, "vector", xn_dep[0], xn_dep[1])
        cx.wait_list(v, "vector", bo_fm[1:])
        for oc in range(NIC):
            cx.inc(cx.dve, v.tensor_scalar(x[:, oc * T:(oc + 1) * T],
                                           x[:, oc * T:(oc + 1) * T],
                                           bo_fm[0][:, oc:oc + 1], None, ALU.add))
    bias_done = cx.dve.val

    for hp in range(8):
        @cx.block.sync
        def _(q, hp=hp):
            for nme in ("Wq", "Wk", "Wv", "Wo"):
                cx.wait_list(q, "sync", wdone[nme])
            cx.wait_list(q, "sync", r.get("_stage_gate", []))
            cx.wait(q, "sync", cx.pe, a.get("_wt_pe_free", 0))
            for nme, dst in (("Wq", a["wq_t"]), ("Wk", a["wk_t"]), ("Wv", a["wv_t"])):
                cx.inc(a["ld_hw"], q.dma_start(
                    out=dst[:].rearrange("p (ic o) -> p ic o", ic=NIC),
                    in_=scr2[nme][:, hp * 128:(hp + 1) * 128].rearrange(
                        "(ic p) o -> p ic o", p=128)), 16)
            cx.inc(a["ld_hw"], q.dma_start(
                out=a["wo_t"][:, :], in_=scr2["Wo"][hp * 128:(hp + 1) * 128, :]), 16)
        ld_w = a["ld_hw"].val

        kv_banks = [b_p0, b_p1]
        kv_free = [bfree[0], bfree[1]]
        for tc in range(4):
            bk = kv_banks[tc % 2]

            @cx.block.tensor
            def _(t, hp=hp, tc=tc, bk=bk, ld_w=ld_w):
                cx.wait(t, "tensor", a["ld_hw"], ld_w)
                cx.wait(t, "tensor", xn_dep[0], xn_dep[1])
                fe, fv = kv_free[tc % 2]
                cx.wait(t, "tensor", fe, fv)
                for ic in range(NIC):
                    cx.inc(cx.pe, t.matmul(
                        bk[:, 0:512], a["wk_t"][:, ic * 128:(ic + 1) * 128],
                        xn[:, ic * T + tc * 512: ic * T + (tc + 1) * 512],
                        start=(ic == 0), stop=(ic == NIC - 1)))
            pev = cx.pe.val

            @cx.block.scalar
            def _(sc, tc=tc, bk=bk, pev=pev):
                cx.wait(sc, "scalar", cx.pe, pev)
                cx.inc(cx.act, sc.activation(a["k_fm"][:, tc * 512:(tc + 1) * 512],
                                             bk[:, 0:512], AF.Copy))
            kv_free[tc % 2] = (cx.act, cx.act.val)

        for tc4 in range(4):
            bk = kv_banks[tc4 % 2]

            @cx.block.tensor
            def _(t, hp=hp, tc4=tc4, bk=bk, ld_w=ld_w):
                fe, fv = kv_free[tc4 % 2]
                cx.wait(t, "tensor", fe, fv)
                for j in range(4):
                    tc = tc4 * 4 + j
                    for ic in range(NIC):
                        cx.inc(cx.pe, t.matmul(
                            bk[:, j * 128:(j + 1) * 128],
                            xn[:, ic * T + tc * 128: ic * T + (tc + 1) * 128],
                            a["wv_t"][:, ic * 128:(ic + 1) * 128],
                            start=(ic == 0), stop=(ic == NIC - 1)))
            pev = cx.pe.val

            @cx.block.scalar
            def _(sc, tc4=tc4, bk=bk, pev=pev):
                cx.wait(sc, "scalar", cx.pe, pev)
                cx.inc(cx.act, sc.activation(a["v_tm"][:, tc4 * 512:(tc4 + 1) * 512],
                                             bk[:, 0:512], AF.Copy))
            kv_free[tc4 % 2] = (cx.act, cx.act.val)
        bfree[0], bfree[1] = kv_free

        for st in range(NST):
            if st == 0:
                c0, W, mk = 0, 256, masks["first"]
            elif st == NST - 1:
                c0, W, mk = (st - 1) * 128, 256, masks["last"]
            else:
                c0, W, mk = (st - 1) * 128, 384, masks["int"]
            nblk = W // 128
            emit_window_subtile(cx, r, a, hp, st, c0, W, nblk, mk, bo_fm,
                                xn_dep, bfree)

        o_dep = a["_o_dep"]
        y_banks = [b_y0, b_y1]
        y_free = [bfree[5], bfree[6]]
        for oc in range(NIC):
            for tc in range(4):
                bk = y_banks[(oc * 4 + tc) % 2]

                @cx.block.tensor
                def _(t, hp=hp, oc=oc, tc=tc, bk=bk, o_dep=o_dep):
                    cx.wait(t, "tensor", o_dep[0], o_dep[1])
                    fe, fv = y_free[(oc * 4 + tc) % 2]
                    cx.wait(t, "tensor", fe, fv)
                    cx.inc(cx.pe, t.matmul(bk[:, 0:512],
                                           a["wo_t"][:, oc * 128:(oc + 1) * 128],
                                           a["o_fm"][:, tc * 512:(tc + 1) * 512],
                                           start=True, stop=True))
                pev = cx.pe.val

                @cx.block.vector
                def _(v, hp=hp, oc=oc, tc=tc, bk=bk, pev=pev):
                    cx.wait(v, "vector", cx.pe, pev)
                    cx.wait(v, "vector", cx.dve, bias_done)
                    xs = x[:, oc * T + tc * 512: oc * T + (tc + 1) * 512]
                    cx.inc(cx.dve, v.tensor_tensor(xs, xs, bk[:, 0:512], ALU.add))
                y_free[(oc * 4 + tc) % 2] = (cx.dve, cx.dve.val)
        bfree[5], bfree[6] = y_free
        a["_wt_pe_free"] = cx.pe.val

    r["bank_free"] = list(bfree)
    return (cx.dve, cx.dve.val)


def emit_window_subtile(cx, r, a, hp, st, c0, W, nblk, mk, bo_fm, xn_dep, bfree):
    xn = a["xn_full"]
    banks = r["ps_mm"]
    b_p0, b_p1, b_s, b_t, b_pv, b_y0, b_y1, b_r = banks
    mk_tile, mk_dep = mk

    @cx.block.tensor
    def _(t):
        fe, fv = bfree[7]
        cx.wait(t, "tensor", fe, fv)
        for ic in range(NIC):
            cx.inc(cx.pe, t.matmul(b_r[:, 0:128],
                                   a["wq_t"][:, ic * 128:(ic + 1) * 128],
                                   xn[:, ic * T + st * 128: ic * T + (st + 1) * 128],
                                   start=(ic == 0), stop=(ic == NIC - 1)))
    pe_q = cx.pe.val

    @cx.block.vector
    def _(v):
        cx.wait(v, "vector", cx.pe, pe_q)
        cx.wait(v, "vector", cx.pe, a.get("_q_pe_cons", 0))
        cx.inc(cx.dve, v.tensor_scalar(a["q_bf"][:, :], b_r[:, 0:128],
                                       1.0 / 8.0, None, ALU.mult))
    q_dve = cx.dve.val
    bfree[7] = (cx.dve, q_dve)

    for h2 in range(2):
        @cx.block.tensor
        def _(t, h2=h2):
            cx.wait(t, "tensor", cx.dve, q_dve)
            fe, fv = bfree[2]
            cx.wait(t, "tensor", fe, fv)
            cx.inc(cx.pe, t.matmul(b_s[:, 0:W],
                                   a["q_bf"][h2 * 64:(h2 + 1) * 64, :],
                                   a["k_fm"][h2 * 64:(h2 + 1) * 64, c0:c0 + W],
                                   start=True, stop=True))
        pe_s = cx.pe.val
        a["_q_pe_cons"] = pe_s

        @cx.block.vector
        def _(v, h2=h2):
            cx.wait(v, "vector", cx.pe, pe_s)
            cx.wait(v, "vector", mk_dep[0], mk_dep[1])
            cx.wait(v, "vector", cx.act, a.get("_sm_act", 0))
            cx.inc(cx.dve, v.tensor_tensor(a["s_m"][:, 0:W], b_s[:, 0:W],
                                           mk_tile[:, 0:W], ALU.add))
            cx.wait(v, "vector", cx.dve, cx.dve.val)
            cx.inc(cx.dve, v.tensor_reduce(a["rows"][:, 0:1], a["s_m"][:, 0:W],
                                           AX.X, ALU.max, negate=True))
        dve_sm = cx.dve.val
        bfree[2] = (cx.dve, dve_sm)

        @cx.block.scalar
        def _(sc, h2=h2):
            cx.wait(sc, "scalar", cx.dve, dve_sm)
            cx.inc(cx.act, sc.activation(a["p_f"][:, 0:W], a["s_m"][:, 0:W], AF.Exp,
                                         bias=a["rows"][:, 0:1],
                                         accum_out=a["rows"][:, 1:2]))
        act_p = cx.act.val
        a["_sm_act"] = act_p

        @cx.block.vector
        def _(v, h2=h2):
            cx.wait(v, "vector", cx.act, act_p)
            cx.wait(v, "vector", cx.pe, a.get("_pn_pe", 0))
            cx.inc(cx.dve, v.reciprocal(a["rows"][:, 2:3], a["rows"][:, 1:2]))
            cx.wait(v, "vector", cx.dve, cx.dve.val)
            cx.inc(cx.dve, v.tensor_scalar(a["p_n"][:, 0:W], a["p_f"][:, 0:W],
                                           a["rows"][:, 2:3], None, ALU.mult))
        dve_pn = cx.dve.val

        @cx.block.tensor
        def _(t, h2=h2):
            cx.wait(t, "tensor", cx.dve, dve_pn)
            fe, fv = bfree[3]
            cx.wait(t, "tensor", fe, fv)
            for j in range(nblk):
                cx.inc(cx.pe, t.transpose(b_t[:, j * 128:(j + 1) * 128],
                                          a["p_n"][:, j * 128:(j + 1) * 128],
                                          r["id32"][:, :]))
        pe_pt = cx.pe.val
        a["_pn_pe"] = pe_pt

        @cx.block.vector
        def _(v, h2=h2):
            cx.wait(v, "vector", cx.pe, pe_pt)
            cx.wait(v, "vector", cx.pe, a.get("_pt_pe_cons", 0))
            cx.inc(cx.dve, v.tensor_copy(a["pt_bf"][:, 0:nblk * 128], b_t[:, 0:nblk * 128]))
        dve_pt = cx.dve.val
        bfree[3] = (cx.dve, dve_pt)

        @cx.block.tensor
        def _(t, h2=h2):
            cx.wait(t, "tensor", cx.dve, dve_pt)
            fe, fv = bfree[4]
            cx.wait(t, "tensor", fe, fv)
            for j in range(nblk):
                cx.inc(cx.pe, t.matmul(b_pv[0:64, 0:128],
                                       a["v_tm"][:, c0 + j * 128 + h2 * 64: c0 + j * 128 + (h2 + 1) * 64],
                                       a["pt_bf"][:, j * 128:(j + 1) * 128],
                                       start=(j == 0), stop=(j == nblk - 1)))
        pe_pv = cx.pe.val
        a["_pt_pe_cons"] = pe_pv

        @cx.block.scalar
        def _(sc, h2=h2, st=st):
            cx.wait(sc, "scalar", cx.pe, pe_pv)
            cx.inc(cx.act, sc.activation(
                a["o_fm"][h2 * 64:(h2 + 1) * 64, st * 128:(st + 1) * 128],
                b_pv[0:64, 0:128], AF.Copy))
        bfree[4] = (cx.act, cx.act.val)
    a["_o_dep"] = (cx.act, cx.act.val)


def build_window_prog(in_tm=False, out_tm=False):
    nc = bass.Bass("TRN2", target_bir_lowering=False, debug=False)
    x_in = nc.dram_tensor("x_in", [T, D] if in_tm else [D, T], F32,
                          kind="ExternalInput").ap()
    x_out = nc.dram_tensor("x_out", [T, D] if out_tm else [D, T], F32,
                           kind="ExternalOutput").ap()
    w, scr1, scr2 = {}, {}, {}
    for nme, shape in WA_W_SHAPES:
        w[nme] = nc.dram_tensor(nme, list(shape), F32, kind="ExternalInput").ap()
        scr1[nme] = nc.dram_tensor("scr1_" + nme, list(shape), BF16, kind="Internal").ap()
        scr2[nme] = nc.dram_tensor("scr2_" + nme, [shape[1], shape[0]], BF16,
                                   kind="Internal").ap()
    bo = nc.dram_tensor("bo", [D], F32, kind="ExternalInput").ap()
    id32 = nc.dram_tensor("ident_f32", [128, 128], F32, kind="ExternalInput").ap()
    cst = {}
    for nme in WA_CONST_NAMES:
        shp = [128, 384] if nme == "w_mask_int" else [128, 256]
        for p in ("_hi", "_lo"):
            cst[nme + p] = nc.dram_tensor(nme + p, shp, F32, kind="ExternalInput").ap()

    with ExitStack() as es:
        cx = Ctx(nc, es)
        r = alloc_common(cx, es)
        a = alloc_window(cx, es)
        r["hl_scr"] = sb(cx, es, "hl_scr", [128, 768], F32)

        @cx.block.sync
        def _(q):
            cx.inc(r["ld_id"], q.dma_start(out=r["id32"][:, :], in_=id32[:, :]), 16)

        x_dep = emit_load_x(cx, es, r, x_in, in_tm)
        pp_rings = [a["xn_full"][:, 0:4096], a["xn_full"][:, 4096:8192]]
        wdone = emit_prepass(cx, es, r, [(n, w[n], scr1[n], scr2[n])
                                         for n, _ in WA_W_SHAPES], rings=pp_rings)
        r["_ln_gate"] = r["_pp_all"]
        bo_fm = emit_load_scale_fm(cx, es, r, "bo", bo, D)
        masks = {}
        masks["int"] = emit_hilo(cx, es, r, "mk_int", cst["w_mask_int_hi"],
                                 cst["w_mask_int_lo"], [128, 384])
        masks["first"] = emit_hilo(cx, es, r, "mk_first", cst["w_mask_first_hi"],
                                   cst["w_mask_first_lo"], [128, 256])
        masks["last"] = emit_hilo(cx, es, r, "mk_last", cst["w_mask_last_hi"],
                                  cst["w_mask_last_lo"], [128, 256])

        x_dep = emit_window(cx, es, r, a, scr2, masks, bo_fm, wdone, x_dep)
        emit_store_x(cx, es, r, x_out, out_tm, x_dep)
    return nc


FUSED_FFN = {0: [("w1", (HID, D)), ("w2", (HID, D)), ("h1", (HID, HID)),
                 ("h2", (HID, HID)), ("w3", (D, HID))]}


def _stage_gate(cx, r):
    g = [(cx.pe, cx.pe.val), (cx.dve, cx.dve.val), (cx.act, cx.act.val),
         (cx.pool, cx.pool.val)]
    r["_stage_gate"] = g
    return g


def build_fused_prog(n_stages=6):
    nc = bass.Bass("TRN2", target_bir_lowering=False, debug=False)
    x_in = nc.dram_tensor("x_in", [T, D], F32, kind="ExternalInput").ap()
    x_out = nc.dram_tensor("x_out", [T, D], F32, kind="ExternalOutput").ap()
    id32 = nc.dram_tensor("ident_f32", [128, 128], F32, kind="ExternalInput").ap()

    wnames = []
    for li in (0, 1):
        for nme, shape in [("w1", (HID, D)), ("w2", (HID, D)), ("h1", (HID, HID)),
                           ("h2", (HID, HID)), ("w3", (D, HID))]:
            wnames.append((f"L{li}_ffn_{nme}_w", shape))
    for nme, shape in RN_W_SHAPES:
        wnames.append((f"L0_attn_{nme}", shape))
    for nme, shape in WA_W_SHAPES:
        wnames.append((f"L1_attn_{nme}", shape))
    w, scr1, scr2 = {}, {}, {}
    for nme, shape in wnames:
        w[nme] = nc.dram_tensor(nme, list(shape), F32, kind="ExternalInput").ap()
        scr1[nme] = nc.dram_tensor("s1_" + nme, list(shape), BF16, kind="Internal").ap()
        scr2[nme] = nc.dram_tensor("s2_" + nme, [shape[1], shape[0]], BF16,
                                   kind="Internal").ap()
    sv = {}
    for li in (0, 1):
        for nme in ("w1", "w2", "h1", "h2"):
            sv[f"L{li}_ffn_{nme}_s"] = nc.dram_tensor(f"L{li}_ffn_{nme}_s", [HID], F32,
                                                      kind="ExternalInput").ap()
        sv[f"L{li}_ffn_w3_s"] = nc.dram_tensor(f"L{li}_ffn_w3_s", [D], F32,
                                               kind="ExternalInput").ap()
    gnw = nc.dram_tensor("L0_attn_g_norm_w", [RN_DV], F32, kind="ExternalInput").ap()
    bo = nc.dram_tensor("L1_attn_bo", [D], F32, kind="ExternalInput").ap()
    cst = {}
    for nme in RN_CONST_NAMES:
        shp = {"rope_cos": [128, T], "rope_sin": [128, T],
               "rn_maskt": [RN_H * 128, 128], "rn_qsc": [RN_H * 128, 128],
               "rn_ksc": [RN_H * 128, 1], "rn_gc": [RN_H * 128, 1]}[nme]
        for p in ("_hi", "_lo"):
            cst[nme + p] = nc.dram_tensor(nme + p, shp, F32, kind="ExternalInput").ap()
    for nme in WA_CONST_NAMES:
        shp = [128, 384] if nme == "w_mask_int" else [128, 256]
        for p in ("_hi", "_lo"):
            cst[nme + p] = nc.dram_tensor(nme + p, shp, F32, kind="ExternalInput").ap()

    with ExitStack() as es:
        cx = Ctx(nc, es)
        A_bf = sb(cx, es, "A_bf", [128, 32768], BF16)
        A_f32 = sb(cx, es, "A_f32", [128, 6208], F32)
        B_bf = sb(cx, es, "B_bf", [128, 16384], BF16)
        B_f32 = sb(cx, es, "B_f32", [128, 1280], F32)

        r = {}
        r["x"] = sb(cx, es, "x_res", [128, NIC * T], F32)
        r["ln_rows"] = sb(cx, es, "ln_rows", [1, 2 * TT], F32)
        r["ones_bf"] = sb(cx, es, "ones_bf", [128, 1], BF16)
        r["ones_f1"] = sb(cx, es, "ones_f1", [1, 128], F32)
        r["eps_t"] = sb(cx, es, "eps_t", [1, 1], F32)
        r["id32"] = sb(cx, es, "id32", [128, 128], F32)
        r["x_stg"] = B_f32[:, 0:1024]
        r["ln_tmp"] = B_f32[:, 0:1024]
        r["ps_mm"] = [ps(cx, es, f"psb{i}", [128, 512]) for i in range(8)]
        r["bank_free"] = [(cx.pe, 0)] * 8
        r["ld_pro"] = cx.new_dma_sem("ld_pro")
        r["ld_id"] = cx.new_dma_sem("ld_id")
        r["ld_x"] = cx.new_dma_sem("ld_x")
        r["_w1_act"] = {}
        r["_ln_pe_prev"] = 0
        r["hl_scr"] = A_f32[:, 0:4096]

        @cx.block.gpsimd
        def _(gp):
            cx.inc(cx.pool, gp.memset(r["ones_bf"][:, :], 1.0))
            cx.inc(cx.pool, gp.memset(r["ones_f1"][:, :], 1.0))
            cx.inc(cx.pool, gp.memset(r["eps_t"][:, :], EPS))
        r["ones_dep"] = cx.pool.val

        @cx.block.sync
        def _(q):
            cx.inc(r["ld_id"], q.dma_start(out=r["id32"][:, :], in_=id32[:, :]), 16)

        x_dep = emit_load_x(cx, es, r, x_in, True)

        pp_rings = [sb(cx, es, f"ppw{i}", [128, 1024], BF16)[:] for i in range(4)]
        plan = []
        order = ([f"L0_ffn_{n}_w" for n in ("w1", "w2", "h1", "h2", "w3")] +
                 [f"L0_attn_{n}" for n, _ in RN_W_SHAPES] +
                 [f"L1_ffn_{n}_w" for n in ("w1", "w2", "h1", "h2", "w3")] +
                 [f"L1_attn_{n}" for n, _ in WA_W_SHAPES])
        for nme in order:
            plan.append((nme, w[nme], scr1[nme], scr2[nme]))
        pp = PrepassStream(cx, es, r, plan, pp_rings)
        r["_pp_step"] = lambda: pp.step(1)

        class WdoneView:
            def __init__(self, mapping):
                self.mapping = mapping

            def __getitem__(self, key):
                full = self.mapping[key]
                pp.drain(full)
                return pp.wdone(full)

        scales = {}
        for li in (0, 1):
            for nme in ("w1", "w2", "h1", "h2", "w3"):
                key = f"L{li}_ffn_{nme}_s"
                scales[key] = emit_load_scale_fm(cx, es, r, key, sv[key],
                                                 HID if nme != "w3" else D)
        gnw_fm, gnw_dep = emit_load_scale_fm(cx, es, r, "gnw", gnw, RN_DV)
        bo_fm = emit_load_scale_fm(cx, es, r, "bo", bo, D)

        def bind_ffn():
            r["h_a"] = A_bf[:, 0:16384]
            r["z1"] = A_bf[:, 16384:32768]
            r["xn_t"] = B_bf[:, 0:4096]
            r["wslc"] = [B_bf[:, 4096 + i * 1024:4096 + (i + 1) * 1024] for i in range(4)]
            r["scrA"] = B_bf[:, 8192:12288]
            r["scrB"] = B_bf[:, 12288:16384]
            r["ytmp"] = B_f32[:, 0:1024]

        if "wslc_sem" not in r:
            r["wslc_sem"] = [cx.new_dma_sem(f"wsl{i}") for i in range(4)]
            r["wslc_free"] = [0, 0, 0, 0]
            r["wslc_next"] = [0]

        def ffn_maps(li):
            s2 = {n: scr2[f"L{li}_ffn_{n}_w"] for n in ("w1", "w2", "h1", "h2", "w3")}
            sc = {n + "_s": scales[f"L{li}_ffn_{n}_s"] for n in ("w1", "w2", "h1", "h2", "w3")}
            wd = WdoneView({n: f"L{li}_ffn_{n}_w" for n in ("w1", "w2", "h1", "h2", "w3")})
            return s2, sc, wd

        a_rn = {
            "ld_hw": cx.new_dma_sem("rn_ld_hw"),
            "ld_hw2": cx.new_dma_sem("rn_ld_hw2"),
            "xn_full": A_bf[:, 0:16384],
            "H": A_f32[:, 0:4096],
            "H_bf": A_bf[:, 16384:17408],
            "wq_t": B_bf[:, 0:2048],
            "wk_t": B_bf[:, 2048:4096],
            "wv_t": B_bf[:, 4096:8192],
            "wg_t": B_bf[:, 8192:12288],
            "wo_t": B_bf[:, 12288:16384],
            "q_bf": A_bf[:, 21504:21760],
            "qt_bf": A_bf[:, 21760:22016],
            "k_bf": A_bf[:, 22016:22272],
            "k_tm": A_bf[:, 22272:22528],
            "v_bf": A_bf[:, 22528:23040],
            "g_si": A_bf[:, 23040:23552],
            "s_bf": A_bf[:, 23552:23680],
            "o_bf": A_bf[:, 23680:24192],
            "o_g": A_bf[:, 23680:24192],
            "osq": A_bf[:, 28288:28800],
            "k_f32": A_f32[:, 5128:5384],
            "t128": A_f32[:, 5384:5896],
            "rrow": A_f32[0:1, 5896:6152],
        }
        rn_scr2 = {n: scr2[f"L0_attn_{n}"] for n, _ in RN_W_SHAPES}
        rn_wdone = WdoneView({n: f"L0_attn_{n}" for n, _ in RN_W_SHAPES})

        a_wa = {
            "ld_hw": cx.new_dma_sem("wa_ld_hw"),
            "xn_full": A_bf[:, 0:16384],
            "k_fm": A_bf[:, 16384:18432],
            "v_tm": A_bf[:, 18432:20480],
            "o_fm": A_bf[:, 20480:22528],
            "q_bf": B_bf[:, 4480:4608],
            "pt_bf": B_bf[:, 4096:4480],
            "s_m": B_f32[:, 0:384],
            "p_f": B_f32[:, 384:768],
            "p_n": B_f32[:, 768:1152],
            "rows": A_f32[:, 6152:6155],
            "wq_t": B_bf[:, 0:1024],
            "wk_t": B_bf[:, 1024:2048],
            "wv_t": B_bf[:, 2048:3072],
            "wo_t": B_bf[:, 3072:4096],
        }
        wa_scr2 = {n: scr2[f"L1_attn_{n}"] for n, _ in WA_W_SHAPES}
        wa_wdone = WdoneView({n: f"L1_attn_{n}" for n, _ in WA_W_SHAPES})

        def bind_attn_scr():
            r["scrA"] = A_bf[:, 24192:28288]
            r["scrB"] = A_bf[:, 28288:32384]

        def bind_wa_scr():
            r["scrA"] = A_bf[:, 24192:28288]
            r["scrB"] = A_bf[:, 28288:32384]

        def stage_ffn(li):
            nonlocal x_dep
            bind_ffn()
            _stage_gate(cx, r)
            s2m, scm, wdm = ffn_maps(li)
            x_dep = emit_ffn(cx, es, r, s2m, scm, wdm, x_dep)

        def stage_rn():
            nonlocal x_dep
            gate = _stage_gate(cx, r)
            r["_hl_gate"] = gate
            hp_re = lambda ap: ap.rearrange("(h p) c -> p h c", p=128)
            hv_re = lambda ap: ap.rearrange("p (h c) -> p h c", h=RN_H)
            consts = {}
            consts["cos"], _ = emit_hilo(cx, es, r, "cos_f", cst["rope_cos_hi"],
                                         cst["rope_cos_lo"], [128, T],
                                         out=A_bf[:, 17408:19456])
            consts["sin"], _ = emit_hilo(cx, es, r, "sin_f", cst["rope_sin_hi"],
                                         cst["rope_sin_lo"], [128, T],
                                         out=A_bf[:, 19456:21504])
            consts["maskt"], _ = emit_hilo(cx, es, r, "maskt_f", cst["rn_maskt_hi"],
                                           cst["rn_maskt_lo"], [128, RN_H * 128],
                                           rearr=hp_re, out_rearr=hv_re,
                                           out=A_f32[:, 4096:4608])
            consts["qsc"], _ = emit_hilo(cx, es, r, "qsc_f", cst["rn_qsc_hi"],
                                         cst["rn_qsc_lo"], [128, RN_H * 128],
                                         rearr=hp_re, out_rearr=hv_re,
                                         out=A_f32[:, 4608:5120])
            consts["ksc"], _ = emit_hilo(cx, es, r, "ksc_f", cst["rn_ksc_hi"],
                                         cst["rn_ksc_lo"], [128, RN_H],
                                         rearr=hp_re, out_rearr=hv_re,
                                         out=A_f32[:, 5120:5124])
            consts["gc"], cdep = emit_hilo(cx, es, r, "gc_f", cst["rn_gc_hi"],
                                           cst["rn_gc_lo"], [128, RN_H],
                                           rearr=hp_re, out_rearr=hv_re,
                                           out=A_f32[:, 5124:5128])
            consts["dep"] = (cx.dve, max(cdep[1], gnw_dep[1]))
            bind_attn_scr()
            _stage_gate(cx, r)
            x_dep = emit_retnet(cx, es, r, a_rn, rn_scr2, gnw_fm, consts, rn_wdone, x_dep)

        def stage_wa():
            nonlocal x_dep
            gate = _stage_gate(cx, r)
            r["_hl_gate"] = gate
            masks = {}
            masks["int"] = emit_hilo(cx, es, r, "mk_int", cst["w_mask_int_hi"],
                                     cst["w_mask_int_lo"], [128, 384],
                                     out=A_f32[:, 4096:4480])
            masks["first"] = emit_hilo(cx, es, r, "mk_first", cst["w_mask_first_hi"],
                                       cst["w_mask_first_lo"], [128, 256],
                                       out=A_f32[:, 4480:4736])
            masks["last"] = emit_hilo(cx, es, r, "mk_last", cst["w_mask_last_hi"],
                                      cst["w_mask_last_lo"], [128, 256],
                                      out=A_f32[:, 4736:4992])
            bind_wa_scr()
            _stage_gate(cx, r)
            x_dep = emit_window(cx, es, r, a_wa, wa_scr2, masks, bo_fm, wa_wdone, x_dep)

        stages = [lambda: stage_ffn(0), stage_rn, lambda: stage_ffn(0),
                  lambda: stage_ffn(1), stage_wa, lambda: stage_ffn(1)]
        for si in range(n_stages):
            stages[si]()

        emit_store_x(cx, es, r, x_out, True, x_dep)
    return nc


def fused_im(inputs):
    im = {"ident_f32": _CONSTS["ident_f32"]}
    for li in (0, 1):
        for nme in ("w1", "w2", "h1", "h2", "w3"):
            im[f"L{li}_ffn_{nme}_w"] = inputs[f"L{li}_ffn_{nme}_w"]
            im[f"L{li}_ffn_{nme}_s"] = inputs[f"L{li}_ffn_{nme}_s"]
    for nme, _ in RN_W_SHAPES:
        im["L0_attn_" + nme] = inputs["L0_attn_" + nme]
    im["L0_attn_g_norm_w"] = inputs["L0_attn_g_norm_w"]
    for nme, _ in WA_W_SHAPES:
        im["L1_attn_" + nme] = inputs["L1_attn_" + nme]
    im["L1_attn_bo"] = inputs["L1_attn_bo"]
    for nme in RN_CONST_NAMES + WA_CONST_NAMES:
        for p in ("_hi", "_lo"):
            im[nme + p] = _CONSTS[nme + p]
    return im


_PROGS = {}


def _prog(kind, in_tm, out_tm):
    key = (kind, in_tm, out_tm)
    if key not in _PROGS:
        if kind == "ffn":
            _PROGS[key] = build_ffn_prog(in_tm, out_tm)
        elif kind == "rn":
            _PROGS[key] = build_retnet_prog(in_tm, out_tm)
        else:
            _PROGS[key] = build_window_prog(in_tm, out_tm)
    return _PROGS[key]


def _ffn_im(inputs, layer):
    p = f"L{layer}_ffn_"
    im = {"ident_f32": _CONSTS["ident_f32"]}
    for nme in ("w1", "w2", "h1", "h2", "w3"):
        im[nme] = inputs[p + nme + "_w"]
        im[nme + "_s"] = inputs[p + nme + "_s"]
    return im


def _rn_im(inputs):
    im = {"ident_f32": _CONSTS["ident_f32"], "g_norm_w": inputs["L0_attn_g_norm_w"]}
    for nme in ("Wq", "Wk", "Wv", "Wg", "Wo"):
        im[nme] = inputs["L0_attn_" + nme]
    for nme in RN_CONST_NAMES:
        for pfx in ("_hi", "_lo"):
            im[nme + pfx] = _CONSTS[nme + pfx]
    return im


def _wa_im(inputs):
    im = {"ident_f32": _CONSTS["ident_f32"], "bo": inputs["L1_attn_bo"]}
    for nme in ("Wq", "Wk", "Wv", "Wo"):
        im[nme] = inputs["L1_attn_" + nme]
    for nme in WA_CONST_NAMES:
        for pfx in ("_hi", "_lo"):
            im[nme + pfx] = _CONSTS[nme + pfx]
    return im


def kernel(**inputs):
    x = np.ascontiguousarray(inputs["L0_attn_x"], dtype=np.float32)
    n = 8
    if "fused" not in _PROGS:
        _PROGS["fused"] = build_fused_prog()
    nc = _PROGS["fused"]
    wim = fused_im(inputs)
    in_maps = [dict(wim, x_in=np.ascontiguousarray(x[b])) for b in range(n)]
    res = run_bass_kernel_spmd(nc, in_maps, core_ids=list(range(n)))
    return np.stack([res.results[b]["x_out"] for b in range(n)], axis=0)


def kernel_unfused(**inputs):
    x = np.ascontiguousarray(inputs["L0_attn_x"], dtype=np.float32)
    n = 8
    shards = [np.ascontiguousarray(x[b]) for b in range(n)]

    seq = [
        ("ffn", _ffn_im(inputs, 0), True, False),
        ("rn", _rn_im(inputs), False, False),
        ("ffn", _ffn_im(inputs, 0), False, False),
        ("ffn", _ffn_im(inputs, 1), False, False),
        ("wa", _wa_im(inputs), False, False),
        ("ffn", _ffn_im(inputs, 1), False, True),
    ]
    cur = shards
    for (kind, wim, in_tm, out_tm) in seq:
        nc = _prog(kind, in_tm, out_tm)
        in_maps = [dict(wim, x_in=cur[b]) for b in range(n)]
        res = run_bass_kernel_spmd(nc, in_maps, core_ids=list(range(n)))
        cur = [np.ascontiguousarray(res.results[b]["x_out"]) for b in range(n)]
    return np.stack(cur, axis=0)


# version 05
# speedup vs baseline: 1.0594x; 1.0594x over previous
import numpy as np
import ml_dtypes
from contextlib import ExitStack

import concourse.bass as bass
import concourse.mybir as mybir
from concourse.bass_utils import run_bass_kernel_spmd

F32 = mybir.dt.float32
BF16 = mybir.dt.bfloat16
AF = mybir.ActivationFunctionType
ALU = mybir.AluOpType
AX = mybir.AxisListType

D = 1024
T = 2048
HID = 4096
TT = 512
NTT = T // TT
NIC = D // 128
EPS = 1e-5

RN_H, RN_DK, RN_DV, CHUNK = 4, 256, 512, 128
W_H, W_DK, W_WIN = 16, 64, 64


def _hilo(x64):
    hi = x64.astype(np.float32)
    lo = (x64 - hi.astype(np.float64)).astype(np.float32)
    return hi, lo


def build_consts():
    c = {}
    c["ident_f32"] = np.eye(128, dtype=np.float32)

    j = np.arange(0, RN_DK, 2, dtype=np.float64) / RN_DK
    inv_freq = 1.0 / (10000.0 ** j)
    ang = inv_freq[:, None] * np.arange(T, dtype=np.float64)[None, :]
    c["rope_cos_hi"], c["rope_cos_lo"] = _hilo(np.cos(ang))
    c["rope_sin_hi"], c["rope_sin_lo"] = _hilo(np.sin(ang))

    r = np.arange(CHUNK, dtype=np.float64)
    maskt = np.zeros((RN_H * 128, CHUNK))
    qsc = np.zeros((RN_H * 128, CHUNK))
    ksc = np.zeros((RN_H * 128, 1))
    gC = np.zeros((RN_H * 128, 1))
    for h in range(RN_H):
        lg = float(np.log1p(-np.exp2(-5.0 - h)))
        diff = r[None, :] - r[:, None]
        maskt[h * 128:(h + 1) * 128, :] = np.where(diff >= 0, np.exp(lg * diff) / 16.0, 0.0)
        qsc[h * 128:(h + 1) * 128, :] = np.exp(lg * (r[None, :] + 1.0)) / 16.0
        ksc[h * 128:(h + 1) * 128, 0] = np.exp(lg * (CHUNK - 1.0 - r))
        gC[h * 128:(h + 1) * 128, 0] = np.exp(lg * CHUNK)
    c["rn_maskt_hi"], c["rn_maskt_lo"] = _hilo(maskt)
    c["rn_qsc_hi"], c["rn_qsc_lo"] = _hilo(qsc)
    c["rn_ksc_hi"], c["rn_ksc_lo"] = _hilo(ksc)
    c["rn_gc_hi"], c["rn_gc_lo"] = _hilo(gC)

    NEG = -1e30
    mi = np.full((128, 384), NEG)
    for rr in range(128):
        mi[rr, 96 + rr:160 + rr] = 0.0
    c["w_mask_int_hi"], c["w_mask_int_lo"] = _hilo(mi)
    mf = np.full((128, 256), NEG)
    for rr in range(128):
        lo_k, hi_k = rr - 32, min(rr + 31, 255)
        if lo_k < 0:
            mf[rr, 0] = np.log(float(1 - lo_k))
            mf[rr, 1:hi_k + 1] = 0.0
        else:
            mf[rr, lo_k:hi_k + 1] = 0.0
    c["w_mask_first_hi"], c["w_mask_first_lo"] = _hilo(mf)
    ml = np.full((128, 256), NEG)
    for rr in range(128):
        t = 1920 + rr
        lo_k, hi_k = t - 32 - 1792, t + 31 - 1792
        if hi_k > 255:
            ml[rr, 255] = np.log(float(hi_k - 255 + 1))
            ml[rr, lo_k:255] = 0.0
        else:
            ml[rr, lo_k:hi_k + 1] = 0.0
    c["w_mask_last_hi"], c["w_mask_last_lo"] = _hilo(ml)
    return c


_CONSTS = build_consts()


class Eng:
    def __init__(self, name, sem):
        self.name = name
        self.sem = sem
        self.val = 0


class Ctx:
    def __init__(self, nc, es: ExitStack):
        self.nc = nc
        self.es = es
        E = es.enter_context
        self.pe = Eng("pe", E(nc.semaphore("pe_s")))
        self.dve = Eng("dve", E(nc.semaphore("dve_s")))
        self.act = Eng("act", E(nc.semaphore("act_s")))
        self.pool = Eng("pool", E(nc.semaphore("pool_s")))
        self.ld_sp = Eng("ld_sp", E(nc.semaphore("ld_sp_s")))
        self.ld_act = Eng("ld_act", E(nc.semaphore("ld_act_s")))
        self.ldg = Eng("ldg", E(nc.semaphore("ldg_s")))
        self.st = Eng("st", E(nc.semaphore("st_s")))
        self.block = E(nc.Block())
        self._waited = {}

    def wait(self, q, qname, eng, val):
        if val <= 0:
            return
        assert val <= eng.val, (
            f"wait for {eng.name}>={val} but only {eng.val} scheduled (hang)")
        key = (qname, eng.name)
        if self._waited.get(key, 0) >= val:
            return
        q.wait_ge(eng.sem, val)
        self._waited[key] = val

    def inc(self, eng, instr, n=1):
        instr.then_inc(eng.sem, n)
        eng.val += n
        return eng.val

    def new_dma_sem(self, name):
        return Eng(name, self.es.enter_context(self.nc.semaphore(name)))

    def wait_list(self, q, qname, deps):
        for (eng, val) in deps:
            self.wait(q, qname, eng, val)


def sb(cx, es, name, shape, dtype):
    return es.enter_context(cx.nc.sbuf_tensor(name, shape, dtype))


def ps(cx, es, name, shape, dtype=F32):
    return es.enter_context(cx.nc.psum_tensor(name, shape, dtype))


def alloc_common(cx, es):
    r = {}
    r["x"] = sb(cx, es, "x_res", [128, NIC * T], F32)
    r["scrA"] = sb(cx, es, "scrA", [128, NIC * TT], BF16)
    r["scrB"] = sb(cx, es, "scrB", [128, NIC * TT], BF16)
    r["ln_rows"] = sb(cx, es, "ln_rows", [1, 2 * TT], F32)
    r["ones_bf"] = sb(cx, es, "ones_bf", [128, 1], BF16)
    r["ones_f1"] = sb(cx, es, "ones_f1", [1, 128], F32)
    r["eps_t"] = sb(cx, es, "eps_t", [1, 1], F32)
    r["id32"] = sb(cx, es, "id32", [128, 128], F32)
    r["id_bf"] = sb(cx, es, "id_bf", [128, 128], BF16)
    r["x_stg"] = sb(cx, es, "x_stg", [128, D], F32)
    r["ln_tmp"] = r["x_stg"]
    r["ps_mm"] = [ps(cx, es, f"psb{i}", [128, 512]) for i in range(8)]
    r["bank_free"] = [(cx.pe, 0)] * 8
    r["ld_pro"] = cx.new_dma_sem("ld_pro")
    r["ld_id"] = cx.new_dma_sem("ld_id")
    r["ld_x"] = cx.new_dma_sem("ld_x")
    r["_w1_act"] = {}
    r["_ln_pe_prev"] = 0

    @cx.block.gpsimd
    def _(gp):
        cx.inc(cx.pool, gp.memset(r["ones_bf"][:, :], 1.0))
        cx.inc(cx.pool, gp.memset(r["ones_f1"][:, :], 1.0))
        cx.inc(cx.pool, gp.memset(r["eps_t"][:, :], EPS))
    r["ones_dep"] = cx.pool.val
    return r


class PrepassStream:

    def __init__(self, cx, es, r, plan, rings):
        self.cx = cx
        self.r = r
        self.rings = rings
        self.nring = len(rings)
        self.ring_cols = rings[0].shape[1]
        self.ppl = [cx.new_dma_sem(f"ppl{i}") for i in range(self.nring)]
        self.pps = [cx.new_dma_sem(f"pps{i}") for i in range(self.nring)]
        self.ring_free = [0] * self.nring
        self.steps = []
        self.cast_done = {}
        self.cast_info = {}
        self.worder = []
        self.done = {}
        self.idx = 0
        self.ring_i = 0
        self.limit = None
        for (name, w, scr1, scr2) in plan:
            O, I = w.shape
            self.cast_info[name] = (w, scr1, O, I)
            self.worder.append(name)
            for ic in range(I // 128):
                for o0 in range(0, O, self.ring_cols):
                    o1 = min(O, o0 + self.ring_cols)
                    self.steps.append((name, scr1, scr2, ic, o0, o1))

    def _ensure_cast(self, name):
        if name in self.cast_done:
            return
        cx = self.cx
        w, scr1, O, I = self.cast_info[name]
        ldg_w = cx.new_dma_sem("ldg_" + name)

        @cx.block.gpsimd
        def _(gp, w=w, scr1=scr1, O=O, I=I, ldg_w=ldg_w):
            rows_per = max(128, min(O, (8 << 20) // (I * 2)))
            r0 = 0
            while r0 < O:
                r1 = min(O, r0 + rows_per)
                cx.inc(ldg_w, gp.dma_start(out=scr1[r0:r1, :], in_=w[r0:r1, :]), 16)
                r0 = r1
        self.cast_done[name] = (ldg_w, ldg_w.val)

    def step(self, k=1, ignore_limit=False):
        cx = self.cx
        for _ in range(k):
            if self.idx >= len(self.steps):
                return
            if not ignore_limit and self.limit is not None and self.idx >= self.limit:
                return
            name, scr1, scr2, ic, o0, o1 = self.steps[self.idx]
            self.idx += 1
            self._ensure_cast(name)
            wi = self.worder.index(name)
            if wi + 1 < len(self.worder):
                self._ensure_cast(self.worder[wi + 1])
            b = self.ring_i % self.nring
            self.ring_i += 1
            ldg_w, cval = self.cast_done[name]

            @cx.block.sync
            def _(q, scr1=scr1, ic=ic, b=b, o0=o0, o1=o1, ldg_w=ldg_w, cval=cval):
                cx.wait(q, "sync", ldg_w, cval)
                cx.wait(q, "sync", self.pps[b], self.ring_free[b])
                for r0 in range(o0, o1, 1024):
                    r1 = min(o1, r0 + 1024)
                    cx.inc(self.ppl[b], q.dma_start(
                        out=self.rings[b][:, r0 - o0:r1 - o0],
                        in_=scr1[r0:r1, ic * 128:(ic + 1) * 128],
                        transpose=True), 16)
            ldv = self.ppl[b].val

            @cx.block.gpsimd
            def _(q, scr2=scr2, ic=ic, b=b, ldv=ldv, o0=o0, o1=o1):
                cx.wait(q, "gpsimd", self.ppl[b], ldv)
                cx.inc(self.pps[b], q.dma_start(out=scr2[ic * 128:(ic + 1) * 128, o0:o1],
                                                in_=self.rings[b][:, 0:o1 - o0]), 16)
                self.ring_free[b] = self.pps[b].val
            if self.idx == len(self.steps) or self.steps[self.idx][0] != name:
                self.done[name] = [(sem, sem.val) for sem in self.pps]

    def drain(self, name):
        while name not in self.done and self.idx < len(self.steps):
            self.step(ignore_limit=True)

    def drain_all(self):
        while self.idx < len(self.steps):
            self.step(ignore_limit=True)

    def wdone(self, name):
        assert name in self.done, f"prepass for {name} not drained"
        return self.done[name]


def emit_prepass(cx, es, r, plan, rings=None):
    if rings is None:
        rings = [sb(cx, es, "pp_wt0", [128, 1024], BF16)[:],
                 sb(cx, es, "pp_wt1", [128, 1024], BF16)[:],
                 sb(cx, es, "pp_wt2", [128, 1024], BF16)[:],
                 sb(cx, es, "pp_wt3", [128, 1024], BF16)[:]]
    pp = PrepassStream(cx, es, r, plan, rings)
    pp.drain_all()
    r["_pp_all"] = [(sem, sem.val) for sem in pp.pps]
    return {name: pp.wdone(name) for (name, _, _, _) in plan}


def emit_load_scale_fm(cx, es, r, name, s_ap, n, out=None):
    g = n // 128
    if "scale_nat" not in r:
        r["scale_nat"] = sb(cx, es, "scale_nat", [32, 128], F32)
    nat = r["scale_nat"][0:g, :]
    if out is None:
        out = sb(cx, es, name + "_fm", [128, g], F32)
    pt = r["ps_mm"][4]
    ld_pro = r["ld_pro"]

    @cx.block.sync
    def _(q):
        cx.wait(q, "sync", cx.pe, r.get("_pro_pe_prev", 0))
        cx.wait(q, "sync", cx.dve, r.get("_hl_dve", 0))
        cx.inc(ld_pro, q.dma_start(out=nat, in_=s_ap.rearrange("(a b) -> a b", b=128)), 16)
    ldv = ld_pro.val

    @cx.block.tensor
    def _(t):
        cx.wait(t, "tensor", ld_pro, ldv)
        cx.wait(t, "tensor", r["ld_id"], 16)
        fe, fv = r["bank_free"][4]
        cx.wait(t, "tensor", fe, fv)
        cx.inc(cx.pe, t.transpose(pt[0:128, 0:g], nat, r["id32"][0:g, 0:g]))
    pev = cx.pe.val
    r["_pro_pe_prev"] = pev

    @cx.block.vector
    def _(v):
        cx.wait(v, "vector", cx.pe, pev)
        cx.inc(cx.dve, v.tensor_copy(out[:, :], pt[0:128, 0:g]))
    r["bank_free"][4] = (cx.dve, cx.dve.val)
    return out, (cx.dve, cx.dve.val)


class LNRes:
    pass


def emit_ln(cx, r, t0, ncol, x_dep, xn_dst=None):
    x = r["x"]
    if xn_dst is None:
        xn_dst = lambda ic: r["xn_t"][:, ic * ncol:(ic + 1) * ncol]
    xbf, xsq = r["scrA"], r["scrB"]
    rows, tmp = r["ln_rows"], r["ln_tmp"]
    banks = r["ps_mm"]
    bank_free = r["bank_free"]
    s1 = banks[0][0:1, :]
    s2 = banks[3][0:1, :]
    bcr = banks[1]
    bcm = banks[2]
    pe_prev = r["_ln_pe_prev"]

    act_per_chunk = []
    for ic in range(NIC):
        xc = x[:, ic * T + t0: ic * T + t0 + ncol]

        @cx.block.scalar
        def _(sc, ic=ic, xc=xc):
            cx.wait(sc, "scalar", x_dep[0], x_dep[1])
            cx.wait(sc, "scalar", cx.dve, r.get("_hl_dve", 0))
            cx.wait_list(sc, "scalar", r.get("_ln_gate", []))
            cx.wait_list(sc, "scalar", r.get("_stage_gate", []))
            cx.wait(sc, "scalar", cx.pe, pe_prev)
            cx.inc(cx.act, sc.activation(xbf[:, ic * ncol:(ic + 1) * ncol], xc, AF.Copy))
            cx.inc(cx.act, sc.activation(xsq[:, ic * ncol:(ic + 1) * ncol], xc, AF.Square))
        act_per_chunk.append(cx.act.val)

    @cx.block.tensor
    def _(t):
        cx.wait(t, "tensor", cx.pool, r["ones_dep"])
        fe, fv = bank_free[0]
        cx.wait(t, "tensor", fe, fv)
        fe, fv = bank_free[3]
        cx.wait(t, "tensor", fe, fv)
        for ic in range(NIC):
            cx.wait(t, "tensor", cx.act, act_per_chunk[ic])
            cx.inc(cx.pe, t.matmul(s1[:, 0:ncol], r["ones_bf"][:, :],
                                   xbf[:, ic * ncol:(ic + 1) * ncol],
                                   start=(ic == 0), stop=(ic == NIC - 1)))
            cx.inc(cx.pe, t.matmul(s2[:, 0:ncol], r["ones_bf"][:, :],
                                   xsq[:, ic * ncol:(ic + 1) * ncol],
                                   start=(ic == 0), stop=(ic == NIC - 1)))
    pe_stats = cx.pe.val
    r["_ln_pe_prev"] = pe_stats

    r_m = s1[:, 0:ncol]
    r_e2 = s2[:, 0:ncol]
    r_rs = rows[:, 0:ncol]
    r_mr = rows[:, TT:TT + ncol]

    @cx.block.vector
    def _(v):
        cx.wait(v, "vector", cx.pe, pe_stats)
        cx.inc(cx.dve, v.tensor_scalar(r_m, s1[:, 0:ncol], 1.0 / D, None, ALU.mult))
        cx.inc(cx.dve, v.tensor_scalar(r_e2, s2[:, 0:ncol], 1.0 / D, None, ALU.mult))
    dve_ms = cx.dve.val

    @cx.block.scalar
    def _(sc):
        cx.wait(sc, "scalar", cx.dve, dve_ms)
        cx.wait(sc, "scalar", cx.pe, r.get("_ln_rows_pe", 0))
        cx.inc(cx.act, sc.activation(r_rs, r_m, AF.Square))
    act_m2 = cx.act.val

    @cx.block.vector
    def _(v):
        cx.wait(v, "vector", cx.act, act_m2)
        cx.inc(cx.dve, v.tensor_tensor(r_rs, r_e2, r_rs, ALU.subtract))
    dve_rows = cx.dve.val

    @cx.block.scalar
    def _(sc):
        cx.wait(sc, "scalar", cx.dve, dve_rows)
        cx.wait(sc, "scalar", cx.pool, r["ones_dep"])
        cx.inc(cx.act, sc.activation(r_rs, r_rs, AF.Ln, bias=r["eps_t"][:, :]))
        cx.wait(sc, "scalar", cx.act, cx.act.val)
        cx.inc(cx.act, sc.activation(r_rs, r_rs, AF.Exp, scale=-0.5))
    act_rs = cx.act.val

    @cx.block.vector
    def _(v):
        cx.wait(v, "vector", cx.act, act_rs)
        cx.wait(v, "vector", cx.pe, r.get("_ln_rows_pe", 0))
        cx.inc(cx.dve, v.tensor_tensor(r_mr, r_m, r_rs, ALU.mult))
    dve_mr = cx.dve.val
    bank_free[0] = (cx.dve, dve_mr)
    bank_free[3] = (cx.dve, dve_mr)

    @cx.block.tensor
    def _(t):
        cx.wait(t, "tensor", cx.act, act_rs)
        cx.wait(t, "tensor", cx.dve, dve_mr)
        for bidx, row in ((1, r_rs), (2, r_mr)):
            fe, fv = bank_free[bidx]
            cx.wait(t, "tensor", fe, fv)
        cx.inc(cx.pe, t.matmul(bcr[:, 0:ncol], r["ones_f1"][:, :], r_rs, start=True, stop=True))
        cx.inc(cx.pe, t.matmul(bcm[:, 0:ncol], r["ones_f1"][:, :], r_mr, start=True, stop=True))
    pe_bc = cx.pe.val
    r["_ln_rows_pe"] = pe_bc

    for ic in range(NIC):
        xc = x[:, ic * T + t0: ic * T + t0 + ncol]

        @cx.block.vector
        def _(v, ic=ic, xc=xc):
            cx.wait(v, "vector", cx.pe, pe_bc)
            tb = tmp[:, (ic % 2) * TT:(ic % 2) * TT + ncol]
            cx.inc(cx.dve, v.tensor_tensor(tb, xc, bcr[:, 0:ncol], ALU.mult))
            cx.wait(v, "vector", cx.dve, cx.dve.val)
            cx.inc(cx.dve, v.tensor_tensor(xn_dst(ic), tb,
                                           bcm[:, 0:ncol], ALU.subtract))
    res = LNRes()
    res.dve_done = cx.dve.val
    bank_free[1] = (cx.dve, res.dve_done)
    bank_free[2] = (cx.dve, res.dve_done)
    return res


def emit_mm_stage(cx, r, scr2, O, I, rhs_fn, rhs_dep, evac_fn, w_gate,
                  groups=None, qrot=(0, 1), ncol=TT):
    n_oc, n_ic = O // 128, I // 128
    if groups is None:
        groups = [(g * 8, min(8, n_oc - g * 8)) for g in range((n_oc + 7) // 8)]
    banks, bank_free = r["ps_mm"], r["bank_free"]
    wring, wfree = r["wslc"], r["wslc_free"]
    wsem = r["wslc_sem"]
    nslot = len(wring)
    items = [(oc0, now, ic) for (oc0, now) in groups for ic in range(n_ic)]
    LA = nslot - 1
    loads = [None] * len(items)

    def emit_load(j):
        oc0, now, ic = items[j]
        slot = r["wslc_next"][0] % nslot
        r["wslc_next"][0] += 1
        qi = qrot[j % len(qrot)]
        qname = "sync" if qi == 0 else "scalar"

        def body(q):
            cx.wait_list(q, qname, w_gate)
            cx.wait_list(q, qname, r.get("_stage_gate", []))
            cx.wait(q, qname, cx.pe, wfree[slot])
            cx.inc(wsem[slot], q.dma_start(out=wring[slot][:, 0:now * 128],
                                           in_=scr2[ic * 128:(ic + 1) * 128,
                                                    oc0 * 128:(oc0 + now) * 128]), 16)
            loads[j] = (slot, wsem[slot], wsem[slot].val)
        if qi == 0:
            @cx.block.sync
            def _(q):
                body(q)
        else:
            @cx.block.scalar
            def _(q):
                body(q)

    pp_step = r.get("_pp_step")
    for j in range(min(LA, len(items))):
        emit_load(j)
    gi = 0
    for (oc0, now) in groups:
        for ic in range(n_ic):
            j = gi
            gi += 1
            if pp_step is not None and (j % 2 == 0):
                pp_step()
            if j + LA < len(items):
                emit_load(j + LA)
            slot, ldeng, ldval = loads[j]

            @cx.block.tensor
            def _(t, oc0=oc0, now=now, ic=ic, slot=slot, ldeng=ldeng, ldval=ldval):
                cx.wait(t, "tensor", ldeng, ldval)
                cx.wait(t, "tensor", rhs_dep[0], rhs_dep[1])
                rhs = rhs_fn(ic)
                for oj in range(now):
                    if ic == 0:
                        fe, fv = bank_free[oj]
                        cx.wait(t, "tensor", fe, fv)
                    cx.inc(cx.pe, t.matmul(banks[oj][:, 0:ncol],
                                           wring[slot][:, oj * 128:(oj + 1) * 128],
                                           rhs, start=(ic == 0), stop=(ic == n_ic - 1)))
                wfree[slot] = cx.pe.val
        pev = cx.pe.val
        for oj in range(now):
            bank_free[oj] = evac_fn(oc0 + oj, banks[oj], pev)


def emit_ffn(cx, es, r, scr2, scales, wdone, x_dep):
    xn = r["xn_t"]
    h_sil, h_b, h_a, z1 = r["scrA"], r["scrB"], r["h_a"], r["z1"]
    ytmp = r["ytmp"]
    w1s, w2s = scales["w1_s"][0], scales["w2_s"][0]
    h1s, h2s = scales["h1_s"][0], scales["h2_s"][0]
    w3s = scales["w3_s"][0]

    for tt in range(NTT):
        t0 = tt * TT
        ln = emit_ln(cx, r, t0, TT, x_dep)
        xn_dep = (cx.dve, ln.dve_done)

        def rhs_xn(ic):
            return xn[:, ic * TT:(ic + 1) * TT]

        for g in range(HID // 128 // 8):
            def evac_w1(oc, bank, pev, g=g):
                @cx.block.scalar
                def _(sc):
                    cx.wait(sc, "scalar", cx.pe, pev)
                    oj = oc - g * 8
                    cx.inc(cx.act, sc.activation(h_sil[:, oj * TT:(oj + 1) * TT],
                                                 bank[:, 0:TT], AF.Silu,
                                                 scale=w1s[:, oc:oc + 1]))
                r["_w1_act"][oc] = cx.act.val
                return (cx.act, cx.act.val)

            emit_mm_stage(cx, r, scr2["w1"], HID, D, rhs_xn, xn_dep, evac_w1,
                          wdone["w1"], groups=[(g * 8, 8)], qrot=(0,))

            def evac_w2(oc, bank, pev, g=g):
                oj = oc - g * 8

                @cx.block.vector
                def _(v):
                    cx.wait(v, "vector", cx.pe, pev)
                    cx.inc(cx.dve, v.tensor_scalar(h_b[:, oj * TT:(oj + 1) * TT],
                                                   bank[:, 0:TT], w2s[:, oc:oc + 1],
                                                   None, ALU.mult))
                    cx.wait(v, "vector", cx.dve, cx.dve.val)
                    cx.wait(v, "vector", cx.act, r["_w1_act"][oc])
                    cx.inc(cx.dve, v.tensor_tensor(h_a[:, oc * TT:(oc + 1) * TT],
                                                   h_sil[:, oj * TT:(oj + 1) * TT],
                                                   h_b[:, oj * TT:(oj + 1) * TT],
                                                   ALU.mult))
                return (cx.dve, cx.dve.val)

            emit_mm_stage(cx, r, scr2["w2"], HID, D, rhs_xn, xn_dep, evac_w2,
                          wdone["w2"], groups=[(g * 8, 8)], qrot=(1,))
        h_a_dep = (cx.dve, cx.dve.val)

        def rhs_ha(ic):
            return h_a[:, ic * TT:(ic + 1) * TT]

        def evac_h1(oc, bank, pev):
            @cx.block.scalar
            def _(sc):
                cx.wait(sc, "scalar", cx.pe, pev)
                cx.inc(cx.act, sc.activation(z1[:, oc * TT:(oc + 1) * TT],
                                             bank[:, 0:TT], AF.Gelu,
                                             scale=h1s[:, oc:oc + 1]))
            return (cx.act, cx.act.val)

        emit_mm_stage(cx, r, scr2["h1"], HID, HID, rhs_ha, h_a_dep, evac_h1, wdone["h1"])
        z1_dep = (cx.act, cx.act.val)

        def rhs_z1(ic):
            return z1[:, ic * TT:(ic + 1) * TT]

        def evac_h2(oc, bank, pev):
            @cx.block.scalar
            def _(sc):
                cx.wait(sc, "scalar", cx.pe, pev)
                cx.inc(cx.act, sc.activation(h_a[:, oc * TT:(oc + 1) * TT],
                                             bank[:, 0:TT], AF.Gelu,
                                             scale=h2s[:, oc:oc + 1]))
            return (cx.act, cx.act.val)

        emit_mm_stage(cx, r, scr2["h2"], HID, HID, rhs_z1, z1_dep, evac_h2, wdone["h2"])
        z2_dep = (cx.act, cx.act.val)

        def evac_w3(oc, bank, pev, t0=t0):
            @cx.block.vector
            def _(v):
                cx.wait(v, "vector", cx.pe, pev)
                yb = ytmp[:, (oc % 2) * TT:(oc % 2) * TT + TT]
                cx.inc(cx.dve, v.tensor_scalar(yb, bank[:, 0:TT], w3s[:, oc:oc + 1],
                                               0.5, ALU.mult, ALU.mult))
                cx.wait(v, "vector", cx.dve, cx.dve.val)
                xs = r["x"][:, oc * T + t0: oc * T + t0 + TT]
                cx.inc(cx.dve, v.tensor_tensor(xs, xs, yb, ALU.add))
            return (cx.dve, cx.dve.val)

        emit_mm_stage(cx, r, scr2["w3"], D, HID, rhs_ha, z2_dep, evac_w3, wdone["w3"])
        x_dep = (cx.dve, cx.dve.val)
    return x_dep


def alloc_ffn(cx, es):
    r = {}
    r["h_a"] = sb(cx, es, "h_a", [128, 32 * TT], BF16)
    r["z1"] = sb(cx, es, "z1", [128, 32 * TT], BF16)
    r["xn_t"] = sb(cx, es, "xn_t", [128, NIC * TT], BF16)
    r["ytmp"] = sb(cx, es, "ytmp", [128, 2 * TT], F32)
    r["wslc"] = [sb(cx, es, f"wslc{i}", [128, 1024], BF16) for i in range(4)]
    r["wslc_sem"] = [cx.new_dma_sem(f"wsl{i}") for i in range(4)]
    r["wslc_free"] = [0, 0, 0, 0]
    r["wslc_next"] = [0]
    return r


def emit_load_x(cx, es, r, x_in, in_tm):
    x = r["x"]
    ld_x = r["ld_x"]
    if not in_tm:
        @cx.block.sync
        def _(q):
            cx.inc(ld_x, q.dma_start(
                out=x[:].rearrange("p (ic t) -> p ic t", ic=NIC),
                in_=x_in.rearrange("(ic p) t -> p ic t", p=128)), 16)
        return (ld_x, ld_x.val)

    stg = r["x_stg"]
    banks = r["ps_mm"]
    prev_dve = [0]
    for ti in range(T // 128):
        @cx.block.sync
        def _(q, ti=ti):
            cx.wait(q, "sync", cx.dve, prev_dve[0])
            cx.inc(ld_x, q.dma_start(out=stg[:, :],
                                     in_=x_in[ti * 128:(ti + 1) * 128, :]), 16)
        ldv = ld_x.val
        for half in range(2):
            bank = banks[4 + half]

            @cx.block.tensor
            def _(t, ti=ti, half=half, ldv=ldv, bank=bank):
                cx.wait(t, "tensor", ld_x, ldv)
                cx.wait(t, "tensor", r["ld_id"], 16)
                fe, fv = r["bank_free"][4 + half]
                cx.wait(t, "tensor", fe, fv)
                for q4 in range(4):
                    ic = half * 4 + q4
                    cx.inc(cx.pe, t.transpose(bank[:, q4 * 128:(q4 + 1) * 128],
                                              stg[:, ic * 128:(ic + 1) * 128],
                                              r["id32"][:, :]))
            pev = cx.pe.val

            @cx.block.vector
            def _(v, ti=ti, half=half, pev=pev, bank=bank):
                cx.wait(v, "vector", cx.pe, pev)
                for q4 in range(4):
                    ic = half * 4 + q4
                    cx.inc(cx.dve, v.tensor_copy(
                        x[:, ic * T + ti * 128: ic * T + (ti + 1) * 128],
                        bank[:, q4 * 128:(q4 + 1) * 128]))
            r["bank_free"][4 + half] = (cx.dve, cx.dve.val)
        prev_dve[0] = cx.dve.val
    return (cx.dve, cx.dve.val)


def emit_store_x(cx, es, r, x_out, out_tm, x_dep):
    x = r["x"]
    if not out_tm:
        @cx.block.sync
        def _(q):
            cx.wait(q, "sync", x_dep[0], x_dep[1])
            cx.inc(cx.st, q.dma_start(
                out=x_out.rearrange("(ic p) t -> p ic t", p=128),
                in_=x[:].rearrange("p (ic t) -> p ic t", ic=NIC)), 16)
            q.wait_ge(cx.st.sem, cx.st.val)
        return

    stg = r["x_stg"]
    banks = r["ps_mm"]
    st_prev = [cx.st.val]
    for ti in range(T // 128):
        dvev = 0
        for half in range(2):
            bank = banks[4 + half]

            @cx.block.tensor
            def _(t, ti=ti, half=half, bank=bank):
                cx.wait(t, "tensor", x_dep[0], x_dep[1])
                fe, fv = r["bank_free"][4 + half]
                cx.wait(t, "tensor", fe, fv)
                for q4 in range(4):
                    ic = half * 4 + q4
                    cx.inc(cx.pe, t.transpose(
                        bank[:, q4 * 128:(q4 + 1) * 128],
                        x[:, ic * T + ti * 128: ic * T + (ti + 1) * 128],
                        r["id32"][:, :]))
            pev = cx.pe.val

            @cx.block.vector
            def _(v, ti=ti, half=half, pev=pev, bank=bank):
                cx.wait(v, "vector", cx.pe, pev)
                cx.wait(v, "vector", cx.st, st_prev[0])
                for q4 in range(4):
                    ic = half * 4 + q4
                    cx.inc(cx.dve, v.tensor_copy(stg[:, ic * 128:(ic + 1) * 128],
                                                 bank[:, q4 * 128:(q4 + 1) * 128]))
            r["bank_free"][4 + half] = (cx.dve, cx.dve.val)
            dvev = cx.dve.val

        @cx.block.sync
        def _(q, ti=ti, dvev=dvev):
            cx.wait(q, "sync", cx.dve, dvev)
            cx.inc(cx.st, q.dma_start(out=x_out[ti * 128:(ti + 1) * 128, :],
                                      in_=stg[:, :]), 16)
        st_prev[0] = cx.st.val

    @cx.block.sync
    def _(q):
        q.wait_ge(cx.st.sem, cx.st.val)


def emit_hilo(cx, es, r, name, hi_ap, lo_ap, shape, rearr=None, out_rearr=None,
              out=None, dtype=F32):
    if out is None:
        out = sb(cx, es, name, shape, dtype)[:]
    scr = r["hl_scr"]
    F = int(np.prod(shape[1:]))

    @cx.block.sync
    def _(q):
        ha = rearr(hi_ap) if rearr else hi_ap[:, :]
        la = rearr(lo_ap) if rearr else lo_ap[:, :]
        cx.wait(q, "sync", cx.dve, r.get("_hl_dve", 0))
        cx.wait(q, "sync", cx.pe, r.get("_pro_pe_prev", 0))
        cx.wait_list(q, "sync", r.get("_hl_gate", []))
        sa = out_rearr(scr[:, 0:F]) if out_rearr else scr[:, 0:F]
        sb_ = out_rearr(scr[:, F:2 * F]) if out_rearr else scr[:, F:2 * F]
        with cx.nc.allow_non_contiguous_dma(reason="tiny one-time const load"):
            cx.inc(r["ld_pro"], q.dma_start(out=sa, in_=ha), 16)
            cx.inc(r["ld_pro"], q.dma_start(out=sb_, in_=la), 16)
    ldv = r["ld_pro"].val

    @cx.block.vector
    def _(v):
        cx.wait(v, "vector", r["ld_pro"], ldv)
        cx.wait_list(v, "vector", r.get("_hl_gate", []))
        cx.inc(cx.dve, v.tensor_tensor(out, scr[:, 0:F], scr[:, F:2 * F], ALU.add))
    r["_hl_dve"] = cx.dve.val
    return out, (cx.dve, cx.dve.val)


def alloc_retnet(cx, es):
    a = {}
    a["ld_hw"] = cx.new_dma_sem("ld_hw")
    a["ld_hw2"] = cx.new_dma_sem("ld_hw2")
    a["xn_full"] = sb(cx, es, "xn_full", [128, NIC * T], BF16)
    a["H"] = sb(cx, es, "rn_H", [128, RN_H * 2 * RN_DV], F32)
    a["H_bf"] = sb(cx, es, "rn_Hbf", [128, 2 * RN_DV], BF16)
    a["wq_t"] = sb(cx, es, "rn_wq", [128, NIC * 256], BF16)
    a["wk_t"] = sb(cx, es, "rn_wk", [128, NIC * 256], BF16)
    a["wv_t"] = sb(cx, es, "rn_wv", [128, NIC * 512], BF16)
    a["wg_t"] = sb(cx, es, "rn_wg", [128, NIC * 512], BF16)
    a["wo_t"] = sb(cx, es, "rn_wo", [128, 4 * 1024], BF16)
    a["q_bf"] = sb(cx, es, "rn_qbf", [128, 256], BF16)
    a["qt_bf"] = sb(cx, es, "rn_qtbf", [128, 256], BF16)
    a["k_bf"] = sb(cx, es, "rn_kbf", [128, 256], BF16)
    a["k_f32"] = sb(cx, es, "rn_kf32", [128, 256], F32)
    a["k_tm"] = sb(cx, es, "rn_ktm", [128, 256], BF16)
    a["v_bf"] = sb(cx, es, "rn_vbf", [128, 512], BF16)
    a["g_si"] = sb(cx, es, "rn_gsi", [128, 512], BF16)
    a["s_bf"] = sb(cx, es, "rn_sbf", [128, 128], BF16)
    a["o_bf"] = sb(cx, es, "rn_obf", [128, 512], BF16)
    a["osq"] = sb(cx, es, "rn_osq", [128, 512], BF16)
    a["o_g"] = sb(cx, es, "rn_og", [128, 512], BF16)
    a["t128"] = sb(cx, es, "rn_t128", [128, 4 * 128], F32)
    a["rrow"] = sb(cx, es, "rn_rrow", [1, 256], F32)
    return a


def emit_retnet(cx, es, r, a, scr2, gnw_fm, consts, wdone, x_dep, bias_fm=None):
    nc = cx.nc
    x = r["x"]
    xn = a["xn_full"]
    banks = r["ps_mm"]

    for tt in range(NTT):
        t0 = tt * TT
        ln = emit_ln(cx, r, t0, TT, x_dep,
                     xn_dst=lambda ic, t0=t0: xn[:, ic * T + t0: ic * T + t0 + TT])
    xn_dep = (cx.dve, cx.dve.val)

    @cx.block.gpsimd
    def _(gp):
        cx.wait_list(gp, "gpsimd", r.get("_pp_all", []))
        cx.wait(gp, "gpsimd", cx.dve, r.get("_hl_dve", 0))
        cx.inc(cx.pool, gp.memset(a["H"][:, :], 0.0))
    H_dep = cx.pool.val

    cos_f, sin_f = consts["cos"], consts["sin"]
    maskt_f, qsc_f = consts["maskt"], consts["qsc"]
    ksc_f, gc_f = consts["ksc"], consts["gc"]
    cdep = consts["dep"]

    bfree = list(r["bank_free"])

    for h in range(RN_H):
        @cx.block.sync
        def _(q, h=h):
            cx.wait_list(q, "sync", wdone["Wq"])
            cx.wait_list(q, "sync", wdone["Wk"])
            cx.wait_list(q, "sync", wdone["Wv"])
            cx.wait_list(q, "sync", r.get("_stage_gate", []))
            cx.wait(q, "sync", cx.pe, a.get("_wt_pe_free", 0))
            cx.inc(a["ld_hw"], q.dma_start(
                out=a["wq_t"][:].rearrange("p (ic o) -> p ic o", ic=NIC),
                in_=scr2["Wq"][:, h * 256:(h + 1) * 256].rearrange("(ic p) o -> p ic o", p=128)), 16)
            cx.inc(a["ld_hw"], q.dma_start(
                out=a["wk_t"][:].rearrange("p (ic o) -> p ic o", ic=NIC),
                in_=scr2["Wk"][:, h * 256:(h + 1) * 256].rearrange("(ic p) o -> p ic o", p=128)), 16)
            cx.inc(a["ld_hw"], q.dma_start(
                out=a["wv_t"][:].rearrange("p (ic o) -> p ic o", ic=NIC),
                in_=scr2["Wv"][:, h * 512:(h + 1) * 512].rearrange("(ic p) o -> p ic o", p=128)), 16)
        ld_qkv = a["ld_hw"].val

        @cx.block.scalar
        def _(q, h=h):
            cx.wait_list(q, "scalar", wdone["Wg"])
            cx.wait_list(q, "scalar", wdone["Wo"])
            cx.wait_list(q, "scalar", r.get("_stage_gate", []))
            cx.wait(q, "scalar", cx.pe, a.get("_wt_pe_free", 0))
            cx.inc(a["ld_hw2"], q.dma_start(
                out=a["wg_t"][:].rearrange("p (ic o) -> p ic o", ic=NIC),
                in_=scr2["Wg"][:, h * 512:(h + 1) * 512].rearrange("(ic p) o -> p ic o", p=128)), 16)
            cx.inc(a["ld_hw2"], q.dma_start(
                out=a["wo_t"][:].rearrange("p (kc o) -> p kc o", kc=4),
                in_=scr2["Wo"][h * 512:(h + 1) * 512, :].rearrange("(kc p) o -> p kc o", p=128)), 16)
        ld_go = a["ld_hw2"].val

        for c in range(T // CHUNK):
            cols = slice(c * CHUNK, (c + 1) * CHUNK)
            emit_retnet_chunk(cx, r, a, h, c, ld_qkv, ld_go, xn_dep, H_dep,
                              cdep, cos_f, sin_f, maskt_f, qsc_f, ksc_f, gc_f,
                              gnw_fm, bfree, bias_fm)
        a["_wt_pe_free"] = cx.pe.val

    r["bank_free"] = list(bfree)
    return (cx.dve, cx.dve.val)


def emit_retnet_chunk(cx, r, a, h, c, ld_qkv, ld_go, xn_dep, H_dep, cdep,
                      cos_f, sin_f, maskt_f, qsc_f, ksc_f, gc_f, gnw_fm,
                      bfree, bias_fm):
    x = r["x"]
    xn = a["xn_full"]
    banks = r["ps_mm"]
    b_q, b_k, b_v, b_g, b_s, b_t, b_o, b_r = banks

    def xn_sl(ic):
        return xn[:, ic * T + c * CHUNK: ic * T + (c + 1) * CHUNK]

    @cx.block.tensor
    def _(t):
        cx.wait(t, "tensor", a["ld_hw"], ld_qkv)
        cx.wait(t, "tensor", a["ld_hw2"], ld_go)
        cx.wait(t, "tensor", xn_dep[0], xn_dep[1])
        for bi, dep in ((0, bfree[0]), (1, bfree[1]), (2, bfree[2]), (3, bfree[3])):
            cx.wait(t, "tensor", dep[0], dep[1])
        for sl in range(2):
            for ic in range(NIC):
                cx.inc(cx.pe, t.matmul(b_q[:, sl * 128:(sl + 1) * 128],
                                       a["wq_t"][:, ic * 256 + sl * 128: ic * 256 + (sl + 1) * 128],
                                       xn_sl(ic), start=(ic == 0), stop=(ic == NIC - 1)))
        for sl in range(2):
            for ic in range(NIC):
                cx.inc(cx.pe, t.matmul(b_k[:, sl * 128:(sl + 1) * 128],
                                       a["wk_t"][:, ic * 256 + sl * 128: ic * 256 + (sl + 1) * 128],
                                       xn_sl(ic), start=(ic == 0), stop=(ic == NIC - 1)))
        for ic in range(NIC):
            cx.inc(cx.pe, t.matmul(b_v[:, 0:512], xn_sl(ic),
                                   a["wv_t"][:, ic * 512:(ic + 1) * 512],
                                   start=(ic == 0), stop=(ic == NIC - 1)))
        for sl in range(4):
            for ic in range(NIC):
                cx.inc(cx.pe, t.matmul(b_g[:, sl * 128:(sl + 1) * 128],
                                       a["wg_t"][:, ic * 512 + sl * 128: ic * 512 + (sl + 1) * 128],
                                       xn_sl(ic), start=(ic == 0), stop=(ic == NIC - 1)))
    pe_proj = cx.pe.val

    t128 = a["t128"]
    ccol = cos_f[:, c * CHUNK:(c + 1) * CHUNK]
    scol = sin_f[:, c * CHUNK:(c + 1) * CHUNK]

    @cx.block.vector
    def _(v):
        cx.wait(v, "vector", cx.pe, pe_proj)
        cx.wait(v, "vector", cdep[0], cdep[1])
        for (bank, dst) in ((b_q, a["q_bf"]), (b_k, a["k_f32"])):
            cx.wait(v, "vector", cx.dve, cx.dve.val)
            ta = t128[:, 0:128]
            tb = t128[:, 128:256]
            tc2 = t128[:, 256:384]
            td = t128[:, 384:512]
            cx.inc(cx.dve, v.tensor_tensor(ta, bank[:, 0:128], ccol, ALU.mult))
            cx.inc(cx.dve, v.tensor_tensor(tb, bank[:, 128:256], scol, ALU.mult))
            cx.inc(cx.dve, v.tensor_tensor(tc2, bank[:, 0:128], scol, ALU.mult))
            cx.inc(cx.dve, v.tensor_tensor(td, bank[:, 128:256], ccol, ALU.mult))
            cx.wait(v, "vector", cx.dve, cx.dve.val)
            cx.inc(cx.dve, v.tensor_tensor(dst[:, 0:128], ta, tb, ALU.subtract))
            cx.inc(cx.dve, v.tensor_tensor(dst[:, 128:256], tc2, td, ALU.add))
        cx.wait(v, "vector", cx.dve, cx.dve.val)
        cx.inc(cx.dve, v.tensor_copy(a["k_bf"][:, :], a["k_f32"][:, :]))
        cx.wait(v, "vector", cx.dve, cx.dve.val)
        for sl in range(2):
            cx.inc(cx.dve, v.tensor_tensor(a["qt_bf"][:, sl * 128:(sl + 1) * 128],
                                           a["q_bf"][:, sl * 128:(sl + 1) * 128],
                                           qsc_f[:, h * 128:(h + 1) * 128], ALU.mult))
    dve_rope = cx.dve.val
    bfree[0] = (cx.dve, dve_rope)

    @cx.block.tensor
    def _(t):
        cx.wait(t, "tensor", cx.dve, dve_rope)
        fe, fv = bfree[4]
        cx.wait(t, "tensor", fe, fv)
        for sl in range(2):
            cx.inc(cx.pe, t.matmul(b_s[:, 0:128],
                                   a["k_bf"][:, sl * 128:(sl + 1) * 128],
                                   a["q_bf"][:, sl * 128:(sl + 1) * 128],
                                   start=(sl == 0), stop=(sl == 1)))
        fe, fv = bfree[5]
        cx.wait(t, "tensor", fe, fv)
        for sl in range(2):
            cx.inc(cx.pe, t.transpose(b_t[:, sl * 128:(sl + 1) * 128],
                                      a["k_f32"][:, sl * 128:(sl + 1) * 128],
                                      r["id32"][:, :]))
    pe_sc = cx.pe.val

    @cx.block.vector
    def _(v):
        cx.wait(v, "vector", cx.pe, pe_sc)
        cx.inc(cx.dve, v.tensor_tensor(a["s_bf"][:, :], b_s[:, 0:128],
                                       maskt_f[:, h * 128:(h + 1) * 128], ALU.mult))
        cx.inc(cx.dve, v.tensor_scalar(a["k_tm"][:, :], b_t[:, 0:256],
                                       ksc_f[:, h:h + 1], None, ALU.mult))
    dve_sc = cx.dve.val
    bfree[1] = (cx.dve, dve_sc)
    bfree[4] = (cx.dve, dve_sc)

    @cx.block.scalar
    def _(sc):
        cx.wait(sc, "scalar", cx.pe, pe_proj)
        cx.inc(cx.act, sc.activation(a["v_bf"][:, :], b_v[:, 0:512], AF.Copy))
        cx.inc(cx.act, sc.activation(a["g_si"][:, :], b_g[:, 0:512], AF.Silu))
    act_vg = cx.act.val
    bfree[2] = (cx.act, act_vg)
    bfree[3] = (cx.act, act_vg)

    @cx.block.tensor
    def _(t):
        cx.wait(t, "tensor", cx.dve, dve_sc)
        cx.wait(t, "tensor", cx.act, act_vg)
        fe, fv = bfree[6]
        cx.wait(t, "tensor", fe, fv)
        if c > 0:
            cx.wait(t, "tensor", cx.dve, a["_Hbf_dve"])
        for dvs in range(4):
            cx.inc(cx.pe, t.matmul(b_o[:, dvs * 128:(dvs + 1) * 128],
                                   a["v_bf"][:, dvs * 128:(dvs + 1) * 128],
                                   a["s_bf"][:, :], start=True, stop=(c == 0)))
            if c > 0:
                for kc in range(2):
                    cx.inc(cx.pe, t.matmul(
                        b_o[:, dvs * 128:(dvs + 1) * 128],
                        a["H_bf"][:, kc * 512 + dvs * 128: kc * 512 + (dvs + 1) * 128],
                        a["qt_bf"][:, kc * 128:(kc + 1) * 128],
                        start=False, stop=(kc == 1)))
    pe_o = cx.pe.val

    st_banks = [b_t, b_s]

    @cx.block.tensor
    def _(t):
        cx.wait(t, "tensor", cx.dve, dve_sc)
        cx.wait(t, "tensor", cx.act, act_vg)
        if c > 0:
            cx.wait(t, "tensor", cx.dve, a["_Hbf_dve"])
        for kc in range(2):
            cx.inc(cx.pe, t.matmul(st_banks[kc][:, 0:512],
                                   a["k_tm"][:, kc * 128:(kc + 1) * 128],
                                   a["v_bf"][:, :], start=True, stop=True))
    pe_st = cx.pe.val

    @cx.block.vector
    def _(v):
        cx.wait(v, "vector", cx.pe, pe_st)
        cx.wait(v, "vector", cx.pool, H_dep)
        for kc in range(2):
            H_sl = a["H"][:, (h * 2 + kc) * RN_DV:(h * 2 + kc + 1) * RN_DV]
            cx.inc(cx.dve, v.tensor_scalar(H_sl, H_sl, gc_f[:, h:h + 1], None, ALU.mult))
            cx.wait(v, "vector", cx.dve, cx.dve.val)
            cx.inc(cx.dve, v.tensor_tensor(H_sl, H_sl, st_banks[kc][:, 0:512], ALU.add))
            cx.wait(v, "vector", cx.dve, cx.dve.val)
    a["_H_dve"] = cx.dve.val

    @cx.block.vector
    def _(v):
        cx.wait(v, "vector", cx.pe, pe_o)
        for kc in range(2):
            H_sl = a["H"][:, (h * 2 + kc) * RN_DV:(h * 2 + kc + 1) * RN_DV]
            cx.inc(cx.dve, v.tensor_copy(a["H_bf"][:, kc * RN_DV:(kc + 1) * RN_DV], H_sl))
    a["_Hbf_dve"] = cx.dve.val
    bfree[4] = (cx.dve, a["_H_dve"])
    bfree[5] = (cx.dve, a["_H_dve"])

    @cx.block.scalar
    def _(sc):
        cx.wait(sc, "scalar", cx.pe, pe_o)
        cx.inc(cx.act, sc.activation(a["o_bf"][:, :], b_o[:, 0:512], AF.Copy))
        cx.wait(sc, "scalar", cx.act, cx.act.val)
        cx.inc(cx.act, sc.activation(a["osq"][:, :], a["o_bf"][:, :], AF.Square))
    act_o = cx.act.val
    bfree[6] = (cx.act, act_o)

    @cx.block.tensor
    def _(t):
        cx.wait(t, "tensor", cx.act, act_o)
        fe, fv = bfree[7]
        cx.wait(t, "tensor", fe, fv)
        for dvs in range(4):
            cx.inc(cx.pe, t.matmul(b_r[0:1, 0:128], r["ones_bf"][:, :],
                                   a["osq"][:, dvs * 128:(dvs + 1) * 128],
                                   start=(dvs == 0), stop=(dvs == 3)))
    pe_rms = cx.pe.val

    r_ms = a["rrow"][:, 0:128]
    r_rstd = a["rrow"][:, 128:256]

    @cx.block.vector
    def _(v):
        cx.wait(v, "vector", cx.pe, pe_rms)
        cx.inc(cx.dve, v.tensor_scalar(r_ms, b_r[0:1, 0:128], 1.0 / RN_DV, None, ALU.mult))
    dve_ms = cx.dve.val

    @cx.block.scalar
    def _(sc):
        cx.wait(sc, "scalar", cx.dve, dve_ms)
        cx.inc(cx.act, sc.activation(r_rstd, r_ms, AF.Ln, bias=r["eps_t"][:, :]))
        cx.wait(sc, "scalar", cx.act, cx.act.val)
        cx.inc(cx.act, sc.activation(r_rstd, r_rstd, AF.Exp, scale=-0.5))
    act_rstd = cx.act.val

    @cx.block.tensor
    def _(t):
        cx.wait(t, "tensor", cx.act, act_rstd)
        cx.inc(cx.pe, t.matmul(b_r[:, 128:256], r["ones_f1"][:, :], r_rstd,
                               start=True, stop=True))
    pe_bc = cx.pe.val

    @cx.block.vector
    def _(v):
        cx.wait(v, "vector", cx.pe, pe_bc)
        cx.wait(v, "vector", cx.act, act_o)
        for dvs in range(4):
            tmp = a["t128"][:, dvs * 128:(dvs + 1) * 128]
            cx.inc(cx.dve, v.tensor_tensor(tmp, a["o_bf"][:, dvs * 128:(dvs + 1) * 128],
                                           b_r[:, 128:256], ALU.mult))
            cx.wait(v, "vector", cx.dve, cx.dve.val)
            cx.inc(cx.dve, v.tensor_scalar(tmp, tmp, gnw_fm[:, dvs:dvs + 1], None, ALU.mult))
            cx.wait(v, "vector", cx.dve, cx.dve.val)
            cx.inc(cx.dve, v.tensor_tensor(a["o_g"][:, dvs * 128:(dvs + 1) * 128],
                                           tmp, a["g_si"][:, dvs * 128:(dvs + 1) * 128],
                                           ALU.mult))
    dve_gate = cx.dve.val
    bfree[7] = (cx.dve, dve_gate)

    y_banks = [b_q, b_k]
    yfree = [bfree[0], bfree[1]]
    for oc in range(NIC):
        bk = y_banks[oc % 2]

        @cx.block.tensor
        def _(t, oc=oc, bk=bk):
            cx.wait(t, "tensor", cx.dve, dve_gate)
            fe, fv = yfree[oc % 2]
            cx.wait(t, "tensor", fe, fv)
            for kc2 in range(4):
                cx.inc(cx.pe, t.matmul(bk[:, 0:128],
                                       a["wo_t"][:, kc2 * 1024 + oc * 128: kc2 * 1024 + (oc + 1) * 128],
                                       a["o_g"][:, kc2 * 128:(kc2 + 1) * 128],
                                       start=(kc2 == 0), stop=(kc2 == 3)))
        pev = cx.pe.val

        @cx.block.vector
        def _(v, oc=oc, bk=bk, pev=pev):
            cx.wait(v, "vector", cx.pe, pev)
            xs = x[:, oc * T + c * CHUNK: oc * T + (c + 1) * CHUNK]
            cx.inc(cx.dve, v.tensor_tensor(xs, xs, bk[:, 0:128], ALU.add))
        yfree[oc % 2] = (cx.dve, cx.dve.val)
    bfree[0] = yfree[0]
    bfree[1] = yfree[1]
    if r.get("_pp_step") is not None:
        for _ in range(6):
            r["_pp_step"]()



FFN_W_SHAPES = [("w1", (HID, D)), ("w2", (HID, D)), ("h1", (HID, HID)),
                ("h2", (HID, HID)), ("w3", (D, HID))]


def build_ffn_prog(in_tm, out_tm):
    nc = bass.Bass("TRN2", target_bir_lowering=False, debug=False)
    x_in = nc.dram_tensor("x_in", [T, D] if in_tm else [D, T], F32,
                          kind="ExternalInput").ap()
    x_out = nc.dram_tensor("x_out", [T, D] if out_tm else [D, T], F32,
                           kind="ExternalOutput").ap()
    w, sv, scr1, scr2 = {}, {}, {}, {}
    for nme, shape in FFN_W_SHAPES:
        w[nme] = nc.dram_tensor(nme, list(shape), F32, kind="ExternalInput").ap()
        scr1[nme] = nc.dram_tensor("scr1_" + nme, list(shape), BF16, kind="Internal").ap()
        scr2[nme] = nc.dram_tensor("scr2_" + nme, [shape[1], shape[0]], BF16,
                                   kind="Internal").ap()
        sv[nme + "_s"] = nc.dram_tensor(nme + "_s", [shape[0]], F32,
                                        kind="ExternalInput").ap()
    id32 = nc.dram_tensor("ident_f32", [128, 128], F32, kind="ExternalInput").ap()

    with ExitStack() as es:
        cx = Ctx(nc, es)
        r = alloc_common(cx, es)
        r.update(alloc_ffn(cx, es))

        @cx.block.sync
        def _(q):
            cx.inc(r["ld_id"], q.dma_start(out=r["id32"][:, :], in_=id32[:, :]), 16)

        x_dep = emit_load_x(cx, es, r, x_in, in_tm)

        wdone = emit_prepass(cx, es, r, [(n, w[n], scr1[n], scr2[n])
                                         for n, _ in FFN_W_SHAPES])
        scales = {}
        for nme, shape in FFN_W_SHAPES:
            scales[nme + "_s"] = emit_load_scale_fm(cx, es, r, nme + "_s",
                                                    sv[nme + "_s"], shape[0])

        x_dep = emit_ffn(cx, es, r, scr2, scales, wdone, x_dep)
        emit_store_x(cx, es, r, x_out, out_tm, x_dep)
    return nc


RN_W_SHAPES = [("Wq", (D, D)), ("Wk", (D, D)), ("Wv", (2 * D, D)),
               ("Wg", (2 * D, D)), ("Wo", (D, 2 * D))]

RN_CONST_NAMES = ["rope_cos", "rope_sin", "rn_maskt", "rn_qsc", "rn_ksc", "rn_gc"]


def build_retnet_prog(in_tm=False, out_tm=False):
    nc = bass.Bass("TRN2", target_bir_lowering=False, debug=False)
    x_in = nc.dram_tensor("x_in", [T, D] if in_tm else [D, T], F32,
                          kind="ExternalInput").ap()
    x_out = nc.dram_tensor("x_out", [T, D] if out_tm else [D, T], F32,
                           kind="ExternalOutput").ap()
    w, scr1, scr2 = {}, {}, {}
    for nme, shape in RN_W_SHAPES:
        w[nme] = nc.dram_tensor(nme, list(shape), F32, kind="ExternalInput").ap()
        scr1[nme] = nc.dram_tensor("scr1_" + nme, list(shape), BF16, kind="Internal").ap()
        scr2[nme] = nc.dram_tensor("scr2_" + nme, [shape[1], shape[0]], BF16,
                                   kind="Internal").ap()
    gnw = nc.dram_tensor("g_norm_w", [RN_DV], F32, kind="ExternalInput").ap()
    id32 = nc.dram_tensor("ident_f32", [128, 128], F32, kind="ExternalInput").ap()
    cst = {}
    for nme in RN_CONST_NAMES:
        for p in ("_hi", "_lo"):
            shp = {"rope_cos": [128, T], "rope_sin": [128, T],
                   "rn_maskt": [RN_H * 128, 128], "rn_qsc": [RN_H * 128, 128],
                   "rn_ksc": [RN_H * 128, 1], "rn_gc": [RN_H * 128, 1]}[nme]
            cst[nme + p] = nc.dram_tensor(nme + p, shp, F32, kind="ExternalInput").ap()

    with ExitStack() as es:
        cx = Ctx(nc, es)
        r = alloc_common(cx, es)
        a = alloc_retnet(cx, es)
        r["hl_scr"] = a["H"][:, 0:4096]

        @cx.block.sync
        def _(q):
            cx.inc(r["ld_id"], q.dma_start(out=r["id32"][:, :], in_=id32[:, :]), 16)

        @cx.block.vector
        def _(v):
            cx.wait(v, "vector", r["ld_id"], 16)
            cx.inc(cx.dve, v.tensor_copy(r["id_bf"][:, :], r["id32"][:, :]))

        x_dep = emit_load_x(cx, es, r, x_in, in_tm)
        pp_rings = [a["xn_full"][:, 0:4096], a["xn_full"][:, 4096:8192]]
        wdone = emit_prepass(cx, es, r, [(n, w[n], scr1[n], scr2[n])
                                         for n, _ in RN_W_SHAPES], rings=pp_rings)
        r["_ln_gate"] = r["_pp_all"]
        gnw_fm, gnw_dep = emit_load_scale_fm(cx, es, r, "gnw", gnw, RN_DV)

        hp = lambda ap: ap.rearrange("(h p) c -> p h c", p=128)
        hv = lambda ap: ap.rearrange("p (h c) -> p h c", h=RN_H)
        consts = {}
        consts["cos"], _ = emit_hilo(cx, es, r, "cos_f", cst["rope_cos_hi"],
                                     cst["rope_cos_lo"], [128, T])
        consts["sin"], _ = emit_hilo(cx, es, r, "sin_f", cst["rope_sin_hi"],
                                     cst["rope_sin_lo"], [128, T])
        consts["maskt"], _ = emit_hilo(cx, es, r, "maskt_f", cst["rn_maskt_hi"],
                                       cst["rn_maskt_lo"], [128, RN_H * 128],
                                       rearr=hp, out_rearr=hv)
        consts["qsc"], _ = emit_hilo(cx, es, r, "qsc_f", cst["rn_qsc_hi"],
                                     cst["rn_qsc_lo"], [128, RN_H * 128],
                                     rearr=hp, out_rearr=hv)
        consts["ksc"], _ = emit_hilo(cx, es, r, "ksc_f", cst["rn_ksc_hi"],
                                     cst["rn_ksc_lo"], [128, RN_H],
                                     rearr=hp, out_rearr=hv)
        consts["gc"], dep = emit_hilo(cx, es, r, "gc_f", cst["rn_gc_hi"],
                                      cst["rn_gc_lo"], [128, RN_H],
                                      rearr=hp, out_rearr=hv)
        consts["dep"] = (cx.dve, max(dep[1], gnw_dep[1]))

        x_dep = emit_retnet(cx, es, r, a, scr2, gnw_fm, consts, wdone, x_dep)
        emit_store_x(cx, es, r, x_out, out_tm, x_dep)
    return nc


WA_W_SHAPES = [("Wq", (D, D)), ("Wk", (D, D)), ("Wv", (D, D)), ("Wo", (D, D))]
WA_CONST_NAMES = ["w_mask_int", "w_mask_first", "w_mask_last"]
NST = T // 128


def alloc_window(cx, es):
    a = {}
    a["ld_hw"] = cx.new_dma_sem("wa_ld_hw")
    a["xn_full"] = sb(cx, es, "xn_full", [128, NIC * T], BF16)
    a["k_fm"] = sb(cx, es, "wa_kfm", [128, T], BF16)
    a["v_tm"] = sb(cx, es, "wa_vtm", [128, T], BF16)
    a["o_fm"] = sb(cx, es, "wa_ofm", [128, T], BF16)
    a["q_bf"] = sb(cx, es, "wa_qbf", [128, 128], BF16)
    a["s_m"] = sb(cx, es, "wa_sm", [128, 384], F32)
    a["p_f"] = sb(cx, es, "wa_pf", [128, 384], F32)
    a["p_n"] = sb(cx, es, "wa_pn", [128, 384], F32)
    a["pt_bf"] = sb(cx, es, "wa_ptbf", [128, 384], BF16)
    a["rows"] = sb(cx, es, "wa_rows", [128, 3], F32)
    a["wq_t"] = sb(cx, es, "wa_wq", [128, NIC * 128], BF16)
    a["wk_t"] = sb(cx, es, "wa_wk", [128, NIC * 128], BF16)
    a["wv_t"] = sb(cx, es, "wa_wv", [128, NIC * 128], BF16)
    a["wo_t"] = sb(cx, es, "wa_wo", [128, D], BF16)
    return a


def emit_window(cx, es, r, a, scr2, masks, bo_fm, wdone, x_dep):
    x = r["x"]
    xn = a["xn_full"]
    banks = r["ps_mm"]

    for tt in range(NTT):
        t0 = tt * TT
        ln = emit_ln(cx, r, t0, TT, x_dep,
                     xn_dst=lambda ic, t0=t0: xn[:, ic * T + t0: ic * T + t0 + TT])
    xn_dep = (cx.dve, cx.dve.val)
    bfree = list(r["bank_free"])
    b_p0, b_p1, b_s, b_t, b_pv, b_y0, b_y1, b_r = banks

    @cx.block.vector
    def _(v):
        cx.wait(v, "vector", xn_dep[0], xn_dep[1])
        cx.wait_list(v, "vector", bo_fm[1:])
        for oc in range(NIC):
            cx.inc(cx.dve, v.tensor_scalar(x[:, oc * T:(oc + 1) * T],
                                           x[:, oc * T:(oc + 1) * T],
                                           bo_fm[0][:, oc:oc + 1], None, ALU.add))
    bias_done = cx.dve.val

    for hp in range(8):
        @cx.block.sync
        def _(q, hp=hp):
            for nme in ("Wq", "Wk", "Wv", "Wo"):
                cx.wait_list(q, "sync", wdone[nme])
            cx.wait_list(q, "sync", r.get("_stage_gate", []))
            cx.wait(q, "sync", cx.pe, a.get("_wt_pe_free", 0))
            for nme, dst in (("Wq", a["wq_t"]), ("Wk", a["wk_t"]), ("Wv", a["wv_t"])):
                cx.inc(a["ld_hw"], q.dma_start(
                    out=dst[:].rearrange("p (ic o) -> p ic o", ic=NIC),
                    in_=scr2[nme][:, hp * 128:(hp + 1) * 128].rearrange(
                        "(ic p) o -> p ic o", p=128)), 16)
            cx.inc(a["ld_hw"], q.dma_start(
                out=a["wo_t"][:, :], in_=scr2["Wo"][hp * 128:(hp + 1) * 128, :]), 16)
        ld_w = a["ld_hw"].val

        kv_banks = [b_p0, b_p1]
        kv_free = [bfree[0], bfree[1]]
        for tc in range(4):
            bk = kv_banks[tc % 2]

            @cx.block.tensor
            def _(t, hp=hp, tc=tc, bk=bk, ld_w=ld_w):
                cx.wait(t, "tensor", a["ld_hw"], ld_w)
                cx.wait(t, "tensor", xn_dep[0], xn_dep[1])
                fe, fv = kv_free[tc % 2]
                cx.wait(t, "tensor", fe, fv)
                for ic in range(NIC):
                    cx.inc(cx.pe, t.matmul(
                        bk[:, 0:512], a["wk_t"][:, ic * 128:(ic + 1) * 128],
                        xn[:, ic * T + tc * 512: ic * T + (tc + 1) * 512],
                        start=(ic == 0), stop=(ic == NIC - 1)))
            pev = cx.pe.val

            @cx.block.scalar
            def _(sc, tc=tc, bk=bk, pev=pev):
                cx.wait(sc, "scalar", cx.pe, pev)
                cx.inc(cx.act, sc.activation(a["k_fm"][:, tc * 512:(tc + 1) * 512],
                                             bk[:, 0:512], AF.Copy))
            kv_free[tc % 2] = (cx.act, cx.act.val)

        for tc4 in range(4):
            bk = kv_banks[tc4 % 2]

            @cx.block.tensor
            def _(t, hp=hp, tc4=tc4, bk=bk, ld_w=ld_w):
                fe, fv = kv_free[tc4 % 2]
                cx.wait(t, "tensor", fe, fv)
                for j in range(4):
                    tc = tc4 * 4 + j
                    for ic in range(NIC):
                        cx.inc(cx.pe, t.matmul(
                            bk[:, j * 128:(j + 1) * 128],
                            xn[:, ic * T + tc * 128: ic * T + (tc + 1) * 128],
                            a["wv_t"][:, ic * 128:(ic + 1) * 128],
                            start=(ic == 0), stop=(ic == NIC - 1)))
            pev = cx.pe.val

            @cx.block.scalar
            def _(sc, tc4=tc4, bk=bk, pev=pev):
                cx.wait(sc, "scalar", cx.pe, pev)
                cx.inc(cx.act, sc.activation(a["v_tm"][:, tc4 * 512:(tc4 + 1) * 512],
                                             bk[:, 0:512], AF.Copy))
            kv_free[tc4 % 2] = (cx.act, cx.act.val)
        bfree[0], bfree[1] = kv_free

        for st in range(NST):
            if st == 0:
                c0, W, mk = 0, 256, masks["first"]
            elif st == NST - 1:
                c0, W, mk = (st - 1) * 128, 256, masks["last"]
            else:
                c0, W, mk = (st - 1) * 128, 384, masks["int"]
            nblk = W // 128
            emit_window_subtile(cx, r, a, hp, st, c0, W, nblk, mk, bo_fm,
                                xn_dep, bfree)

        o_dep = a["_o_dep"]
        y_banks = [b_y0, b_y1]
        y_free = [bfree[5], bfree[6]]
        for oc in range(NIC):
            for tc in range(4):
                bk = y_banks[(oc * 4 + tc) % 2]

                @cx.block.tensor
                def _(t, hp=hp, oc=oc, tc=tc, bk=bk, o_dep=o_dep):
                    cx.wait(t, "tensor", o_dep[0], o_dep[1])
                    fe, fv = y_free[(oc * 4 + tc) % 2]
                    cx.wait(t, "tensor", fe, fv)
                    cx.inc(cx.pe, t.matmul(bk[:, 0:512],
                                           a["wo_t"][:, oc * 128:(oc + 1) * 128],
                                           a["o_fm"][:, tc * 512:(tc + 1) * 512],
                                           start=True, stop=True))
                pev = cx.pe.val

                @cx.block.vector
                def _(v, hp=hp, oc=oc, tc=tc, bk=bk, pev=pev):
                    cx.wait(v, "vector", cx.pe, pev)
                    cx.wait(v, "vector", cx.dve, bias_done)
                    xs = x[:, oc * T + tc * 512: oc * T + (tc + 1) * 512]
                    cx.inc(cx.dve, v.tensor_tensor(xs, xs, bk[:, 0:512], ALU.add))
                y_free[(oc * 4 + tc) % 2] = (cx.dve, cx.dve.val)
        bfree[5], bfree[6] = y_free
        a["_wt_pe_free"] = cx.pe.val

    r["bank_free"] = list(bfree)
    return (cx.dve, cx.dve.val)


def emit_window_subtile(cx, r, a, hp, st, c0, W, nblk, mk, bo_fm, xn_dep, bfree):
    xn = a["xn_full"]
    banks = r["ps_mm"]
    b_p0, b_p1, b_s, b_t, b_pv, b_y0, b_y1, b_r = banks
    mk_tile, mk_dep = mk

    @cx.block.tensor
    def _(t):
        fe, fv = bfree[7]
        cx.wait(t, "tensor", fe, fv)
        for ic in range(NIC):
            cx.inc(cx.pe, t.matmul(b_r[:, 0:128],
                                   a["wq_t"][:, ic * 128:(ic + 1) * 128],
                                   xn[:, ic * T + st * 128: ic * T + (st + 1) * 128],
                                   start=(ic == 0), stop=(ic == NIC - 1)))
    pe_q = cx.pe.val

    @cx.block.vector
    def _(v):
        cx.wait(v, "vector", cx.pe, pe_q)
        cx.wait(v, "vector", cx.pe, a.get("_q_pe_cons", 0))
        cx.inc(cx.dve, v.tensor_scalar(a["q_bf"][:, :], b_r[:, 0:128],
                                       1.0 / 8.0, None, ALU.mult))
    q_dve = cx.dve.val
    bfree[7] = (cx.dve, q_dve)

    for h2 in range(2):
        @cx.block.tensor
        def _(t, h2=h2):
            cx.wait(t, "tensor", cx.dve, q_dve)
            fe, fv = bfree[2]
            cx.wait(t, "tensor", fe, fv)
            cx.inc(cx.pe, t.matmul(b_s[:, 0:W],
                                   a["q_bf"][h2 * 64:(h2 + 1) * 64, :],
                                   a["k_fm"][h2 * 64:(h2 + 1) * 64, c0:c0 + W],
                                   start=True, stop=True))
        pe_s = cx.pe.val
        a["_q_pe_cons"] = pe_s

        @cx.block.vector
        def _(v, h2=h2):
            cx.wait(v, "vector", cx.pe, pe_s)
            cx.wait(v, "vector", mk_dep[0], mk_dep[1])
            cx.wait(v, "vector", cx.act, a.get("_sm_act", 0))
            cx.inc(cx.dve, v.tensor_tensor(a["s_m"][:, 0:W], b_s[:, 0:W],
                                           mk_tile[:, 0:W], ALU.add))
            cx.wait(v, "vector", cx.dve, cx.dve.val)
            cx.inc(cx.dve, v.tensor_reduce(a["rows"][:, 0:1], a["s_m"][:, 0:W],
                                           AX.X, ALU.max, negate=True))
        dve_sm = cx.dve.val
        bfree[2] = (cx.dve, dve_sm)

        @cx.block.scalar
        def _(sc, h2=h2):
            cx.wait(sc, "scalar", cx.dve, dve_sm)
            cx.inc(cx.act, sc.activation(a["p_f"][:, 0:W], a["s_m"][:, 0:W], AF.Exp,
                                         bias=a["rows"][:, 0:1],
                                         accum_out=a["rows"][:, 1:2]))
        act_p = cx.act.val
        a["_sm_act"] = act_p

        @cx.block.vector
        def _(v, h2=h2):
            cx.wait(v, "vector", cx.act, act_p)
            cx.wait(v, "vector", cx.pe, a.get("_pn_pe", 0))
            cx.inc(cx.dve, v.reciprocal(a["rows"][:, 2:3], a["rows"][:, 1:2]))
            cx.wait(v, "vector", cx.dve, cx.dve.val)
            cx.inc(cx.dve, v.tensor_scalar(a["p_n"][:, 0:W], a["p_f"][:, 0:W],
                                           a["rows"][:, 2:3], None, ALU.mult))
        dve_pn = cx.dve.val

        @cx.block.tensor
        def _(t, h2=h2):
            cx.wait(t, "tensor", cx.dve, dve_pn)
            fe, fv = bfree[3]
            cx.wait(t, "tensor", fe, fv)
            for j in range(nblk):
                cx.inc(cx.pe, t.transpose(b_t[:, j * 128:(j + 1) * 128],
                                          a["p_n"][:, j * 128:(j + 1) * 128],
                                          r["id32"][:, :]))
        pe_pt = cx.pe.val
        a["_pn_pe"] = pe_pt

        @cx.block.vector
        def _(v, h2=h2):
            cx.wait(v, "vector", cx.pe, pe_pt)
            cx.wait(v, "vector", cx.pe, a.get("_pt_pe_cons", 0))
            cx.inc(cx.dve, v.tensor_copy(a["pt_bf"][:, 0:nblk * 128], b_t[:, 0:nblk * 128]))
        dve_pt = cx.dve.val
        bfree[3] = (cx.dve, dve_pt)

        @cx.block.tensor
        def _(t, h2=h2):
            cx.wait(t, "tensor", cx.dve, dve_pt)
            fe, fv = bfree[4]
            cx.wait(t, "tensor", fe, fv)
            for j in range(nblk):
                cx.inc(cx.pe, t.matmul(b_pv[0:64, 0:128],
                                       a["v_tm"][:, c0 + j * 128 + h2 * 64: c0 + j * 128 + (h2 + 1) * 64],
                                       a["pt_bf"][:, j * 128:(j + 1) * 128],
                                       start=(j == 0), stop=(j == nblk - 1)))
        pe_pv = cx.pe.val
        a["_pt_pe_cons"] = pe_pv

        @cx.block.scalar
        def _(sc, h2=h2, st=st):
            cx.wait(sc, "scalar", cx.pe, pe_pv)
            cx.inc(cx.act, sc.activation(
                a["o_fm"][h2 * 64:(h2 + 1) * 64, st * 128:(st + 1) * 128],
                b_pv[0:64, 0:128], AF.Copy))
        bfree[4] = (cx.act, cx.act.val)
    a["_o_dep"] = (cx.act, cx.act.val)


def build_window_prog(in_tm=False, out_tm=False):
    nc = bass.Bass("TRN2", target_bir_lowering=False, debug=False)
    x_in = nc.dram_tensor("x_in", [T, D] if in_tm else [D, T], F32,
                          kind="ExternalInput").ap()
    x_out = nc.dram_tensor("x_out", [T, D] if out_tm else [D, T], F32,
                           kind="ExternalOutput").ap()
    w, scr1, scr2 = {}, {}, {}
    for nme, shape in WA_W_SHAPES:
        w[nme] = nc.dram_tensor(nme, list(shape), F32, kind="ExternalInput").ap()
        scr1[nme] = nc.dram_tensor("scr1_" + nme, list(shape), BF16, kind="Internal").ap()
        scr2[nme] = nc.dram_tensor("scr2_" + nme, [shape[1], shape[0]], BF16,
                                   kind="Internal").ap()
    bo = nc.dram_tensor("bo", [D], F32, kind="ExternalInput").ap()
    id32 = nc.dram_tensor("ident_f32", [128, 128], F32, kind="ExternalInput").ap()
    cst = {}
    for nme in WA_CONST_NAMES:
        shp = [128, 384] if nme == "w_mask_int" else [128, 256]
        for p in ("_hi", "_lo"):
            cst[nme + p] = nc.dram_tensor(nme + p, shp, F32, kind="ExternalInput").ap()

    with ExitStack() as es:
        cx = Ctx(nc, es)
        r = alloc_common(cx, es)
        a = alloc_window(cx, es)
        r["hl_scr"] = sb(cx, es, "hl_scr", [128, 768], F32)

        @cx.block.sync
        def _(q):
            cx.inc(r["ld_id"], q.dma_start(out=r["id32"][:, :], in_=id32[:, :]), 16)

        x_dep = emit_load_x(cx, es, r, x_in, in_tm)
        pp_rings = [a["xn_full"][:, 0:4096], a["xn_full"][:, 4096:8192]]
        wdone = emit_prepass(cx, es, r, [(n, w[n], scr1[n], scr2[n])
                                         for n, _ in WA_W_SHAPES], rings=pp_rings)
        r["_ln_gate"] = r["_pp_all"]
        bo_fm = emit_load_scale_fm(cx, es, r, "bo", bo, D)
        masks = {}
        masks["int"] = emit_hilo(cx, es, r, "mk_int", cst["w_mask_int_hi"],
                                 cst["w_mask_int_lo"], [128, 384])
        masks["first"] = emit_hilo(cx, es, r, "mk_first", cst["w_mask_first_hi"],
                                   cst["w_mask_first_lo"], [128, 256])
        masks["last"] = emit_hilo(cx, es, r, "mk_last", cst["w_mask_last_hi"],
                                  cst["w_mask_last_lo"], [128, 256])

        x_dep = emit_window(cx, es, r, a, scr2, masks, bo_fm, wdone, x_dep)
        emit_store_x(cx, es, r, x_out, out_tm, x_dep)
    return nc


FUSED_FFN = {0: [("w1", (HID, D)), ("w2", (HID, D)), ("h1", (HID, HID)),
                 ("h2", (HID, HID)), ("w3", (D, HID))]}


def _stage_gate(cx, r):
    g = [(cx.pe, cx.pe.val), (cx.dve, cx.dve.val), (cx.act, cx.act.val),
         (cx.pool, cx.pool.val)]
    r["_stage_gate"] = g
    return g


def build_fused_prog(n_stages=6):
    nc = bass.Bass("TRN2", target_bir_lowering=False, debug=False)
    x_in = nc.dram_tensor("x_in", [T, D], F32, kind="ExternalInput").ap()
    x_out = nc.dram_tensor("x_out", [T, D], F32, kind="ExternalOutput").ap()
    id32 = nc.dram_tensor("ident_f32", [128, 128], F32, kind="ExternalInput").ap()

    wnames = []
    for li in (0, 1):
        for nme, shape in [("w1", (HID, D)), ("w2", (HID, D)), ("h1", (HID, HID)),
                           ("h2", (HID, HID)), ("w3", (D, HID))]:
            wnames.append((f"L{li}_ffn_{nme}_w", shape))
    for nme, shape in RN_W_SHAPES:
        wnames.append((f"L0_attn_{nme}", shape))
    for nme, shape in WA_W_SHAPES:
        wnames.append((f"L1_attn_{nme}", shape))
    w, scr1, scr2 = {}, {}, {}
    for nme, shape in wnames:
        w[nme] = nc.dram_tensor(nme, list(shape), F32, kind="ExternalInput").ap()
        scr1[nme] = nc.dram_tensor("s1_" + nme, list(shape), BF16, kind="Internal").ap()
        scr2[nme] = nc.dram_tensor("s2_" + nme, [shape[1], shape[0]], BF16,
                                   kind="Internal").ap()
    sv = {}
    for li in (0, 1):
        for nme in ("w1", "w2", "h1", "h2"):
            sv[f"L{li}_ffn_{nme}_s"] = nc.dram_tensor(f"L{li}_ffn_{nme}_s", [HID], F32,
                                                      kind="ExternalInput").ap()
        sv[f"L{li}_ffn_w3_s"] = nc.dram_tensor(f"L{li}_ffn_w3_s", [D], F32,
                                               kind="ExternalInput").ap()
    gnw = nc.dram_tensor("L0_attn_g_norm_w", [RN_DV], F32, kind="ExternalInput").ap()
    bo = nc.dram_tensor("L1_attn_bo", [D], F32, kind="ExternalInput").ap()
    cst = {}
    for nme in RN_CONST_NAMES:
        shp = {"rope_cos": [128, T], "rope_sin": [128, T],
               "rn_maskt": [RN_H * 128, 128], "rn_qsc": [RN_H * 128, 128],
               "rn_ksc": [RN_H * 128, 1], "rn_gc": [RN_H * 128, 1]}[nme]
        for p in ("_hi", "_lo"):
            cst[nme + p] = nc.dram_tensor(nme + p, shp, F32, kind="ExternalInput").ap()
    for nme in WA_CONST_NAMES:
        shp = [128, 384] if nme == "w_mask_int" else [128, 256]
        for p in ("_hi", "_lo"):
            cst[nme + p] = nc.dram_tensor(nme + p, shp, F32, kind="ExternalInput").ap()

    with ExitStack() as es:
        cx = Ctx(nc, es)
        A_bf = sb(cx, es, "A_bf", [128, 32768], BF16)
        A_f32 = sb(cx, es, "A_f32", [128, 6208], F32)
        B_bf = sb(cx, es, "B_bf", [128, 16384], BF16)
        B_f32 = sb(cx, es, "B_f32", [128, 1280], F32)

        r = {}
        r["x"] = sb(cx, es, "x_res", [128, NIC * T], F32)
        r["ln_rows"] = sb(cx, es, "ln_rows", [1, 2 * TT], F32)
        r["ones_bf"] = sb(cx, es, "ones_bf", [128, 1], BF16)
        r["ones_f1"] = sb(cx, es, "ones_f1", [1, 128], F32)
        r["eps_t"] = sb(cx, es, "eps_t", [1, 1], F32)
        r["id32"] = sb(cx, es, "id32", [128, 128], F32)
        r["x_stg"] = B_f32[:, 0:1024]
        r["ln_tmp"] = B_f32[:, 0:1024]
        r["ps_mm"] = [ps(cx, es, f"psb{i}", [128, 512]) for i in range(8)]
        r["bank_free"] = [(cx.pe, 0)] * 8
        r["ld_pro"] = cx.new_dma_sem("ld_pro")
        r["ld_id"] = cx.new_dma_sem("ld_id")
        r["ld_x"] = cx.new_dma_sem("ld_x")
        r["_w1_act"] = {}
        r["_ln_pe_prev"] = 0
        r["hl_scr"] = A_f32[:, 0:4096]

        @cx.block.gpsimd
        def _(gp):
            cx.inc(cx.pool, gp.memset(r["ones_bf"][:, :], 1.0))
            cx.inc(cx.pool, gp.memset(r["ones_f1"][:, :], 1.0))
            cx.inc(cx.pool, gp.memset(r["eps_t"][:, :], EPS))
        r["ones_dep"] = cx.pool.val

        @cx.block.sync
        def _(q):
            cx.inc(r["ld_id"], q.dma_start(out=r["id32"][:, :], in_=id32[:, :]), 16)

        x_dep = emit_load_x(cx, es, r, x_in, True)

        pp_rings = [sb(cx, es, f"ppw{i}", [128, 1024], BF16)[:] for i in range(4)]
        plan = []
        order = ([f"L0_ffn_{n}_w" for n in ("w1", "w2", "h1", "h2", "w3")] +
                 [f"L0_attn_{n}" for n, _ in RN_W_SHAPES] +
                 [f"L1_ffn_{n}_w" for n in ("w1", "w2", "h1", "h2", "w3")] +
                 [f"L1_attn_{n}" for n, _ in WA_W_SHAPES])
        for nme in order:
            plan.append((nme, w[nme], scr1[nme], scr2[nme]))
        pp = PrepassStream(cx, es, r, plan, pp_rings)
        r["_pp_step"] = lambda: pp.step(1)
        lim_l0 = next(i for i, st in enumerate(pp.steps) if st[0].startswith("L1"))
        pp.limit = lim_l0

        class WdoneView:
            def __init__(self, mapping):
                self.mapping = mapping

            def __getitem__(self, key):
                full = self.mapping[key]
                pp.drain(full)
                return pp.wdone(full)

        scales = {}
        for li in (0, 1):
            for nme in ("w1", "w2", "h1", "h2", "w3"):
                key = f"L{li}_ffn_{nme}_s"
                scales[key] = emit_load_scale_fm(cx, es, r, key, sv[key],
                                                 HID if nme != "w3" else D)
        gnw_fm, gnw_dep = emit_load_scale_fm(cx, es, r, "gnw", gnw, RN_DV)
        bo_fm = emit_load_scale_fm(cx, es, r, "bo", bo, D)

        def bind_ffn():
            r["h_a"] = A_bf[:, 0:16384]
            r["z1"] = A_bf[:, 16384:32768]
            r["xn_t"] = B_bf[:, 0:4096]
            r["wslc"] = [B_bf[:, 4096 + i * 1024:4096 + (i + 1) * 1024] for i in range(4)]
            r["scrA"] = B_bf[:, 8192:12288]
            r["scrB"] = B_bf[:, 12288:16384]
            r["ytmp"] = B_f32[:, 0:1024]

        if "wslc_sem" not in r:
            r["wslc_sem"] = [cx.new_dma_sem(f"wsl{i}") for i in range(4)]
            r["wslc_free"] = [0, 0, 0, 0]
            r["wslc_next"] = [0]

        def ffn_maps(li):
            s2 = {n: scr2[f"L{li}_ffn_{n}_w"] for n in ("w1", "w2", "h1", "h2", "w3")}
            sc = {n + "_s": scales[f"L{li}_ffn_{n}_s"] for n in ("w1", "w2", "h1", "h2", "w3")}
            wd = WdoneView({n: f"L{li}_ffn_{n}_w" for n in ("w1", "w2", "h1", "h2", "w3")})
            return s2, sc, wd

        a_rn = {
            "ld_hw": cx.new_dma_sem("rn_ld_hw"),
            "ld_hw2": cx.new_dma_sem("rn_ld_hw2"),
            "xn_full": A_bf[:, 0:16384],
            "H": A_f32[:, 0:4096],
            "H_bf": A_bf[:, 16384:17408],
            "wq_t": B_bf[:, 0:2048],
            "wk_t": B_bf[:, 2048:4096],
            "wv_t": B_bf[:, 4096:8192],
            "wg_t": B_bf[:, 8192:12288],
            "wo_t": B_bf[:, 12288:16384],
            "q_bf": A_bf[:, 21504:21760],
            "qt_bf": A_bf[:, 21760:22016],
            "k_bf": A_bf[:, 22016:22272],
            "k_tm": A_bf[:, 22272:22528],
            "v_bf": A_bf[:, 22528:23040],
            "g_si": A_bf[:, 23040:23552],
            "s_bf": A_bf[:, 23552:23680],
            "o_bf": A_bf[:, 23680:24192],
            "o_g": A_bf[:, 23680:24192],
            "osq": A_bf[:, 28288:28800],
            "k_f32": A_f32[:, 5128:5384],
            "t128": A_f32[:, 5384:5896],
            "rrow": A_f32[0:1, 5896:6152],
        }
        rn_scr2 = {n: scr2[f"L0_attn_{n}"] for n, _ in RN_W_SHAPES}
        rn_wdone = WdoneView({n: f"L0_attn_{n}" for n, _ in RN_W_SHAPES})

        a_wa = {
            "ld_hw": cx.new_dma_sem("wa_ld_hw"),
            "xn_full": A_bf[:, 0:16384],
            "k_fm": A_bf[:, 16384:18432],
            "v_tm": A_bf[:, 18432:20480],
            "o_fm": A_bf[:, 20480:22528],
            "q_bf": B_bf[:, 4480:4608],
            "pt_bf": B_bf[:, 4096:4480],
            "s_m": B_f32[:, 0:384],
            "p_f": B_f32[:, 384:768],
            "p_n": B_f32[:, 768:1152],
            "rows": A_f32[:, 6152:6155],
            "wq_t": B_bf[:, 0:1024],
            "wk_t": B_bf[:, 1024:2048],
            "wv_t": B_bf[:, 2048:3072],
            "wo_t": B_bf[:, 3072:4096],
        }
        wa_scr2 = {n: scr2[f"L1_attn_{n}"] for n, _ in WA_W_SHAPES}
        wa_wdone = WdoneView({n: f"L1_attn_{n}" for n, _ in WA_W_SHAPES})

        def bind_attn_scr():
            r["scrA"] = A_bf[:, 24192:28288]
            r["scrB"] = A_bf[:, 28288:32384]

        def bind_wa_scr():
            r["scrA"] = A_bf[:, 24192:28288]
            r["scrB"] = A_bf[:, 28288:32384]

        def stage_ffn(li):
            nonlocal x_dep
            bind_ffn()
            _stage_gate(cx, r)
            s2m, scm, wdm = ffn_maps(li)
            x_dep = emit_ffn(cx, es, r, s2m, scm, wdm, x_dep)

        def stage_rn():
            nonlocal x_dep
            pp.limit = None
            gate = _stage_gate(cx, r)
            r["_hl_gate"] = gate
            hp_re = lambda ap: ap.rearrange("(h p) c -> p h c", p=128)
            hv_re = lambda ap: ap.rearrange("p (h c) -> p h c", h=RN_H)
            consts = {}
            consts["cos"], _ = emit_hilo(cx, es, r, "cos_f", cst["rope_cos_hi"],
                                         cst["rope_cos_lo"], [128, T],
                                         out=A_bf[:, 17408:19456])
            consts["sin"], _ = emit_hilo(cx, es, r, "sin_f", cst["rope_sin_hi"],
                                         cst["rope_sin_lo"], [128, T],
                                         out=A_bf[:, 19456:21504])
            consts["maskt"], _ = emit_hilo(cx, es, r, "maskt_f", cst["rn_maskt_hi"],
                                           cst["rn_maskt_lo"], [128, RN_H * 128],
                                           rearr=hp_re, out_rearr=hv_re,
                                           out=A_f32[:, 4096:4608])
            consts["qsc"], _ = emit_hilo(cx, es, r, "qsc_f", cst["rn_qsc_hi"],
                                         cst["rn_qsc_lo"], [128, RN_H * 128],
                                         rearr=hp_re, out_rearr=hv_re,
                                         out=A_f32[:, 4608:5120])
            consts["ksc"], _ = emit_hilo(cx, es, r, "ksc_f", cst["rn_ksc_hi"],
                                         cst["rn_ksc_lo"], [128, RN_H],
                                         rearr=hp_re, out_rearr=hv_re,
                                         out=A_f32[:, 5120:5124])
            consts["gc"], cdep = emit_hilo(cx, es, r, "gc_f", cst["rn_gc_hi"],
                                           cst["rn_gc_lo"], [128, RN_H],
                                           rearr=hp_re, out_rearr=hv_re,
                                           out=A_f32[:, 5124:5128])
            consts["dep"] = (cx.dve, max(cdep[1], gnw_dep[1]))
            bind_attn_scr()
            _stage_gate(cx, r)
            x_dep = emit_retnet(cx, es, r, a_rn, rn_scr2, gnw_fm, consts, rn_wdone, x_dep)

        def stage_wa():
            nonlocal x_dep
            gate = _stage_gate(cx, r)
            r["_hl_gate"] = gate
            masks = {}
            masks["int"] = emit_hilo(cx, es, r, "mk_int", cst["w_mask_int_hi"],
                                     cst["w_mask_int_lo"], [128, 384],
                                     out=A_f32[:, 4096:4480])
            masks["first"] = emit_hilo(cx, es, r, "mk_first", cst["w_mask_first_hi"],
                                       cst["w_mask_first_lo"], [128, 256],
                                       out=A_f32[:, 4480:4736])
            masks["last"] = emit_hilo(cx, es, r, "mk_last", cst["w_mask_last_hi"],
                                      cst["w_mask_last_lo"], [128, 256],
                                      out=A_f32[:, 4736:4992])
            bind_wa_scr()
            _stage_gate(cx, r)
            x_dep = emit_window(cx, es, r, a_wa, wa_scr2, masks, bo_fm, wa_wdone, x_dep)

        stages = [lambda: stage_ffn(0), stage_rn, lambda: stage_ffn(0),
                  lambda: stage_ffn(1), stage_wa, lambda: stage_ffn(1)]
        for si in range(n_stages):
            stages[si]()

        emit_store_x(cx, es, r, x_out, True, x_dep)
    return nc


def fused_im(inputs):
    im = {"ident_f32": _CONSTS["ident_f32"]}
    for li in (0, 1):
        for nme in ("w1", "w2", "h1", "h2", "w3"):
            im[f"L{li}_ffn_{nme}_w"] = inputs[f"L{li}_ffn_{nme}_w"]
            im[f"L{li}_ffn_{nme}_s"] = inputs[f"L{li}_ffn_{nme}_s"]
    for nme, _ in RN_W_SHAPES:
        im["L0_attn_" + nme] = inputs["L0_attn_" + nme]
    im["L0_attn_g_norm_w"] = inputs["L0_attn_g_norm_w"]
    for nme, _ in WA_W_SHAPES:
        im["L1_attn_" + nme] = inputs["L1_attn_" + nme]
    im["L1_attn_bo"] = inputs["L1_attn_bo"]
    for nme in RN_CONST_NAMES + WA_CONST_NAMES:
        for p in ("_hi", "_lo"):
            im[nme + p] = _CONSTS[nme + p]
    return im


_PROGS = {}


def _prog(kind, in_tm, out_tm):
    key = (kind, in_tm, out_tm)
    if key not in _PROGS:
        if kind == "ffn":
            _PROGS[key] = build_ffn_prog(in_tm, out_tm)
        elif kind == "rn":
            _PROGS[key] = build_retnet_prog(in_tm, out_tm)
        else:
            _PROGS[key] = build_window_prog(in_tm, out_tm)
    return _PROGS[key]


def _ffn_im(inputs, layer):
    p = f"L{layer}_ffn_"
    im = {"ident_f32": _CONSTS["ident_f32"]}
    for nme in ("w1", "w2", "h1", "h2", "w3"):
        im[nme] = inputs[p + nme + "_w"]
        im[nme + "_s"] = inputs[p + nme + "_s"]
    return im


def _rn_im(inputs):
    im = {"ident_f32": _CONSTS["ident_f32"], "g_norm_w": inputs["L0_attn_g_norm_w"]}
    for nme in ("Wq", "Wk", "Wv", "Wg", "Wo"):
        im[nme] = inputs["L0_attn_" + nme]
    for nme in RN_CONST_NAMES:
        for pfx in ("_hi", "_lo"):
            im[nme + pfx] = _CONSTS[nme + pfx]
    return im


def _wa_im(inputs):
    im = {"ident_f32": _CONSTS["ident_f32"], "bo": inputs["L1_attn_bo"]}
    for nme in ("Wq", "Wk", "Wv", "Wo"):
        im[nme] = inputs["L1_attn_" + nme]
    for nme in WA_CONST_NAMES:
        for pfx in ("_hi", "_lo"):
            im[nme + pfx] = _CONSTS[nme + pfx]
    return im


def kernel(**inputs):
    x = np.ascontiguousarray(inputs["L0_attn_x"], dtype=np.float32)
    n = 8
    if "fused" not in _PROGS:
        _PROGS["fused"] = build_fused_prog()
    nc = _PROGS["fused"]
    wim = fused_im(inputs)
    in_maps = [dict(wim, x_in=np.ascontiguousarray(x[b])) for b in range(n)]
    res = run_bass_kernel_spmd(nc, in_maps, core_ids=list(range(n)))
    return np.stack([res.results[b]["x_out"] for b in range(n)], axis=0)


def kernel_unfused(**inputs):
    x = np.ascontiguousarray(inputs["L0_attn_x"], dtype=np.float32)
    n = 8
    shards = [np.ascontiguousarray(x[b]) for b in range(n)]

    seq = [
        ("ffn", _ffn_im(inputs, 0), True, False),
        ("rn", _rn_im(inputs), False, False),
        ("ffn", _ffn_im(inputs, 0), False, False),
        ("ffn", _ffn_im(inputs, 1), False, False),
        ("wa", _wa_im(inputs), False, False),
        ("ffn", _ffn_im(inputs, 1), False, True),
    ]
    cur = shards
    for (kind, wim, in_tm, out_tm) in seq:
        nc = _prog(kind, in_tm, out_tm)
        in_maps = [dict(wim, x_in=cur[b]) for b in range(n)]
        res = run_bass_kernel_spmd(nc, in_maps, core_ids=list(range(n)))
        cur = [np.ascontiguousarray(res.results[b]["x_out"]) for b in range(n)]
    return np.stack(cur, axis=0)


# version 06
# speedup vs baseline: 1.1103x; 1.0481x over previous
import numpy as np
import ml_dtypes
from contextlib import ExitStack

import concourse.bass as bass
import concourse.mybir as mybir
from concourse.bass_utils import run_bass_kernel_spmd

F32 = mybir.dt.float32
BF16 = mybir.dt.bfloat16
AF = mybir.ActivationFunctionType
ALU = mybir.AluOpType
AX = mybir.AxisListType

D = 1024
T = 2048
HID = 4096
TT = 512
NTT = T // TT
NIC = D // 128
EPS = 1e-5

RN_H, RN_DK, RN_DV, CHUNK = 4, 256, 512, 128
W_H, W_DK, W_WIN = 16, 64, 64


def _hilo(x64):
    hi = x64.astype(np.float32)
    lo = (x64 - hi.astype(np.float64)).astype(np.float32)
    return hi, lo


def build_consts():
    c = {}
    c["ident_f32"] = np.eye(128, dtype=np.float32)

    j = np.arange(0, RN_DK, 2, dtype=np.float64) / RN_DK
    inv_freq = 1.0 / (10000.0 ** j)
    ang = inv_freq[:, None] * np.arange(T, dtype=np.float64)[None, :]
    c["rope_cos_hi"], c["rope_cos_lo"] = _hilo(np.cos(ang))
    c["rope_sin_hi"], c["rope_sin_lo"] = _hilo(np.sin(ang))
    c["rope_cos_tm_hi"], c["rope_cos_tm_lo"] = _hilo(np.cos(ang).T.copy())
    c["rope_sin_tm_hi"], c["rope_sin_tm_lo"] = _hilo(np.sin(ang).T.copy())

    r = np.arange(CHUNK, dtype=np.float64)
    maskt = np.zeros((RN_H * 128, CHUNK))
    qsc = np.zeros((RN_H * 128, CHUNK))
    ksc = np.zeros((RN_H * 128, 1))
    gC = np.zeros((RN_H * 128, 1))
    for h in range(RN_H):
        lg = float(np.log1p(-np.exp2(-5.0 - h)))
        diff = r[None, :] - r[:, None]
        maskt[h * 128:(h + 1) * 128, :] = np.where(diff >= 0, np.exp(lg * diff) / 16.0, 0.0)
        qsc[h * 128:(h + 1) * 128, :] = np.exp(lg * (r[None, :] + 1.0)) / 16.0
        ksc[h * 128:(h + 1) * 128, 0] = np.exp(lg * (CHUNK - 1.0 - r))
        gC[h * 128:(h + 1) * 128, 0] = np.exp(lg * CHUNK)
    c["rn_maskt_hi"], c["rn_maskt_lo"] = _hilo(maskt)
    c["rn_qsc_hi"], c["rn_qsc_lo"] = _hilo(qsc)
    c["rn_ksc_hi"], c["rn_ksc_lo"] = _hilo(ksc)
    c["rn_gc_hi"], c["rn_gc_lo"] = _hilo(gC)

    NEG = -1e30
    mi = np.full((128, 384), NEG)
    for rr in range(128):
        mi[rr, 96 + rr:160 + rr] = 0.0
    c["w_mask_int_hi"], c["w_mask_int_lo"] = _hilo(mi)
    mf = np.full((128, 256), NEG)
    for rr in range(128):
        lo_k, hi_k = rr - 32, min(rr + 31, 255)
        if lo_k < 0:
            mf[rr, 0] = np.log(float(1 - lo_k))
            mf[rr, 1:hi_k + 1] = 0.0
        else:
            mf[rr, lo_k:hi_k + 1] = 0.0
    c["w_mask_first_hi"], c["w_mask_first_lo"] = _hilo(mf)
    ml = np.full((128, 256), NEG)
    for rr in range(128):
        t = 1920 + rr
        lo_k, hi_k = t - 32 - 1792, t + 31 - 1792
        if hi_k > 255:
            ml[rr, 255] = np.log(float(hi_k - 255 + 1))
            ml[rr, lo_k:255] = 0.0
        else:
            ml[rr, lo_k:hi_k + 1] = 0.0
    c["w_mask_last_hi"], c["w_mask_last_lo"] = _hilo(ml)
    return c


_CONSTS = build_consts()


class Eng:
    def __init__(self, name, sem):
        self.name = name
        self.sem = sem
        self.val = 0


class Ctx:
    def __init__(self, nc, es: ExitStack):
        self.nc = nc
        self.es = es
        E = es.enter_context
        self.pe = Eng("pe", E(nc.semaphore("pe_s")))
        self.dve = Eng("dve", E(nc.semaphore("dve_s")))
        self.act = Eng("act", E(nc.semaphore("act_s")))
        self.pool = Eng("pool", E(nc.semaphore("pool_s")))
        self.ld_sp = Eng("ld_sp", E(nc.semaphore("ld_sp_s")))
        self.ld_act = Eng("ld_act", E(nc.semaphore("ld_act_s")))
        self.ldg = Eng("ldg", E(nc.semaphore("ldg_s")))
        self.st = Eng("st", E(nc.semaphore("st_s")))
        self.block = E(nc.Block())
        self._waited = {}

    def wait(self, q, qname, eng, val):
        if val <= 0:
            return
        assert val <= eng.val, (
            f"wait for {eng.name}>={val} but only {eng.val} scheduled (hang)")
        key = (qname, eng.name)
        if self._waited.get(key, 0) >= val:
            return
        q.wait_ge(eng.sem, val)
        self._waited[key] = val

    def inc(self, eng, instr, n=1):
        instr.then_inc(eng.sem, n)
        eng.val += n
        return eng.val

    def new_dma_sem(self, name):
        return Eng(name, self.es.enter_context(self.nc.semaphore(name)))

    def wait_list(self, q, qname, deps):
        for (eng, val) in deps:
            self.wait(q, qname, eng, val)


def sb(cx, es, name, shape, dtype):
    return es.enter_context(cx.nc.sbuf_tensor(name, shape, dtype))


def ps(cx, es, name, shape, dtype=F32):
    return es.enter_context(cx.nc.psum_tensor(name, shape, dtype))


def alloc_common(cx, es):
    r = {}
    r["x"] = sb(cx, es, "x_res", [128, NIC * T], F32)
    r["scrA"] = sb(cx, es, "scrA", [128, NIC * TT], BF16)
    r["scrB"] = sb(cx, es, "scrB", [128, NIC * TT], BF16)
    r["ln_rows"] = sb(cx, es, "ln_rows", [1, 2 * TT], F32)
    r["ones_bf"] = sb(cx, es, "ones_bf", [128, 1], BF16)
    r["ones_f1"] = sb(cx, es, "ones_f1", [1, 128], F32)
    r["eps_t"] = sb(cx, es, "eps_t", [1, 1], F32)
    r["id32"] = sb(cx, es, "id32", [128, 128], F32)
    r["id_bf"] = sb(cx, es, "id_bf", [128, 128], BF16)
    r["x_stg"] = sb(cx, es, "x_stg", [128, D], F32)
    r["ln_tmp"] = r["x_stg"]
    r["ps_mm"] = [ps(cx, es, f"psb{i}", [128, 512]) for i in range(8)]
    r["bank_free"] = [(cx.pe, 0)] * 8
    r["ld_pro"] = cx.new_dma_sem("ld_pro")
    r["ld_id"] = cx.new_dma_sem("ld_id")
    r["ld_x"] = cx.new_dma_sem("ld_x")
    r["_w1_act"] = {}
    r["_ln_pe_prev"] = 0

    @cx.block.gpsimd
    def _(gp):
        cx.inc(cx.pool, gp.memset(r["ones_bf"][:, :], 1.0))
        cx.inc(cx.pool, gp.memset(r["ones_f1"][:, :], 1.0))
        cx.inc(cx.pool, gp.memset(r["eps_t"][:, :], EPS))
    r["ones_dep"] = cx.pool.val
    return r


class PrepassStream:

    def __init__(self, cx, es, r, plan, rings):
        self.cx = cx
        self.r = r
        self.rings = rings
        self.nring = len(rings)
        self.ring_cols = rings[0].shape[1]
        self.ppl = [cx.new_dma_sem(f"ppl{i}") for i in range(self.nring)]
        self.pps = [cx.new_dma_sem(f"pps{i}") for i in range(self.nring)]
        self.ring_free = [0] * self.nring
        self.steps = []
        self.cast_done = {}
        self.cast_info = {}
        self.worder = []
        self.done = {}
        self.idx = 0
        self.ring_i = 0
        self.limit = None
        for (name, w, scr1, scr2) in plan:
            O, I = w.shape
            self.cast_info[name] = (w, scr1, O, I)
            self.worder.append(name)
            for ic in range(I // 128):
                for o0 in range(0, O, self.ring_cols):
                    o1 = min(O, o0 + self.ring_cols)
                    self.steps.append((name, scr1, scr2, ic, o0, o1))

    def _ensure_cast(self, name):
        if name in self.cast_done:
            return
        cx = self.cx
        w, scr1, O, I = self.cast_info[name]
        ldg_w = cx.new_dma_sem("ldg_" + name)

        @cx.block.gpsimd
        def _(gp, w=w, scr1=scr1, O=O, I=I, ldg_w=ldg_w):
            rows_per = max(128, min(O, (8 << 20) // (I * 2)))
            r0 = 0
            while r0 < O:
                r1 = min(O, r0 + rows_per)
                cx.inc(ldg_w, gp.dma_start(out=scr1[r0:r1, :], in_=w[r0:r1, :]), 16)
                r0 = r1
        self.cast_done[name] = (ldg_w, ldg_w.val)

    def step(self, k=1, ignore_limit=False):
        cx = self.cx
        for _ in range(k):
            if self.idx >= len(self.steps):
                return
            if not ignore_limit and self.limit is not None and self.idx >= self.limit:
                return
            name, scr1, scr2, ic, o0, o1 = self.steps[self.idx]
            self.idx += 1
            self._ensure_cast(name)
            wi = self.worder.index(name)
            if wi + 1 < len(self.worder):
                self._ensure_cast(self.worder[wi + 1])
            b = self.ring_i % self.nring
            self.ring_i += 1
            ldg_w, cval = self.cast_done[name]

            @cx.block.sync
            def _(q, scr1=scr1, ic=ic, b=b, o0=o0, o1=o1, ldg_w=ldg_w, cval=cval):
                cx.wait(q, "sync", ldg_w, cval)
                cx.wait(q, "sync", self.pps[b], self.ring_free[b])
                for r0 in range(o0, o1, 1024):
                    r1 = min(o1, r0 + 1024)
                    cx.inc(self.ppl[b], q.dma_start(
                        out=self.rings[b][:, r0 - o0:r1 - o0],
                        in_=scr1[r0:r1, ic * 128:(ic + 1) * 128],
                        transpose=True), 16)
            ldv = self.ppl[b].val

            @cx.block.gpsimd
            def _(q, scr2=scr2, ic=ic, b=b, ldv=ldv, o0=o0, o1=o1):
                cx.wait(q, "gpsimd", self.ppl[b], ldv)
                cx.inc(self.pps[b], q.dma_start(out=scr2[ic * 128:(ic + 1) * 128, o0:o1],
                                                in_=self.rings[b][:, 0:o1 - o0]), 16)
                self.ring_free[b] = self.pps[b].val
            if self.idx == len(self.steps) or self.steps[self.idx][0] != name:
                self.done[name] = [(sem, sem.val) for sem in self.pps]

    def drain(self, name):
        while name not in self.done and self.idx < len(self.steps):
            self.step(ignore_limit=True)

    def drain_all(self):
        while self.idx < len(self.steps):
            self.step(ignore_limit=True)

    def wdone(self, name):
        assert name in self.done, f"prepass for {name} not drained"
        return self.done[name]


def emit_prepass(cx, es, r, plan, rings=None):
    if rings is None:
        rings = [sb(cx, es, "pp_wt0", [128, 1024], BF16)[:],
                 sb(cx, es, "pp_wt1", [128, 1024], BF16)[:],
                 sb(cx, es, "pp_wt2", [128, 1024], BF16)[:],
                 sb(cx, es, "pp_wt3", [128, 1024], BF16)[:]]
    pp = PrepassStream(cx, es, r, plan, rings)
    pp.drain_all()
    r["_pp_all"] = [(sem, sem.val) for sem in pp.pps]
    return {name: pp.wdone(name) for (name, _, _, _) in plan}


def emit_load_scale_fm(cx, es, r, name, s_ap, n, out=None):
    g = n // 128
    if "scale_nat" not in r:
        r["scale_nat"] = sb(cx, es, "scale_nat", [32, 128], F32)
    nat = r["scale_nat"][0:g, :]
    if out is None:
        out = sb(cx, es, name + "_fm", [128, g], F32)
    pt = r["ps_mm"][4]
    ld_pro = r["ld_pro"]

    @cx.block.sync
    def _(q):
        cx.wait(q, "sync", cx.pe, r.get("_pro_pe_prev", 0))
        cx.wait(q, "sync", cx.dve, r.get("_hl_dve", 0))
        cx.inc(ld_pro, q.dma_start(out=nat, in_=s_ap.rearrange("(a b) -> a b", b=128)), 16)
    ldv = ld_pro.val

    @cx.block.tensor
    def _(t):
        cx.wait(t, "tensor", ld_pro, ldv)
        cx.wait(t, "tensor", r["ld_id"], 16)
        fe, fv = r["bank_free"][4]
        cx.wait(t, "tensor", fe, fv)
        cx.inc(cx.pe, t.transpose(pt[0:128, 0:g], nat, r["id32"][0:g, 0:g]))
    pev = cx.pe.val
    r["_pro_pe_prev"] = pev

    @cx.block.vector
    def _(v):
        cx.wait(v, "vector", cx.pe, pev)
        cx.inc(cx.dve, v.tensor_copy(out[:, :], pt[0:128, 0:g]))
    r["bank_free"][4] = (cx.dve, cx.dve.val)
    return out, (cx.dve, cx.dve.val)


class LNRes:
    pass


def emit_ln(cx, r, t0, ncol, x_dep, xn_dst=None):
    x = r["x"]
    if xn_dst is None:
        xn_dst = lambda ic: r["xn_t"][:, ic * ncol:(ic + 1) * ncol]
    xbf, xsq = r["scrA"], r["scrB"]
    rows, tmp = r["ln_rows"], r["ln_tmp"]
    banks = r["ps_mm"]
    bank_free = r["bank_free"]
    s1 = banks[0][0:1, :]
    s2 = banks[3][0:1, :]
    bcr = banks[1]
    bcm = banks[2]
    pe_prev = r["_ln_pe_prev"]

    act_per_chunk = []
    for ic in range(NIC):
        xc = x[:, ic * T + t0: ic * T + t0 + ncol]

        @cx.block.scalar
        def _(sc, ic=ic, xc=xc):
            cx.wait(sc, "scalar", x_dep[0], x_dep[1])
            cx.wait(sc, "scalar", cx.dve, r.get("_hl_dve", 0))
            cx.wait_list(sc, "scalar", r.get("_ln_gate", []))
            cx.wait_list(sc, "scalar", r.get("_stage_gate", []))
            cx.wait(sc, "scalar", cx.pe, pe_prev)
            cx.inc(cx.act, sc.activation(xbf[:, ic * ncol:(ic + 1) * ncol], xc, AF.Copy))
            cx.inc(cx.act, sc.activation(xsq[:, ic * ncol:(ic + 1) * ncol], xc, AF.Square))
        act_per_chunk.append(cx.act.val)

    @cx.block.tensor
    def _(t):
        cx.wait(t, "tensor", cx.pool, r["ones_dep"])
        fe, fv = bank_free[0]
        cx.wait(t, "tensor", fe, fv)
        fe, fv = bank_free[3]
        cx.wait(t, "tensor", fe, fv)
        for ic in range(NIC):
            cx.wait(t, "tensor", cx.act, act_per_chunk[ic])
            cx.inc(cx.pe, t.matmul(s1[:, 0:ncol], r["ones_bf"][:, :],
                                   xbf[:, ic * ncol:(ic + 1) * ncol],
                                   start=(ic == 0), stop=(ic == NIC - 1)))
            cx.inc(cx.pe, t.matmul(s2[:, 0:ncol], r["ones_bf"][:, :],
                                   xsq[:, ic * ncol:(ic + 1) * ncol],
                                   start=(ic == 0), stop=(ic == NIC - 1)))
    pe_stats = cx.pe.val
    r["_ln_pe_prev"] = pe_stats

    r_m = s1[:, 0:ncol]
    r_e2 = s2[:, 0:ncol]
    r_rs = rows[:, 0:ncol]
    r_mr = rows[:, TT:TT + ncol]

    @cx.block.vector
    def _(v):
        cx.wait(v, "vector", cx.pe, pe_stats)
        cx.inc(cx.dve, v.tensor_scalar(r_m, s1[:, 0:ncol], 1.0 / D, None, ALU.mult))
        cx.inc(cx.dve, v.tensor_scalar(r_e2, s2[:, 0:ncol], 1.0 / D, None, ALU.mult))
    dve_ms = cx.dve.val

    @cx.block.scalar
    def _(sc):
        cx.wait(sc, "scalar", cx.dve, dve_ms)
        cx.wait(sc, "scalar", cx.pe, r.get("_ln_rows_pe", 0))
        cx.inc(cx.act, sc.activation(r_rs, r_m, AF.Square))
    act_m2 = cx.act.val

    @cx.block.vector
    def _(v):
        cx.wait(v, "vector", cx.act, act_m2)
        cx.inc(cx.dve, v.tensor_tensor(r_rs, r_e2, r_rs, ALU.subtract))
    dve_rows = cx.dve.val

    @cx.block.scalar
    def _(sc):
        cx.wait(sc, "scalar", cx.dve, dve_rows)
        cx.wait(sc, "scalar", cx.pool, r["ones_dep"])
        cx.inc(cx.act, sc.activation(r_rs, r_rs, AF.Ln, bias=r["eps_t"][:, :]))
        cx.wait(sc, "scalar", cx.act, cx.act.val)
        cx.inc(cx.act, sc.activation(r_rs, r_rs, AF.Exp, scale=-0.5))
    act_rs = cx.act.val

    @cx.block.vector
    def _(v):
        cx.wait(v, "vector", cx.act, act_rs)
        cx.wait(v, "vector", cx.pe, r.get("_ln_rows_pe", 0))
        cx.inc(cx.dve, v.tensor_tensor(r_mr, r_m, r_rs, ALU.mult))
    dve_mr = cx.dve.val
    bank_free[0] = (cx.dve, dve_mr)
    bank_free[3] = (cx.dve, dve_mr)

    @cx.block.tensor
    def _(t):
        cx.wait(t, "tensor", cx.act, act_rs)
        cx.wait(t, "tensor", cx.dve, dve_mr)
        for bidx, row in ((1, r_rs), (2, r_mr)):
            fe, fv = bank_free[bidx]
            cx.wait(t, "tensor", fe, fv)
        cx.inc(cx.pe, t.matmul(bcr[:, 0:ncol], r["ones_f1"][:, :], r_rs, start=True, stop=True))
        cx.inc(cx.pe, t.matmul(bcm[:, 0:ncol], r["ones_f1"][:, :], r_mr, start=True, stop=True))
    pe_bc = cx.pe.val
    r["_ln_rows_pe"] = pe_bc

    for ic in range(NIC):
        xc = x[:, ic * T + t0: ic * T + t0 + ncol]

        @cx.block.vector
        def _(v, ic=ic, xc=xc):
            cx.wait(v, "vector", cx.pe, pe_bc)
            tb = tmp[:, (ic % 2) * TT:(ic % 2) * TT + ncol]
            cx.inc(cx.dve, v.tensor_tensor(tb, xc, bcr[:, 0:ncol], ALU.mult))
            cx.wait(v, "vector", cx.dve, cx.dve.val)
            cx.inc(cx.dve, v.tensor_tensor(xn_dst(ic), tb,
                                           bcm[:, 0:ncol], ALU.subtract))
    res = LNRes()
    res.dve_done = cx.dve.val
    bank_free[1] = (cx.dve, res.dve_done)
    bank_free[2] = (cx.dve, res.dve_done)
    return res


def emit_mm_stage(cx, r, scr2, O, I, rhs_fn, rhs_dep, evac_fn, w_gate,
                  groups=None, qrot=(0, 1), ncol=TT):
    n_oc, n_ic = O // 128, I // 128
    if groups is None:
        groups = [(g * 8, min(8, n_oc - g * 8)) for g in range((n_oc + 7) // 8)]
    banks, bank_free = r["ps_mm"], r["bank_free"]
    wring, wfree = r["wslc"], r["wslc_free"]
    wsem = r["wslc_sem"]
    nslot = len(wring)
    items = [(oc0, now, ic) for (oc0, now) in groups for ic in range(n_ic)]
    LA = nslot - 1
    loads = [None] * len(items)

    def emit_load(j):
        oc0, now, ic = items[j]
        slot = r["wslc_next"][0] % nslot
        r["wslc_next"][0] += 1
        qi = qrot[j % len(qrot)]
        qname = "sync" if qi == 0 else "scalar"

        def body(q):
            cx.wait_list(q, qname, w_gate)
            cx.wait_list(q, qname, r.get("_stage_gate", []))
            cx.wait(q, qname, cx.pe, wfree[slot])
            cx.inc(wsem[slot], q.dma_start(out=wring[slot][:, 0:now * 128],
                                           in_=scr2[ic * 128:(ic + 1) * 128,
                                                    oc0 * 128:(oc0 + now) * 128]), 16)
            loads[j] = (slot, wsem[slot], wsem[slot].val)
        if qi == 0:
            @cx.block.sync
            def _(q):
                body(q)
        else:
            @cx.block.scalar
            def _(q):
                body(q)

    pp_step = r.get("_pp_step")
    for j in range(min(LA, len(items))):
        emit_load(j)
    gi = 0
    for (oc0, now) in groups:
        for ic in range(n_ic):
            j = gi
            gi += 1
            if pp_step is not None and (j % 2 == 0):
                pp_step()
            if j + LA < len(items):
                emit_load(j + LA)
            slot, ldeng, ldval = loads[j]

            @cx.block.tensor
            def _(t, oc0=oc0, now=now, ic=ic, slot=slot, ldeng=ldeng, ldval=ldval):
                cx.wait(t, "tensor", ldeng, ldval)
                cx.wait(t, "tensor", rhs_dep[0], rhs_dep[1])
                rhs = rhs_fn(ic)
                for oj in range(now):
                    if ic == 0:
                        fe, fv = bank_free[oj]
                        cx.wait(t, "tensor", fe, fv)
                    cx.inc(cx.pe, t.matmul(banks[oj][:, 0:ncol],
                                           wring[slot][:, oj * 128:(oj + 1) * 128],
                                           rhs, start=(ic == 0), stop=(ic == n_ic - 1)))
                wfree[slot] = cx.pe.val
        pev = cx.pe.val
        for oj in range(now):
            bank_free[oj] = evac_fn(oc0 + oj, banks[oj], pev)


def emit_ffn(cx, es, r, scr2, scales, wdone, x_dep):
    xn = r["xn_t"]
    h_sil, h_b, h_a, z1 = r["scrA"], r["scrB"], r["h_a"], r["z1"]
    ytmp = r["ytmp"]
    w1s, w2s = scales["w1_s"][0], scales["w2_s"][0]
    h1s, h2s = scales["h1_s"][0], scales["h2_s"][0]
    w3s = scales["w3_s"][0]

    for tt in range(NTT):
        t0 = tt * TT
        ln = emit_ln(cx, r, t0, TT, x_dep)
        xn_dep = (cx.dve, ln.dve_done)

        def rhs_xn(ic):
            return xn[:, ic * TT:(ic + 1) * TT]

        for g in range(HID // 128 // 8):
            def evac_w1(oc, bank, pev, g=g):
                @cx.block.scalar
                def _(sc):
                    cx.wait(sc, "scalar", cx.pe, pev)
                    oj = oc - g * 8
                    cx.inc(cx.act, sc.activation(h_sil[:, oj * TT:(oj + 1) * TT],
                                                 bank[:, 0:TT], AF.Silu,
                                                 scale=w1s[:, oc:oc + 1]))
                r["_w1_act"][oc] = cx.act.val
                return (cx.act, cx.act.val)

            emit_mm_stage(cx, r, scr2["w1"], HID, D, rhs_xn, xn_dep, evac_w1,
                          wdone["w1"], groups=[(g * 8, 8)], qrot=(0,))

            def evac_w2(oc, bank, pev, g=g):
                oj = oc - g * 8

                @cx.block.vector
                def _(v):
                    cx.wait(v, "vector", cx.pe, pev)
                    cx.inc(cx.dve, v.tensor_scalar(h_b[:, oj * TT:(oj + 1) * TT],
                                                   bank[:, 0:TT], w2s[:, oc:oc + 1],
                                                   None, ALU.mult))
                    cx.wait(v, "vector", cx.dve, cx.dve.val)
                    cx.wait(v, "vector", cx.act, r["_w1_act"][oc])
                    cx.inc(cx.dve, v.tensor_tensor(h_a[:, oc * TT:(oc + 1) * TT],
                                                   h_sil[:, oj * TT:(oj + 1) * TT],
                                                   h_b[:, oj * TT:(oj + 1) * TT],
                                                   ALU.mult))
                return (cx.dve, cx.dve.val)

            emit_mm_stage(cx, r, scr2["w2"], HID, D, rhs_xn, xn_dep, evac_w2,
                          wdone["w2"], groups=[(g * 8, 8)], qrot=(1,))
        h_a_dep = (cx.dve, cx.dve.val)

        def rhs_ha(ic):
            return h_a[:, ic * TT:(ic + 1) * TT]

        def evac_h1(oc, bank, pev):
            @cx.block.scalar
            def _(sc):
                cx.wait(sc, "scalar", cx.pe, pev)
                cx.inc(cx.act, sc.activation(z1[:, oc * TT:(oc + 1) * TT],
                                             bank[:, 0:TT], AF.Gelu,
                                             scale=h1s[:, oc:oc + 1]))
            return (cx.act, cx.act.val)

        emit_mm_stage(cx, r, scr2["h1"], HID, HID, rhs_ha, h_a_dep, evac_h1, wdone["h1"])
        z1_dep = (cx.act, cx.act.val)

        def rhs_z1(ic):
            return z1[:, ic * TT:(ic + 1) * TT]

        def evac_h2(oc, bank, pev):
            @cx.block.scalar
            def _(sc):
                cx.wait(sc, "scalar", cx.pe, pev)
                cx.inc(cx.act, sc.activation(h_a[:, oc * TT:(oc + 1) * TT],
                                             bank[:, 0:TT], AF.Gelu,
                                             scale=h2s[:, oc:oc + 1]))
            return (cx.act, cx.act.val)

        emit_mm_stage(cx, r, scr2["h2"], HID, HID, rhs_z1, z1_dep, evac_h2, wdone["h2"])
        z2_dep = (cx.act, cx.act.val)

        def evac_w3(oc, bank, pev, t0=t0):
            @cx.block.vector
            def _(v):
                cx.wait(v, "vector", cx.pe, pev)
                yb = ytmp[:, (oc % 2) * TT:(oc % 2) * TT + TT]
                cx.inc(cx.dve, v.tensor_scalar(yb, bank[:, 0:TT], w3s[:, oc:oc + 1],
                                               0.5, ALU.mult, ALU.mult))
                cx.wait(v, "vector", cx.dve, cx.dve.val)
                xs = r["x"][:, oc * T + t0: oc * T + t0 + TT]
                cx.inc(cx.dve, v.tensor_tensor(xs, xs, yb, ALU.add))
            return (cx.dve, cx.dve.val)

        emit_mm_stage(cx, r, scr2["w3"], D, HID, rhs_ha, z2_dep, evac_w3, wdone["w3"])
        x_dep = (cx.dve, cx.dve.val)
    return x_dep


def alloc_ffn(cx, es):
    r = {}
    r["h_a"] = sb(cx, es, "h_a", [128, 32 * TT], BF16)
    r["z1"] = sb(cx, es, "z1", [128, 32 * TT], BF16)
    r["xn_t"] = sb(cx, es, "xn_t", [128, NIC * TT], BF16)
    r["ytmp"] = sb(cx, es, "ytmp", [128, 2 * TT], F32)
    r["wslc"] = [sb(cx, es, f"wslc{i}", [128, 1024], BF16) for i in range(4)]
    r["wslc_sem"] = [cx.new_dma_sem(f"wsl{i}") for i in range(4)]
    r["wslc_free"] = [0, 0, 0, 0]
    r["wslc_next"] = [0]
    return r


def emit_load_x(cx, es, r, x_in, in_tm):
    x = r["x"]
    ld_x = r["ld_x"]
    if not in_tm:
        @cx.block.sync
        def _(q):
            cx.inc(ld_x, q.dma_start(
                out=x[:].rearrange("p (ic t) -> p ic t", ic=NIC),
                in_=x_in.rearrange("(ic p) t -> p ic t", p=128)), 16)
        return (ld_x, ld_x.val)

    stg = r["x_stg"]
    banks = r["ps_mm"]
    prev_dve = [0]
    for ti in range(T // 128):
        @cx.block.sync
        def _(q, ti=ti):
            cx.wait(q, "sync", cx.dve, prev_dve[0])
            cx.inc(ld_x, q.dma_start(out=stg[:, :],
                                     in_=x_in[ti * 128:(ti + 1) * 128, :]), 16)
        ldv = ld_x.val
        for half in range(2):
            bank = banks[4 + half]

            @cx.block.tensor
            def _(t, ti=ti, half=half, ldv=ldv, bank=bank):
                cx.wait(t, "tensor", ld_x, ldv)
                cx.wait(t, "tensor", r["ld_id"], 16)
                fe, fv = r["bank_free"][4 + half]
                cx.wait(t, "tensor", fe, fv)
                for q4 in range(4):
                    ic = half * 4 + q4
                    cx.inc(cx.pe, t.transpose(bank[:, q4 * 128:(q4 + 1) * 128],
                                              stg[:, ic * 128:(ic + 1) * 128],
                                              r["id32"][:, :]))
            pev = cx.pe.val

            @cx.block.vector
            def _(v, ti=ti, half=half, pev=pev, bank=bank):
                cx.wait(v, "vector", cx.pe, pev)
                for q4 in range(4):
                    ic = half * 4 + q4
                    cx.inc(cx.dve, v.tensor_copy(
                        x[:, ic * T + ti * 128: ic * T + (ti + 1) * 128],
                        bank[:, q4 * 128:(q4 + 1) * 128]))
            r["bank_free"][4 + half] = (cx.dve, cx.dve.val)
        prev_dve[0] = cx.dve.val
    return (cx.dve, cx.dve.val)


def emit_store_x(cx, es, r, x_out, out_tm, x_dep):
    x = r["x"]
    if not out_tm:
        @cx.block.sync
        def _(q):
            cx.wait(q, "sync", x_dep[0], x_dep[1])
            cx.inc(cx.st, q.dma_start(
                out=x_out.rearrange("(ic p) t -> p ic t", p=128),
                in_=x[:].rearrange("p (ic t) -> p ic t", ic=NIC)), 16)
            q.wait_ge(cx.st.sem, cx.st.val)
        return

    stg = r["x_stg"]
    banks = r["ps_mm"]
    st_prev = [cx.st.val]
    for ti in range(T // 128):
        dvev = 0
        for half in range(2):
            bank = banks[4 + half]

            @cx.block.tensor
            def _(t, ti=ti, half=half, bank=bank):
                cx.wait(t, "tensor", x_dep[0], x_dep[1])
                fe, fv = r["bank_free"][4 + half]
                cx.wait(t, "tensor", fe, fv)
                for q4 in range(4):
                    ic = half * 4 + q4
                    cx.inc(cx.pe, t.transpose(
                        bank[:, q4 * 128:(q4 + 1) * 128],
                        x[:, ic * T + ti * 128: ic * T + (ti + 1) * 128],
                        r["id32"][:, :]))
            pev = cx.pe.val

            @cx.block.vector
            def _(v, ti=ti, half=half, pev=pev, bank=bank):
                cx.wait(v, "vector", cx.pe, pev)
                cx.wait(v, "vector", cx.st, st_prev[0])
                for q4 in range(4):
                    ic = half * 4 + q4
                    cx.inc(cx.dve, v.tensor_copy(stg[:, ic * 128:(ic + 1) * 128],
                                                 bank[:, q4 * 128:(q4 + 1) * 128]))
            r["bank_free"][4 + half] = (cx.dve, cx.dve.val)
            dvev = cx.dve.val

        @cx.block.sync
        def _(q, ti=ti, dvev=dvev):
            cx.wait(q, "sync", cx.dve, dvev)
            cx.inc(cx.st, q.dma_start(out=x_out[ti * 128:(ti + 1) * 128, :],
                                      in_=stg[:, :]), 16)
        st_prev[0] = cx.st.val

    @cx.block.sync
    def _(q):
        q.wait_ge(cx.st.sem, cx.st.val)


def emit_hilo(cx, es, r, name, hi_ap, lo_ap, shape, rearr=None, out_rearr=None,
              out=None, dtype=F32):
    if out is None:
        out = sb(cx, es, name, shape, dtype)[:]
    scr = r["hl_scr"]
    F = int(np.prod(shape[1:]))

    @cx.block.sync
    def _(q):
        ha = rearr(hi_ap) if rearr else hi_ap[:, :]
        la = rearr(lo_ap) if rearr else lo_ap[:, :]
        cx.wait(q, "sync", cx.dve, r.get("_hl_dve", 0))
        cx.wait(q, "sync", cx.pe, r.get("_pro_pe_prev", 0))
        cx.wait_list(q, "sync", r.get("_hl_gate", []))
        sa = out_rearr(scr[:, 0:F]) if out_rearr else scr[:, 0:F]
        sb_ = out_rearr(scr[:, F:2 * F]) if out_rearr else scr[:, F:2 * F]
        with cx.nc.allow_non_contiguous_dma(reason="tiny one-time const load"):
            cx.inc(r["ld_pro"], q.dma_start(out=sa, in_=ha), 16)
            cx.inc(r["ld_pro"], q.dma_start(out=sb_, in_=la), 16)
    ldv = r["ld_pro"].val

    @cx.block.vector
    def _(v):
        cx.wait(v, "vector", r["ld_pro"], ldv)
        cx.wait_list(v, "vector", r.get("_hl_gate", []))
        cx.inc(cx.dve, v.tensor_tensor(out, scr[:, 0:F], scr[:, F:2 * F], ALU.add))
    r["_hl_dve"] = cx.dve.val
    return out, (cx.dve, cx.dve.val)


def alloc_retnet(cx, es):
    a = {}
    a["ld_hw"] = cx.new_dma_sem("ld_hw")
    a["ld_hw2"] = cx.new_dma_sem("ld_hw2")
    a["xn_full"] = sb(cx, es, "xn_full", [128, NIC * T], BF16)
    a["H"] = sb(cx, es, "rn_H", [128, RN_H * 2 * RN_DV], F32)
    a["H_bf"] = sb(cx, es, "rn_Hbf", [128, 2 * RN_DV], BF16)
    a["wq_t"] = sb(cx, es, "rn_wq", [128, NIC * 256], BF16)
    a["wk_t"] = sb(cx, es, "rn_wk", [128, NIC * 256], BF16)
    a["wv_t"] = sb(cx, es, "rn_wv", [128, NIC * 512], BF16)
    a["wg_t"] = sb(cx, es, "rn_wg", [128, NIC * 512], BF16)
    a["wo_t"] = sb(cx, es, "rn_wo", [128, 4 * 1024], BF16)
    a["q_bf"] = sb(cx, es, "rn_qbf", [128, 1024], BF16)
    a["k_bf"] = sb(cx, es, "rn_kbf", [128, 1024], BF16)
    a["k_f32"] = sb(cx, es, "rn_kf32", [128, 1024], F32)
    a["qt_bf"] = sb(cx, es, "rn_qtbf", [128, 256], BF16)
    a["k_tm"] = sb(cx, es, "rn_ktm", [128, 256], BF16)
    a["v_bf"] = sb(cx, es, "rn_vbf", [128, 512], BF16)
    a["g_si"] = sb(cx, es, "rn_gsi", [128, 2048], BF16)
    a["s_bf"] = sb(cx, es, "rn_sbf", [128, 128], BF16)
    a["o_bf"] = sb(cx, es, "rn_obf", [128, 2048], BF16)
    a["o_g"] = a["o_bf"]
    a["osq"] = sb(cx, es, "rn_osq", [128, 512], BF16)
    a["rrow"] = sb(cx, es, "rn_rrow", [1, 512], F32)
    return a


def emit_retnet(cx, es, r, a, scr2, gnw_fm, consts, wdone, x_dep, bias_fm=None):
    x = r["x"]
    xn = a["xn_full"]

    for tt in range(NTT):
        t0 = tt * TT
        emit_ln(cx, r, t0, TT, x_dep,
                xn_dst=lambda ic, t0=t0: xn[:, ic * T + t0: ic * T + t0 + TT])
    xn_dep = (cx.dve, cx.dve.val)
    a["_wt_pe_free"] = cx.pe.val

    @cx.block.gpsimd
    def _(gp):
        cx.wait_list(gp, "gpsimd", r.get("_pp_all", []))
        cx.wait(gp, "gpsimd", cx.dve, r.get("_hl_dve", 0))
        cx.inc(cx.pool, gp.memset(a["H"][:, :], 0.0))
    a["_H_memset"] = cx.pool.val

    bfree = list(r["bank_free"])
    for h in range(RN_H):
        @cx.block.sync
        def _(q, h=h):
            cx.wait_list(q, "sync", wdone["Wq"])
            cx.wait_list(q, "sync", wdone["Wk"])
            cx.wait_list(q, "sync", wdone["Wv"])
            cx.wait_list(q, "sync", r.get("_stage_gate", []))
            cx.wait(q, "sync", cx.pe, a.get("_wt_pe_free", 0))
            cx.inc(a["ld_hw"], q.dma_start(
                out=a["wq_t"][:].rearrange("p (ic o) -> p ic o", ic=NIC),
                in_=scr2["Wq"][:, h * 256:(h + 1) * 256].rearrange("(ic p) o -> p ic o", p=128)), 16)
            cx.inc(a["ld_hw"], q.dma_start(
                out=a["wk_t"][:].rearrange("p (ic o) -> p ic o", ic=NIC),
                in_=scr2["Wk"][:, h * 256:(h + 1) * 256].rearrange("(ic p) o -> p ic o", p=128)), 16)
            cx.inc(a["ld_hw"], q.dma_start(
                out=a["wv_t"][:].rearrange("p (ic o) -> p ic o", ic=NIC),
                in_=scr2["Wv"][:, h * 512:(h + 1) * 512].rearrange("(ic p) o -> p ic o", p=128)), 16)
        ld_qkv = a["ld_hw"].val

        @cx.block.scalar
        def _(q, h=h):
            cx.wait_list(q, "scalar", wdone["Wg"])
            cx.wait_list(q, "scalar", wdone["Wo"])
            cx.wait_list(q, "scalar", r.get("_stage_gate", []))
            cx.wait(q, "scalar", cx.pe, a.get("_wt_pe_free", 0))
            cx.inc(a["ld_hw2"], q.dma_start(
                out=a["wg_t"][:].rearrange("p (ic o) -> p ic o", ic=NIC),
                in_=scr2["Wg"][:, h * 512:(h + 1) * 512].rearrange("(ic p) o -> p ic o", p=128)), 16)
            cx.inc(a["ld_hw2"], q.dma_start(
                out=a["wo_t"][:].rearrange("p (kc o) -> p kc o", kc=4),
                in_=scr2["Wo"][h * 512:(h + 1) * 512, :].rearrange("(kc p) o -> p kc o", p=128)), 16)
        ld_go = a["ld_hw2"].val

        for tt in range(NTT):
            emit_retnet_tile(cx, r, a, h, tt, ld_qkv, ld_go, xn_dep, consts,
                             gnw_fm, bfree)
        a["_wt_pe_free"] = cx.pe.val

    r["bank_free"] = list(bfree)
    return (cx.dve, cx.dve.val)


def emit_retnet_tile(cx, r, a, h, tt, ld_qkv, ld_go, xn_dep, consts, gnw_fm,
                     bfree):
    x = r["x"]
    xn = a["xn_full"]
    banks = r["ps_mm"]
    t0 = tt * TT
    cos_f, sin_f = consts["cos"], consts["sin"]
    maskt_f, qsc_f = consts["maskt"], consts["qsc"]
    ksc_f, gc_f = consts["ksc"], consts["gc"]
    cdep = consts["dep"]
    ta = r["ln_tmp"][:, 0:512]
    tb = r["ln_tmp"][:, 512:1024]
    kf = a["k_f32"]

    def xn_t(ic):
        return xn[:, ic * T + t0: ic * T + t0 + TT]

    @cx.block.tensor
    def _(t):
        cx.wait(t, "tensor", a["ld_hw"], ld_qkv)
        cx.wait(t, "tensor", xn_dep[0], xn_dep[1])
        for bi in range(4):
            fe, fv = bfree[bi]
            cx.wait(t, "tensor", fe, fv)
        for sl in range(2):
            for ic in range(NIC):
                cx.inc(cx.pe, t.matmul(banks[sl][:, 0:512],
                                       a["wq_t"][:, ic * 256 + sl * 128: ic * 256 + (sl + 1) * 128],
                                       xn_t(ic), start=(ic == 0), stop=(ic == NIC - 1)))
        for sl in range(2):
            for ic in range(NIC):
                cx.inc(cx.pe, t.matmul(banks[2 + sl][:, 0:512],
                                       a["wk_t"][:, ic * 256 + sl * 128: ic * 256 + (sl + 1) * 128],
                                       xn_t(ic), start=(ic == 0), stop=(ic == NIC - 1)))
    pe_p1 = cx.pe.val

    ccol = cos_f[:, t0:t0 + TT]
    scol = sin_f[:, t0:t0 + TT]

    @cx.block.vector
    def _(v):
        cx.wait(v, "vector", cx.pe, pe_p1)
        cx.wait(v, "vector", cdep[0], cdep[1])
        cx.wait(v, "vector", cx.pe, a.get("_qk_pe_cons", 0))
        cx.wait(v, "vector", cx.dve, a.get("_tmp_dve", 0))
        for (bA, bB, dst) in ((banks[0], banks[1], a["q_bf"]),
                              (banks[2], banks[3], None)):
            cx.inc(cx.dve, v.tensor_tensor(ta, bA[:, 0:512], ccol, ALU.mult))
            cx.inc(cx.dve, v.tensor_tensor(tb, bB[:, 0:512], scol, ALU.mult))
            cx.wait(v, "vector", cx.dve, cx.dve.val)
            if dst is not None:
                cx.inc(cx.dve, v.tensor_tensor(dst[:, 0:512], ta, tb, ALU.subtract))
            else:
                cx.inc(cx.dve, v.tensor_tensor(kf[:, 0:512], ta, tb, ALU.subtract))
            cx.wait(v, "vector", cx.dve, cx.dve.val)
            cx.inc(cx.dve, v.tensor_tensor(ta, bA[:, 0:512], scol, ALU.mult))
            cx.inc(cx.dve, v.tensor_tensor(tb, bB[:, 0:512], ccol, ALU.mult))
            cx.wait(v, "vector", cx.dve, cx.dve.val)
            if dst is not None:
                cx.inc(cx.dve, v.tensor_tensor(dst[:, 512:1024], ta, tb, ALU.add))
            else:
                cx.inc(cx.dve, v.tensor_tensor(kf[:, 512:1024], ta, tb, ALU.add))
            cx.wait(v, "vector", cx.dve, cx.dve.val)
        cx.inc(cx.dve, v.tensor_copy(a["k_bf"][:, :], kf[:, :]))
    dve_rope = cx.dve.val
    a["_tmp_dve"] = dve_rope
    for bi in range(4):
        bfree[bi] = (cx.dve, dve_rope)

    @cx.block.tensor
    def _(t):
        cx.wait(t, "tensor", a["ld_hw2"], ld_go)
        cx.wait(t, "tensor", cx.dve, dve_rope)
        for sl in range(4):
            for ic in range(NIC):
                cx.inc(cx.pe, t.matmul(banks[sl][:, 0:512],
                                       a["wg_t"][:, ic * 512 + sl * 128: ic * 512 + (sl + 1) * 128],
                                       xn_t(ic), start=(ic == 0), stop=(ic == NIC - 1)))
    pe_g = cx.pe.val

    @cx.block.scalar
    def _(sc):
        cx.wait(sc, "scalar", cx.pe, pe_g)
        cx.wait(sc, "scalar", cx.dve, a.get("_gsi_dve_cons", 0))
        for sl in range(4):
            cx.inc(cx.act, sc.activation(a["g_si"][:, sl * 512:(sl + 1) * 512],
                                         banks[sl][:, 0:512], AF.Silu))
    act_g = cx.act.val
    for bi in range(4):
        bfree[bi] = (cx.act, act_g)

    b_v, b_k2, b_s6, b_o = banks[4], banks[5], banks[6], banks[7]
    o_v = a["o_bf"][:].rearrange("p (sl c t) -> p sl c t", sl=4, c=4)

    for cl in range(4):
        c = tt * 4 + cl
        cols = slice(t0 + cl * 128, t0 + (cl + 1) * 128)

        @cx.block.tensor
        def _(t, cl=cl):
            fe, fv = bfree[4]
            cx.wait(t, "tensor", fe, fv)
            for ic in range(NIC):
                cx.inc(cx.pe, t.matmul(b_v[:, 0:512],
                                       xn[:, ic * T + t0 + cl * 128: ic * T + t0 + (cl + 1) * 128],
                                       a["wv_t"][:, ic * 512:(ic + 1) * 512],
                                       start=(ic == 0), stop=(ic == NIC - 1)))
        pe_v = cx.pe.val

        @cx.block.scalar
        def _(sc, cl=cl):
            cx.wait(sc, "scalar", cx.pe, pe_v)
            cx.wait(sc, "scalar", cx.pe, a.get("_v_pe_cons", 0))
            cx.inc(cx.act, sc.activation(a["v_bf"][:, :], b_v[:, 0:512], AF.Copy))
        act_v = cx.act.val
        bfree[4] = (cx.act, act_v)

        @cx.block.tensor
        def _(t, cl=cl):
            fe, fv = bfree[5]
            cx.wait(t, "tensor", fe, fv)
            cx.wait(t, "tensor", cx.dve, dve_rope)
            for sl in range(2):
                cx.inc(cx.pe, t.transpose(
                    b_k2[:, sl * 128:(sl + 1) * 128],
                    kf[:, sl * 512 + cl * 128: sl * 512 + (cl + 1) * 128],
                    r["id32"][:, :]))
            fe, fv = bfree[6]
            cx.wait(t, "tensor", fe, fv)
            for sl in range(2):
                cx.inc(cx.pe, t.matmul(b_s6[:, 0:128],
                                       a["k_bf"][:, sl * 512 + cl * 128: sl * 512 + (cl + 1) * 128],
                                       a["q_bf"][:, sl * 512 + cl * 128: sl * 512 + (cl + 1) * 128],
                                       start=(sl == 0), stop=(sl == 1)))
        pe_kt = cx.pe.val

        @cx.block.vector
        def _(v, cl=cl):
            cx.wait(v, "vector", cx.pe, pe_kt)
            cx.wait(v, "vector", cx.pe, a.get("_ktm_pe_cons", 0))
            cx.inc(cx.dve, v.tensor_scalar(a["k_tm"][:, :], b_k2[:, 0:256],
                                           ksc_f[:, h:h + 1], None, ALU.mult))
            cx.inc(cx.dve, v.tensor_tensor(a["s_bf"][:, :], b_s6[:, 0:128],
                                           maskt_f[:, h * 128:(h + 1) * 128], ALU.mult))
            for sl in range(2):
                cx.inc(cx.dve, v.tensor_tensor(
                    a["qt_bf"][:, sl * 128:(sl + 1) * 128],
                    a["q_bf"][:, sl * 512 + cl * 128: sl * 512 + (cl + 1) * 128],
                    qsc_f[:, h * 128:(h + 1) * 128], ALU.mult))
        dve_sc = cx.dve.val
        bfree[5] = (cx.dve, dve_sc)
        bfree[6] = (cx.dve, dve_sc)

        @cx.block.tensor
        def _(t, cl=cl, c=c):
            cx.wait(t, "tensor", cx.dve, dve_sc)
            cx.wait(t, "tensor", cx.act, act_v)
            fe, fv = bfree[7]
            cx.wait(t, "tensor", fe, fv)
            if c > 0:
                cx.wait(t, "tensor", cx.dve, a["_Hbf_dve"])
            for sl in range(4):
                cx.inc(cx.pe, t.matmul(b_o[:, sl * 128:(sl + 1) * 128],
                                       a["v_bf"][:, sl * 128:(sl + 1) * 128],
                                       a["s_bf"][:, :], start=True, stop=(c == 0)))
                if c > 0:
                    for kc in range(2):
                        cx.inc(cx.pe, t.matmul(
                            b_o[:, sl * 128:(sl + 1) * 128],
                            a["H_bf"][:, kc * 512 + sl * 128: kc * 512 + (sl + 1) * 128],
                            a["qt_bf"][:, kc * 128:(kc + 1) * 128],
                            start=False, stop=(kc == 1)))
        pe_o = cx.pe.val
        a["_ktm_pe_cons"] = pe_o

        @cx.block.tensor
        def _(t, cl=cl, c=c):
            cx.wait(t, "tensor", cx.dve, dve_sc)
            cx.wait(t, "tensor", cx.act, act_v)
            if c > 0:
                cx.wait(t, "tensor", cx.dve, a["_Hbf_dve"])
            cx.inc(cx.pe, t.matmul(b_v[:, 0:512], a["k_tm"][:, 0:128],
                                   a["v_bf"][:, :], start=True, stop=True))
            cx.inc(cx.pe, t.matmul(b_k2[:, 0:512], a["k_tm"][:, 128:256],
                                   a["v_bf"][:, :], start=True, stop=True))
        pe_st = cx.pe.val
        a["_v_pe_cons"] = pe_st

        @cx.block.vector
        def _(v, c=c):
            cx.wait(v, "vector", cx.pe, pe_st)
            cx.wait(v, "vector", cx.pool, a["_H_memset"])
            for kc, bk in ((0, b_v), (1, b_k2)):
                H_sl = a["H"][:, (h * 2 + kc) * RN_DV:(h * 2 + kc + 1) * RN_DV]
                cx.inc(cx.dve, v.tensor_scalar(H_sl, H_sl, gc_f[:, h:h + 1], None, ALU.mult))
                cx.wait(v, "vector", cx.dve, cx.dve.val)
                cx.inc(cx.dve, v.tensor_tensor(H_sl, H_sl, bk[:, 0:512], ALU.add))
                cx.wait(v, "vector", cx.dve, cx.dve.val)
        a["_H_dve"] = cx.dve.val
        bfree[4] = (cx.dve, a["_H_dve"])
        bfree[5] = (cx.dve, a["_H_dve"])

        @cx.block.vector
        def _(v, c=c):
            cx.wait(v, "vector", cx.pe, pe_o)
            for kc in range(2):
                H_sl = a["H"][:, (h * 2 + kc) * RN_DV:(h * 2 + kc + 1) * RN_DV]
                cx.inc(cx.dve, v.tensor_copy(a["H_bf"][:, kc * RN_DV:(kc + 1) * RN_DV], H_sl))
        a["_Hbf_dve"] = cx.dve.val

        @cx.block.scalar
        def _(sc, cl=cl):
            cx.wait(sc, "scalar", cx.pe, pe_o)
            cx.wait(sc, "scalar", cx.dve, a.get("_ob_dve_cons", 0))
            cx.inc(cx.act, sc.activation(o_v[:, :, cl, :], b_o[:, 0:512], AF.Copy))
        bfree[7] = (cx.act, cx.act.val)
    act_o_all = cx.act.val

    stats = b_s6
    @cx.block.scalar
    def _(sc):
        cx.wait(sc, "scalar", cx.act, act_o_all)
        pass
    for sl in range(4):
        @cx.block.scalar
        def _(sc, sl=sl):
            cx.wait(sc, "scalar", cx.act, act_o_all)
            cx.wait(sc, "scalar", cx.pe, a.get("_osq_pe_cons", 0))
            cx.inc(cx.act, sc.activation(a["osq"][:, :],
                                         a["o_bf"][:, sl * 512:(sl + 1) * 512], AF.Square))
        act_sq = cx.act.val

        @cx.block.tensor
        def _(t, sl=sl):
            cx.wait(t, "tensor", cx.act, act_sq)
            if sl == 0:
                fe, fv = bfree[6]
                cx.wait(t, "tensor", fe, fv)
            cx.inc(cx.pe, t.matmul(stats[0:1, 0:512], r["ones_bf"][:, :],
                                   a["osq"][:, :], start=(sl == 0), stop=(sl == 3)))
        a["_osq_pe_cons"] = cx.pe.val
    pe_stats = cx.pe.val

    @cx.block.vector
    def _(v):
        cx.wait(v, "vector", cx.pe, pe_stats)
        cx.inc(cx.dve, v.tensor_scalar(stats[0:1, 0:512], stats[0:1, 0:512],
                                       1.0 / RN_DV, None, ALU.mult))
    dve_ms = cx.dve.val

    @cx.block.scalar
    def _(sc):
        cx.wait(sc, "scalar", cx.dve, dve_ms)
        cx.wait(sc, "scalar", cx.pe, a.get("_rrow_pe_cons", 0))
        cx.inc(cx.act, sc.activation(a["rrow"][:, :], stats[0:1, 0:512], AF.Ln,
                                     bias=r["eps_t"][:, :]))
        cx.wait(sc, "scalar", cx.act, cx.act.val)
        cx.inc(cx.act, sc.activation(a["rrow"][:, :], a["rrow"][:, :], AF.Exp, scale=-0.5))
    act_rs = cx.act.val

    @cx.block.tensor
    def _(t):
        cx.wait(t, "tensor", cx.act, act_rs)
        cx.inc(cx.pe, t.matmul(stats[:, 0:512], r["ones_f1"][:, :], a["rrow"][:, :],
                               start=True, stop=True))
    pe_bc = cx.pe.val
    a["_rrow_pe_cons"] = pe_bc

    @cx.block.vector
    def _(v):
        cx.wait(v, "vector", cx.pe, pe_bc)
        cx.wait(v, "vector", cx.act, act_g)
        cx.wait(v, "vector", cx.dve, a.get("_tmp_dve", 0))
        for sl in range(4):
            cx.inc(cx.dve, v.tensor_tensor(ta, a["o_bf"][:, sl * 512:(sl + 1) * 512],
                                           stats[:, 0:512], ALU.mult))
            cx.wait(v, "vector", cx.dve, cx.dve.val)
            cx.inc(cx.dve, v.tensor_scalar(ta, ta, gnw_fm[:, sl:sl + 1], None, ALU.mult))
            cx.wait(v, "vector", cx.dve, cx.dve.val)
            cx.inc(cx.dve, v.tensor_tensor(a["o_g"][:, sl * 512:(sl + 1) * 512],
                                           ta, a["g_si"][:, sl * 512:(sl + 1) * 512],
                                           ALU.mult))
            cx.wait(v, "vector", cx.dve, cx.dve.val)
    dve_gate = cx.dve.val
    a["_tmp_dve"] = dve_gate
    a["_gsi_dve_cons"] = dve_gate
    a["_ob_dve_cons"] = dve_gate
    bfree[6] = (cx.dve, dve_gate)

    y_banks = [banks[0], banks[1]]
    y_free = [bfree[0], bfree[1]]
    for oc in range(NIC):
        bk = y_banks[oc % 2]

        @cx.block.tensor
        def _(t, oc=oc, bk=bk):
            cx.wait(t, "tensor", cx.dve, dve_gate)
            fe, fv = y_free[oc % 2]
            cx.wait(t, "tensor", fe, fv)
            for sl in range(4):
                cx.inc(cx.pe, t.matmul(bk[:, 0:512],
                                       a["wo_t"][:, sl * 1024 + oc * 128: sl * 1024 + (oc + 1) * 128],
                                       a["o_g"][:, sl * 512:(sl + 1) * 512],
                                       start=(sl == 0), stop=(sl == 3)))
        pev = cx.pe.val

        @cx.block.vector
        def _(v, oc=oc, bk=bk, pev=pev):
            cx.wait(v, "vector", cx.pe, pev)
            xs = x[:, oc * T + t0: oc * T + t0 + TT]
            cx.inc(cx.dve, v.tensor_tensor(xs, xs, bk[:, 0:512], ALU.add))
        y_free[oc % 2] = (cx.dve, cx.dve.val)
    bfree[0] = y_free[0]
    bfree[1] = y_free[1]
    a["_qk_pe_cons"] = cx.pe.val
    a["_osq_pe_cons"] = a.get("_osq_pe_cons", 0)
    if r.get("_pp_step") is not None:
        for _ in range(24):
            r["_pp_step"]()


def build_ffn_prog(in_tm, out_tm):
    nc = bass.Bass("TRN2", target_bir_lowering=False, debug=False)
    x_in = nc.dram_tensor("x_in", [T, D] if in_tm else [D, T], F32,
                          kind="ExternalInput").ap()
    x_out = nc.dram_tensor("x_out", [T, D] if out_tm else [D, T], F32,
                           kind="ExternalOutput").ap()
    w, sv, scr1, scr2 = {}, {}, {}, {}
    for nme, shape in FFN_W_SHAPES:
        w[nme] = nc.dram_tensor(nme, list(shape), F32, kind="ExternalInput").ap()
        scr1[nme] = nc.dram_tensor("scr1_" + nme, list(shape), BF16, kind="Internal").ap()
        scr2[nme] = nc.dram_tensor("scr2_" + nme, [shape[1], shape[0]], BF16,
                                   kind="Internal").ap()
        sv[nme + "_s"] = nc.dram_tensor(nme + "_s", [shape[0]], F32,
                                        kind="ExternalInput").ap()
    id32 = nc.dram_tensor("ident_f32", [128, 128], F32, kind="ExternalInput").ap()

    with ExitStack() as es:
        cx = Ctx(nc, es)
        r = alloc_common(cx, es)
        r.update(alloc_ffn(cx, es))

        @cx.block.sync
        def _(q):
            cx.inc(r["ld_id"], q.dma_start(out=r["id32"][:, :], in_=id32[:, :]), 16)

        x_dep = emit_load_x(cx, es, r, x_in, in_tm)

        wdone = emit_prepass(cx, es, r, [(n, w[n], scr1[n], scr2[n])
                                         for n, _ in FFN_W_SHAPES])
        scales = {}
        for nme, shape in FFN_W_SHAPES:
            scales[nme + "_s"] = emit_load_scale_fm(cx, es, r, nme + "_s",
                                                    sv[nme + "_s"], shape[0])

        x_dep = emit_ffn(cx, es, r, scr2, scales, wdone, x_dep)
        emit_store_x(cx, es, r, x_out, out_tm, x_dep)
    return nc


RN_W_SHAPES = [("Wq", (D, D)), ("Wk", (D, D)), ("Wv", (2 * D, D)),
               ("Wg", (2 * D, D)), ("Wo", (D, 2 * D))]

RN_CONST_NAMES = ["rope_cos", "rope_sin", "rn_maskt", "rn_qsc", "rn_ksc", "rn_gc"]


def build_retnet_prog(in_tm=False, out_tm=False):
    nc = bass.Bass("TRN2", target_bir_lowering=False, debug=False)
    x_in = nc.dram_tensor("x_in", [T, D] if in_tm else [D, T], F32,
                          kind="ExternalInput").ap()
    x_out = nc.dram_tensor("x_out", [T, D] if out_tm else [D, T], F32,
                           kind="ExternalOutput").ap()
    w, scr1, scr2 = {}, {}, {}
    for nme, shape in RN_W_SHAPES:
        w[nme] = nc.dram_tensor(nme, list(shape), F32, kind="ExternalInput").ap()
        scr1[nme] = nc.dram_tensor("scr1_" + nme, list(shape), BF16, kind="Internal").ap()
        scr2[nme] = nc.dram_tensor("scr2_" + nme, [shape[1], shape[0]], BF16,
                                   kind="Internal").ap()
    gnw = nc.dram_tensor("g_norm_w", [RN_DV], F32, kind="ExternalInput").ap()
    id32 = nc.dram_tensor("ident_f32", [128, 128], F32, kind="ExternalInput").ap()
    cst = {}
    for nme in RN_CONST_NAMES:
        for p in ("_hi", "_lo"):
            shp = {"rope_cos": [128, T], "rope_sin": [128, T],
                   "rn_maskt": [RN_H * 128, 128], "rn_qsc": [RN_H * 128, 128],
                   "rn_ksc": [RN_H * 128, 1], "rn_gc": [RN_H * 128, 1]}[nme]
            cst[nme + p] = nc.dram_tensor(nme + p, shp, F32, kind="ExternalInput").ap()

    with ExitStack() as es:
        cx = Ctx(nc, es)
        r = alloc_common(cx, es)
        a = alloc_retnet(cx, es)
        r["hl_scr"] = a["H"][:, 0:4096]

        @cx.block.sync
        def _(q):
            cx.inc(r["ld_id"], q.dma_start(out=r["id32"][:, :], in_=id32[:, :]), 16)

        @cx.block.vector
        def _(v):
            cx.wait(v, "vector", r["ld_id"], 16)
            cx.inc(cx.dve, v.tensor_copy(r["id_bf"][:, :], r["id32"][:, :]))

        x_dep = emit_load_x(cx, es, r, x_in, in_tm)
        pp_rings = [a["xn_full"][:, 0:4096], a["xn_full"][:, 4096:8192]]
        wdone = emit_prepass(cx, es, r, [(n, w[n], scr1[n], scr2[n])
                                         for n, _ in RN_W_SHAPES], rings=pp_rings)
        r["_ln_gate"] = r["_pp_all"]
        gnw_fm, gnw_dep = emit_load_scale_fm(cx, es, r, "gnw", gnw, RN_DV)

        hp = lambda ap: ap.rearrange("(h p) c -> p h c", p=128)
        hv = lambda ap: ap.rearrange("p (h c) -> p h c", h=RN_H)
        consts = {}
        consts["cos"], _ = emit_hilo(cx, es, r, "cos_f", cst["rope_cos_hi"],
                                     cst["rope_cos_lo"], [128, T])
        consts["sin"], _ = emit_hilo(cx, es, r, "sin_f", cst["rope_sin_hi"],
                                     cst["rope_sin_lo"], [128, T])
        consts["maskt"], _ = emit_hilo(cx, es, r, "maskt_f", cst["rn_maskt_hi"],
                                       cst["rn_maskt_lo"], [128, RN_H * 128],
                                       rearr=hp, out_rearr=hv)
        consts["qsc"], _ = emit_hilo(cx, es, r, "qsc_f", cst["rn_qsc_hi"],
                                     cst["rn_qsc_lo"], [128, RN_H * 128],
                                     rearr=hp, out_rearr=hv)
        consts["ksc"], _ = emit_hilo(cx, es, r, "ksc_f", cst["rn_ksc_hi"],
                                     cst["rn_ksc_lo"], [128, RN_H],
                                     rearr=hp, out_rearr=hv)
        consts["gc"], dep = emit_hilo(cx, es, r, "gc_f", cst["rn_gc_hi"],
                                      cst["rn_gc_lo"], [128, RN_H],
                                      rearr=hp, out_rearr=hv)
        consts["dep"] = (cx.dve, max(dep[1], gnw_dep[1]))

        x_dep = emit_retnet(cx, es, r, a, scr2, gnw_fm, consts, wdone, x_dep)
        emit_store_x(cx, es, r, x_out, out_tm, x_dep)
    return nc


WA_W_SHAPES = [("Wq", (D, D)), ("Wk", (D, D)), ("Wv", (D, D)), ("Wo", (D, D))]
WA_CONST_NAMES = ["w_mask_int", "w_mask_first", "w_mask_last"]
NST = T // 128


def alloc_window(cx, es):
    a = {}
    a["ld_hw"] = cx.new_dma_sem("wa_ld_hw")
    a["xn_full"] = sb(cx, es, "xn_full", [128, NIC * T], BF16)
    a["k_fm"] = sb(cx, es, "wa_kfm", [128, T], BF16)
    a["v_tm"] = sb(cx, es, "wa_vtm", [128, T], BF16)
    a["o_fm"] = sb(cx, es, "wa_ofm", [128, T], BF16)
    a["q_bf"] = sb(cx, es, "wa_qbf", [128, 128], BF16)
    a["s_m"] = sb(cx, es, "wa_sm", [128, 384], F32)
    a["p_f"] = sb(cx, es, "wa_pf", [128, 384], F32)
    a["p_n"] = sb(cx, es, "wa_pn", [128, 384], F32)
    a["pt_bf"] = sb(cx, es, "wa_ptbf", [128, 384], BF16)
    a["rows"] = sb(cx, es, "wa_rows", [128, 3], F32)
    a["wq_t"] = sb(cx, es, "wa_wq", [128, NIC * 128], BF16)
    a["wk_t"] = sb(cx, es, "wa_wk", [128, NIC * 128], BF16)
    a["wv_t"] = sb(cx, es, "wa_wv", [128, NIC * 128], BF16)
    a["wo_t"] = sb(cx, es, "wa_wo", [128, D], BF16)
    return a


def emit_window(cx, es, r, a, scr2, masks, bo_fm, wdone, x_dep):
    x = r["x"]
    xn = a["xn_full"]
    banks = r["ps_mm"]

    for tt in range(NTT):
        t0 = tt * TT
        ln = emit_ln(cx, r, t0, TT, x_dep,
                     xn_dst=lambda ic, t0=t0: xn[:, ic * T + t0: ic * T + t0 + TT])
    xn_dep = (cx.dve, cx.dve.val)
    bfree = list(r["bank_free"])
    b_p0, b_p1, b_s, b_t, b_pv, b_y0, b_y1, b_r = banks

    @cx.block.vector
    def _(v):
        cx.wait(v, "vector", xn_dep[0], xn_dep[1])
        cx.wait_list(v, "vector", bo_fm[1:])
        for oc in range(NIC):
            cx.inc(cx.dve, v.tensor_scalar(x[:, oc * T:(oc + 1) * T],
                                           x[:, oc * T:(oc + 1) * T],
                                           bo_fm[0][:, oc:oc + 1], None, ALU.add))
    bias_done = cx.dve.val

    for hp in range(8):
        @cx.block.sync
        def _(q, hp=hp):
            for nme in ("Wq", "Wk", "Wv", "Wo"):
                cx.wait_list(q, "sync", wdone[nme])
            cx.wait_list(q, "sync", r.get("_stage_gate", []))
            cx.wait(q, "sync", cx.pe, a.get("_wt_pe_free", 0))
            for nme, dst in (("Wq", a["wq_t"]), ("Wk", a["wk_t"]), ("Wv", a["wv_t"])):
                cx.inc(a["ld_hw"], q.dma_start(
                    out=dst[:].rearrange("p (ic o) -> p ic o", ic=NIC),
                    in_=scr2[nme][:, hp * 128:(hp + 1) * 128].rearrange(
                        "(ic p) o -> p ic o", p=128)), 16)
            cx.inc(a["ld_hw"], q.dma_start(
                out=a["wo_t"][:, :], in_=scr2["Wo"][hp * 128:(hp + 1) * 128, :]), 16)
        ld_w = a["ld_hw"].val

        kv_banks = [b_p0, b_p1]
        kv_free = [bfree[0], bfree[1]]
        for tc in range(4):
            bk = kv_banks[tc % 2]

            @cx.block.tensor
            def _(t, hp=hp, tc=tc, bk=bk, ld_w=ld_w):
                cx.wait(t, "tensor", a["ld_hw"], ld_w)
                cx.wait(t, "tensor", xn_dep[0], xn_dep[1])
                fe, fv = kv_free[tc % 2]
                cx.wait(t, "tensor", fe, fv)
                for ic in range(NIC):
                    cx.inc(cx.pe, t.matmul(
                        bk[:, 0:512], a["wk_t"][:, ic * 128:(ic + 1) * 128],
                        xn[:, ic * T + tc * 512: ic * T + (tc + 1) * 512],
                        start=(ic == 0), stop=(ic == NIC - 1)))
            pev = cx.pe.val

            @cx.block.scalar
            def _(sc, tc=tc, bk=bk, pev=pev):
                cx.wait(sc, "scalar", cx.pe, pev)
                cx.inc(cx.act, sc.activation(a["k_fm"][:, tc * 512:(tc + 1) * 512],
                                             bk[:, 0:512], AF.Copy))
            kv_free[tc % 2] = (cx.act, cx.act.val)

        for tc4 in range(4):
            bk = kv_banks[tc4 % 2]

            @cx.block.tensor
            def _(t, hp=hp, tc4=tc4, bk=bk, ld_w=ld_w):
                fe, fv = kv_free[tc4 % 2]
                cx.wait(t, "tensor", fe, fv)
                for j in range(4):
                    tc = tc4 * 4 + j
                    for ic in range(NIC):
                        cx.inc(cx.pe, t.matmul(
                            bk[:, j * 128:(j + 1) * 128],
                            xn[:, ic * T + tc * 128: ic * T + (tc + 1) * 128],
                            a["wv_t"][:, ic * 128:(ic + 1) * 128],
                            start=(ic == 0), stop=(ic == NIC - 1)))
            pev = cx.pe.val

            @cx.block.scalar
            def _(sc, tc4=tc4, bk=bk, pev=pev):
                cx.wait(sc, "scalar", cx.pe, pev)
                cx.inc(cx.act, sc.activation(a["v_tm"][:, tc4 * 512:(tc4 + 1) * 512],
                                             bk[:, 0:512], AF.Copy))
            kv_free[tc4 % 2] = (cx.act, cx.act.val)
        bfree[0], bfree[1] = kv_free

        for st in range(NST):
            if st == 0:
                c0, W, mk = 0, 256, masks["first"]
            elif st == NST - 1:
                c0, W, mk = (st - 1) * 128, 256, masks["last"]
            else:
                c0, W, mk = (st - 1) * 128, 384, masks["int"]
            nblk = W // 128
            emit_window_subtile(cx, r, a, hp, st, c0, W, nblk, mk, bo_fm,
                                xn_dep, bfree)

        o_dep = a["_o_dep"]
        y_banks = [b_y0, b_y1]
        y_free = [bfree[5], bfree[6]]
        for oc in range(NIC):
            for tc in range(4):
                bk = y_banks[(oc * 4 + tc) % 2]

                @cx.block.tensor
                def _(t, hp=hp, oc=oc, tc=tc, bk=bk, o_dep=o_dep):
                    cx.wait(t, "tensor", o_dep[0], o_dep[1])
                    fe, fv = y_free[(oc * 4 + tc) % 2]
                    cx.wait(t, "tensor", fe, fv)
                    cx.inc(cx.pe, t.matmul(bk[:, 0:512],
                                           a["wo_t"][:, oc * 128:(oc + 1) * 128],
                                           a["o_fm"][:, tc * 512:(tc + 1) * 512],
                                           start=True, stop=True))
                pev = cx.pe.val

                @cx.block.vector
                def _(v, hp=hp, oc=oc, tc=tc, bk=bk, pev=pev):
                    cx.wait(v, "vector", cx.pe, pev)
                    cx.wait(v, "vector", cx.dve, bias_done)
                    xs = x[:, oc * T + tc * 512: oc * T + (tc + 1) * 512]
                    cx.inc(cx.dve, v.tensor_tensor(xs, xs, bk[:, 0:512], ALU.add))
                y_free[(oc * 4 + tc) % 2] = (cx.dve, cx.dve.val)
        bfree[5], bfree[6] = y_free
        a["_wt_pe_free"] = cx.pe.val

    r["bank_free"] = list(bfree)
    return (cx.dve, cx.dve.val)


def emit_window_subtile(cx, r, a, hp, st, c0, W, nblk, mk, bo_fm, xn_dep, bfree):
    xn = a["xn_full"]
    banks = r["ps_mm"]
    b_p0, b_p1, b_s, b_t, b_pv, b_y0, b_y1, b_r = banks
    mk_tile, mk_dep = mk

    @cx.block.tensor
    def _(t):
        fe, fv = bfree[7]
        cx.wait(t, "tensor", fe, fv)
        for ic in range(NIC):
            cx.inc(cx.pe, t.matmul(b_r[:, 0:128],
                                   a["wq_t"][:, ic * 128:(ic + 1) * 128],
                                   xn[:, ic * T + st * 128: ic * T + (st + 1) * 128],
                                   start=(ic == 0), stop=(ic == NIC - 1)))
    pe_q = cx.pe.val

    @cx.block.vector
    def _(v):
        cx.wait(v, "vector", cx.pe, pe_q)
        cx.wait(v, "vector", cx.pe, a.get("_q_pe_cons", 0))
        cx.inc(cx.dve, v.tensor_scalar(a["q_bf"][:, :], b_r[:, 0:128],
                                       1.0 / 8.0, None, ALU.mult))
    q_dve = cx.dve.val
    bfree[7] = (cx.dve, q_dve)

    for h2 in range(2):
        @cx.block.tensor
        def _(t, h2=h2):
            cx.wait(t, "tensor", cx.dve, q_dve)
            fe, fv = bfree[2]
            cx.wait(t, "tensor", fe, fv)
            cx.inc(cx.pe, t.matmul(b_s[:, 0:W],
                                   a["q_bf"][h2 * 64:(h2 + 1) * 64, :],
                                   a["k_fm"][h2 * 64:(h2 + 1) * 64, c0:c0 + W],
                                   start=True, stop=True))
        pe_s = cx.pe.val
        a["_q_pe_cons"] = pe_s

        @cx.block.vector
        def _(v, h2=h2):
            cx.wait(v, "vector", cx.pe, pe_s)
            cx.wait(v, "vector", mk_dep[0], mk_dep[1])
            cx.wait(v, "vector", cx.act, a.get("_sm_act", 0))
            cx.inc(cx.dve, v.tensor_tensor(a["s_m"][:, 0:W], b_s[:, 0:W],
                                           mk_tile[:, 0:W], ALU.add))
            cx.wait(v, "vector", cx.dve, cx.dve.val)
            cx.inc(cx.dve, v.tensor_reduce(a["rows"][:, 0:1], a["s_m"][:, 0:W],
                                           AX.X, ALU.max, negate=True))
        dve_sm = cx.dve.val
        bfree[2] = (cx.dve, dve_sm)

        @cx.block.scalar
        def _(sc, h2=h2):
            cx.wait(sc, "scalar", cx.dve, dve_sm)
            cx.inc(cx.act, sc.activation(a["p_f"][:, 0:W], a["s_m"][:, 0:W], AF.Exp,
                                         bias=a["rows"][:, 0:1],
                                         accum_out=a["rows"][:, 1:2]))
        act_p = cx.act.val
        a["_sm_act"] = act_p

        @cx.block.vector
        def _(v, h2=h2):
            cx.wait(v, "vector", cx.act, act_p)
            cx.wait(v, "vector", cx.pe, a.get("_pn_pe", 0))
            cx.inc(cx.dve, v.reciprocal(a["rows"][:, 2:3], a["rows"][:, 1:2]))
            cx.wait(v, "vector", cx.dve, cx.dve.val)
            cx.inc(cx.dve, v.tensor_scalar(a["p_n"][:, 0:W], a["p_f"][:, 0:W],
                                           a["rows"][:, 2:3], None, ALU.mult))
        dve_pn = cx.dve.val

        @cx.block.tensor
        def _(t, h2=h2):
            cx.wait(t, "tensor", cx.dve, dve_pn)
            fe, fv = bfree[3]
            cx.wait(t, "tensor", fe, fv)
            for j in range(nblk):
                cx.inc(cx.pe, t.transpose(b_t[:, j * 128:(j + 1) * 128],
                                          a["p_n"][:, j * 128:(j + 1) * 128],
                                          r["id32"][:, :]))
        pe_pt = cx.pe.val
        a["_pn_pe"] = pe_pt

        @cx.block.vector
        def _(v, h2=h2):
            cx.wait(v, "vector", cx.pe, pe_pt)
            cx.wait(v, "vector", cx.pe, a.get("_pt_pe_cons", 0))
            cx.inc(cx.dve, v.tensor_copy(a["pt_bf"][:, 0:nblk * 128], b_t[:, 0:nblk * 128]))
        dve_pt = cx.dve.val
        bfree[3] = (cx.dve, dve_pt)

        @cx.block.tensor
        def _(t, h2=h2):
            cx.wait(t, "tensor", cx.dve, dve_pt)
            fe, fv = bfree[4]
            cx.wait(t, "tensor", fe, fv)
            for j in range(nblk):
                cx.inc(cx.pe, t.matmul(b_pv[0:64, 0:128],
                                       a["v_tm"][:, c0 + j * 128 + h2 * 64: c0 + j * 128 + (h2 + 1) * 64],
                                       a["pt_bf"][:, j * 128:(j + 1) * 128],
                                       start=(j == 0), stop=(j == nblk - 1)))
        pe_pv = cx.pe.val
        a["_pt_pe_cons"] = pe_pv

        @cx.block.scalar
        def _(sc, h2=h2, st=st):
            cx.wait(sc, "scalar", cx.pe, pe_pv)
            cx.inc(cx.act, sc.activation(
                a["o_fm"][h2 * 64:(h2 + 1) * 64, st * 128:(st + 1) * 128],
                b_pv[0:64, 0:128], AF.Copy))
        bfree[4] = (cx.act, cx.act.val)
    a["_o_dep"] = (cx.act, cx.act.val)


def build_window_prog(in_tm=False, out_tm=False):
    nc = bass.Bass("TRN2", target_bir_lowering=False, debug=False)
    x_in = nc.dram_tensor("x_in", [T, D] if in_tm else [D, T], F32,
                          kind="ExternalInput").ap()
    x_out = nc.dram_tensor("x_out", [T, D] if out_tm else [D, T], F32,
                           kind="ExternalOutput").ap()
    w, scr1, scr2 = {}, {}, {}
    for nme, shape in WA_W_SHAPES:
        w[nme] = nc.dram_tensor(nme, list(shape), F32, kind="ExternalInput").ap()
        scr1[nme] = nc.dram_tensor("scr1_" + nme, list(shape), BF16, kind="Internal").ap()
        scr2[nme] = nc.dram_tensor("scr2_" + nme, [shape[1], shape[0]], BF16,
                                   kind="Internal").ap()
    bo = nc.dram_tensor("bo", [D], F32, kind="ExternalInput").ap()
    id32 = nc.dram_tensor("ident_f32", [128, 128], F32, kind="ExternalInput").ap()
    cst = {}
    for nme in WA_CONST_NAMES:
        shp = [128, 384] if nme == "w_mask_int" else [128, 256]
        for p in ("_hi", "_lo"):
            cst[nme + p] = nc.dram_tensor(nme + p, shp, F32, kind="ExternalInput").ap()

    with ExitStack() as es:
        cx = Ctx(nc, es)
        r = alloc_common(cx, es)
        a = alloc_window(cx, es)
        r["hl_scr"] = sb(cx, es, "hl_scr", [128, 768], F32)

        @cx.block.sync
        def _(q):
            cx.inc(r["ld_id"], q.dma_start(out=r["id32"][:, :], in_=id32[:, :]), 16)

        x_dep = emit_load_x(cx, es, r, x_in, in_tm)
        pp_rings = [a["xn_full"][:, 0:4096], a["xn_full"][:, 4096:8192]]
        wdone = emit_prepass(cx, es, r, [(n, w[n], scr1[n], scr2[n])
                                         for n, _ in WA_W_SHAPES], rings=pp_rings)
        r["_ln_gate"] = r["_pp_all"]
        bo_fm = emit_load_scale_fm(cx, es, r, "bo", bo, D)
        masks = {}
        masks["int"] = emit_hilo(cx, es, r, "mk_int", cst["w_mask_int_hi"],
                                 cst["w_mask_int_lo"], [128, 384])
        masks["first"] = emit_hilo(cx, es, r, "mk_first", cst["w_mask_first_hi"],
                                   cst["w_mask_first_lo"], [128, 256])
        masks["last"] = emit_hilo(cx, es, r, "mk_last", cst["w_mask_last_hi"],
                                  cst["w_mask_last_lo"], [128, 256])

        x_dep = emit_window(cx, es, r, a, scr2, masks, bo_fm, wdone, x_dep)
        emit_store_x(cx, es, r, x_out, out_tm, x_dep)
    return nc


FUSED_FFN = {0: [("w1", (HID, D)), ("w2", (HID, D)), ("h1", (HID, HID)),
                 ("h2", (HID, HID)), ("w3", (D, HID))]}


def _stage_gate(cx, r):
    g = [(cx.pe, cx.pe.val), (cx.dve, cx.dve.val), (cx.act, cx.act.val),
         (cx.pool, cx.pool.val)]
    r["_stage_gate"] = g
    return g


def build_fused_prog(n_stages=6):
    nc = bass.Bass("TRN2", target_bir_lowering=False, debug=False)
    x_in = nc.dram_tensor("x_in", [T, D], F32, kind="ExternalInput").ap()
    x_out = nc.dram_tensor("x_out", [T, D], F32, kind="ExternalOutput").ap()
    id32 = nc.dram_tensor("ident_f32", [128, 128], F32, kind="ExternalInput").ap()

    wnames = []
    for li in (0, 1):
        for nme, shape in [("w1", (HID, D)), ("w2", (HID, D)), ("h1", (HID, HID)),
                           ("h2", (HID, HID)), ("w3", (D, HID))]:
            wnames.append((f"L{li}_ffn_{nme}_w", shape))
    for nme, shape in RN_W_SHAPES:
        wnames.append((f"L0_attn_{nme}", shape))
    for nme, shape in WA_W_SHAPES:
        wnames.append((f"L1_attn_{nme}", shape))
    w, scr1, scr2 = {}, {}, {}
    for nme, shape in wnames:
        w[nme] = nc.dram_tensor(nme, list(shape), F32, kind="ExternalInput").ap()
        scr1[nme] = nc.dram_tensor("s1_" + nme, list(shape), BF16, kind="Internal").ap()
        scr2[nme] = nc.dram_tensor("s2_" + nme, [shape[1], shape[0]], BF16,
                                   kind="Internal").ap()
    sv = {}
    for li in (0, 1):
        for nme in ("w1", "w2", "h1", "h2"):
            sv[f"L{li}_ffn_{nme}_s"] = nc.dram_tensor(f"L{li}_ffn_{nme}_s", [HID], F32,
                                                      kind="ExternalInput").ap()
        sv[f"L{li}_ffn_w3_s"] = nc.dram_tensor(f"L{li}_ffn_w3_s", [D], F32,
                                               kind="ExternalInput").ap()
    gnw = nc.dram_tensor("L0_attn_g_norm_w", [RN_DV], F32, kind="ExternalInput").ap()
    bo = nc.dram_tensor("L1_attn_bo", [D], F32, kind="ExternalInput").ap()
    cst = {}
    for nme in RN_CONST_NAMES:
        shp = {"rope_cos": [128, T], "rope_sin": [128, T],
               "rn_maskt": [RN_H * 128, 128], "rn_qsc": [RN_H * 128, 128],
               "rn_ksc": [RN_H * 128, 1], "rn_gc": [RN_H * 128, 1]}[nme]
        for p in ("_hi", "_lo"):
            cst[nme + p] = nc.dram_tensor(nme + p, shp, F32, kind="ExternalInput").ap()
    for nme in WA_CONST_NAMES:
        shp = [128, 384] if nme == "w_mask_int" else [128, 256]
        for p in ("_hi", "_lo"):
            cst[nme + p] = nc.dram_tensor(nme + p, shp, F32, kind="ExternalInput").ap()

    with ExitStack() as es:
        cx = Ctx(nc, es)
        A_bf = sb(cx, es, "A_bf", [128, 32768], BF16)
        A_f32 = sb(cx, es, "A_f32", [128, 6208], F32)
        B_bf = sb(cx, es, "B_bf", [128, 16384], BF16)
        B_f32 = sb(cx, es, "B_f32", [128, 2304], F32)

        r = {}
        r["x"] = sb(cx, es, "x_res", [128, NIC * T], F32)
        r["ln_rows"] = sb(cx, es, "ln_rows", [1, 2 * TT], F32)
        r["ones_bf"] = sb(cx, es, "ones_bf", [128, 1], BF16)
        r["ones_f1"] = sb(cx, es, "ones_f1", [1, 128], F32)
        r["eps_t"] = sb(cx, es, "eps_t", [1, 1], F32)
        r["id32"] = sb(cx, es, "id32", [128, 128], F32)
        r["x_stg"] = B_f32[:, 0:1024]
        r["ln_tmp"] = B_f32[:, 0:1024]
        r["ps_mm"] = [ps(cx, es, f"psb{i}", [128, 512]) for i in range(8)]
        r["bank_free"] = [(cx.pe, 0)] * 8
        r["ld_pro"] = cx.new_dma_sem("ld_pro")
        r["ld_id"] = cx.new_dma_sem("ld_id")
        r["ld_x"] = cx.new_dma_sem("ld_x")
        r["_w1_act"] = {}
        r["_ln_pe_prev"] = 0
        r["hl_scr"] = A_f32[:, 0:4096]

        @cx.block.gpsimd
        def _(gp):
            cx.inc(cx.pool, gp.memset(r["ones_bf"][:, :], 1.0))
            cx.inc(cx.pool, gp.memset(r["ones_f1"][:, :], 1.0))
            cx.inc(cx.pool, gp.memset(r["eps_t"][:, :], EPS))
        r["ones_dep"] = cx.pool.val

        @cx.block.sync
        def _(q):
            cx.inc(r["ld_id"], q.dma_start(out=r["id32"][:, :], in_=id32[:, :]), 16)

        x_dep = emit_load_x(cx, es, r, x_in, True)

        pp_rings = [sb(cx, es, f"ppw{i}", [128, 1024], BF16)[:] for i in range(3)]
        plan = []
        order = ([f"L0_ffn_{n}_w" for n in ("w1", "w2", "h1", "h2", "w3")] +
                 [f"L0_attn_{n}" for n, _ in RN_W_SHAPES] +
                 [f"L1_ffn_{n}_w" for n in ("w1", "w2", "h1", "h2", "w3")] +
                 [f"L1_attn_{n}" for n, _ in WA_W_SHAPES])
        for nme in order:
            plan.append((nme, w[nme], scr1[nme], scr2[nme]))
        pp = PrepassStream(cx, es, r, plan, pp_rings)
        r["_pp_step"] = lambda: pp.step(1)
        lim_l0 = next(i for i, st in enumerate(pp.steps) if st[0].startswith("L1"))
        pp.limit = lim_l0

        class WdoneView:
            def __init__(self, mapping):
                self.mapping = mapping

            def __getitem__(self, key):
                full = self.mapping[key]
                pp.drain(full)
                return pp.wdone(full)

        scales = {}
        for li in (0, 1):
            for nme in ("w1", "w2", "h1", "h2", "w3"):
                key = f"L{li}_ffn_{nme}_s"
                scales[key] = emit_load_scale_fm(cx, es, r, key, sv[key],
                                                 HID if nme != "w3" else D)
        gnw_fm, gnw_dep = emit_load_scale_fm(cx, es, r, "gnw", gnw, RN_DV)
        bo_fm = emit_load_scale_fm(cx, es, r, "bo", bo, D)

        def bind_ffn():
            r["h_a"] = A_bf[:, 0:16384]
            r["z1"] = A_bf[:, 16384:32768]
            r["xn_t"] = B_bf[:, 0:4096]
            r["wslc"] = [B_bf[:, 4096 + i * 1024:4096 + (i + 1) * 1024] for i in range(4)]
            r["scrA"] = B_bf[:, 8192:12288]
            r["scrB"] = B_bf[:, 12288:16384]
            r["ytmp"] = B_f32[:, 0:1024]

        if "wslc_sem" not in r:
            r["wslc_sem"] = [cx.new_dma_sem(f"wsl{i}") for i in range(4)]
            r["wslc_free"] = [0, 0, 0, 0]
            r["wslc_next"] = [0]

        def ffn_maps(li):
            s2 = {n: scr2[f"L{li}_ffn_{n}_w"] for n in ("w1", "w2", "h1", "h2", "w3")}
            sc = {n + "_s": scales[f"L{li}_ffn_{n}_s"] for n in ("w1", "w2", "h1", "h2", "w3")}
            wd = WdoneView({n: f"L{li}_ffn_{n}_w" for n in ("w1", "w2", "h1", "h2", "w3")})
            return s2, sc, wd

        a_rn = {
            "ld_hw": cx.new_dma_sem("rn_ld_hw"),
            "ld_hw2": cx.new_dma_sem("rn_ld_hw2"),
            "xn_full": A_bf[:, 0:16384],
            "H": A_f32[:, 0:4096],
            "H_bf": A_bf[:, 16384:17408],
            "wq_t": B_bf[:, 0:2048],
            "wk_t": B_bf[:, 2048:4096],
            "wv_t": B_bf[:, 4096:8192],
            "wg_t": B_bf[:, 8192:12288],
            "wo_t": B_bf[:, 12288:16384],
            "q_bf": A_bf[:, 21504:22528],
            "k_bf": A_bf[:, 22528:23552],
            "qt_bf": A_bf[:, 23552:23808],
            "k_tm": A_bf[:, 23808:24064],
            "v_bf": A_bf[:, 24064:24576],
            "g_si": A_bf[:, 24576:26624],
            "s_bf": A_bf[:, 26624:26752],
            "o_bf": A_bf[:, 26752:28800],
            "o_g": A_bf[:, 26752:28800],
            "osq": A_bf[:, 28800:29312],
            "k_f32": B_f32[:, 1024:2048],
            "rrow": A_f32[0:1, 5128:5640],
        }
        rn_scr2 = {n: scr2[f"L0_attn_{n}"] for n, _ in RN_W_SHAPES}
        rn_wdone = WdoneView({n: f"L0_attn_{n}" for n, _ in RN_W_SHAPES})

        a_wa = {
            "ld_hw": cx.new_dma_sem("wa_ld_hw"),
            "xn_full": A_bf[:, 0:16384],
            "k_fm": A_bf[:, 16384:18432],
            "v_tm": A_bf[:, 18432:20480],
            "o_fm": A_bf[:, 20480:22528],
            "q_bf": B_bf[:, 4480:4608],
            "pt_bf": B_bf[:, 4096:4480],
            "s_m": B_f32[:, 0:384],
            "p_f": B_f32[:, 384:768],
            "p_n": B_f32[:, 768:1152],
            "rows": A_f32[:, 6152:6155],
            "wq_t": B_bf[:, 0:1024],
            "wk_t": B_bf[:, 1024:2048],
            "wv_t": B_bf[:, 2048:3072],
            "wo_t": B_bf[:, 3072:4096],
        }
        wa_scr2 = {n: scr2[f"L1_attn_{n}"] for n, _ in WA_W_SHAPES}
        wa_wdone = WdoneView({n: f"L1_attn_{n}" for n, _ in WA_W_SHAPES})

        def bind_attn_scr():
            r["scrA"] = B_bf[:, 8192:12288]
            r["scrB"] = B_bf[:, 12288:16384]

        def bind_wa_scr():
            r["scrA"] = A_bf[:, 24192:28288]
            r["scrB"] = A_bf[:, 28288:32384]

        def stage_ffn(li):
            nonlocal x_dep
            bind_ffn()
            _stage_gate(cx, r)
            s2m, scm, wdm = ffn_maps(li)
            x_dep = emit_ffn(cx, es, r, s2m, scm, wdm, x_dep)

        def stage_rn():
            nonlocal x_dep
            pp.limit = None
            gate = _stage_gate(cx, r)
            r["_hl_gate"] = gate
            hp_re = lambda ap: ap.rearrange("(h p) c -> p h c", p=128)
            hv_re = lambda ap: ap.rearrange("p (h c) -> p h c", h=RN_H)
            consts = {}
            consts["cos"], _ = emit_hilo(cx, es, r, "cos_f", cst["rope_cos_hi"],
                                         cst["rope_cos_lo"], [128, T],
                                         out=A_bf[:, 17408:19456])
            consts["sin"], _ = emit_hilo(cx, es, r, "sin_f", cst["rope_sin_hi"],
                                         cst["rope_sin_lo"], [128, T],
                                         out=A_bf[:, 19456:21504])
            consts["maskt"], _ = emit_hilo(cx, es, r, "maskt_f", cst["rn_maskt_hi"],
                                           cst["rn_maskt_lo"], [128, RN_H * 128],
                                           rearr=hp_re, out_rearr=hv_re,
                                           out=A_f32[:, 4096:4608])
            consts["qsc"], _ = emit_hilo(cx, es, r, "qsc_f", cst["rn_qsc_hi"],
                                         cst["rn_qsc_lo"], [128, RN_H * 128],
                                         rearr=hp_re, out_rearr=hv_re,
                                         out=A_f32[:, 4608:5120])
            consts["ksc"], _ = emit_hilo(cx, es, r, "ksc_f", cst["rn_ksc_hi"],
                                         cst["rn_ksc_lo"], [128, RN_H],
                                         rearr=hp_re, out_rearr=hv_re,
                                         out=A_f32[:, 5120:5124])
            consts["gc"], cdep = emit_hilo(cx, es, r, "gc_f", cst["rn_gc_hi"],
                                           cst["rn_gc_lo"], [128, RN_H],
                                           rearr=hp_re, out_rearr=hv_re,
                                           out=A_f32[:, 5124:5128])
            consts["dep"] = (cx.dve, max(cdep[1], gnw_dep[1]))
            bind_attn_scr()
            _stage_gate(cx, r)
            x_dep = emit_retnet(cx, es, r, a_rn, rn_scr2, gnw_fm, consts, rn_wdone, x_dep)

        def stage_wa():
            nonlocal x_dep
            gate = _stage_gate(cx, r)
            r["_hl_gate"] = gate
            masks = {}
            masks["int"] = emit_hilo(cx, es, r, "mk_int", cst["w_mask_int_hi"],
                                     cst["w_mask_int_lo"], [128, 384],
                                     out=A_f32[:, 4096:4480])
            masks["first"] = emit_hilo(cx, es, r, "mk_first", cst["w_mask_first_hi"],
                                       cst["w_mask_first_lo"], [128, 256],
                                       out=A_f32[:, 4480:4736])
            masks["last"] = emit_hilo(cx, es, r, "mk_last", cst["w_mask_last_hi"],
                                      cst["w_mask_last_lo"], [128, 256],
                                      out=A_f32[:, 4736:4992])
            bind_wa_scr()
            _stage_gate(cx, r)
            x_dep = emit_window(cx, es, r, a_wa, wa_scr2, masks, bo_fm, wa_wdone, x_dep)

        stages = [lambda: stage_ffn(0), stage_rn, lambda: stage_ffn(0),
                  lambda: stage_ffn(1), stage_wa, lambda: stage_ffn(1)]
        for si in range(n_stages):
            stages[si]()

        emit_store_x(cx, es, r, x_out, True, x_dep)
    return nc


def fused_im(inputs):
    im = {"ident_f32": _CONSTS["ident_f32"]}
    for li in (0, 1):
        for nme in ("w1", "w2", "h1", "h2", "w3"):
            im[f"L{li}_ffn_{nme}_w"] = inputs[f"L{li}_ffn_{nme}_w"]
            im[f"L{li}_ffn_{nme}_s"] = inputs[f"L{li}_ffn_{nme}_s"]
    for nme, _ in RN_W_SHAPES:
        im["L0_attn_" + nme] = inputs["L0_attn_" + nme]
    im["L0_attn_g_norm_w"] = inputs["L0_attn_g_norm_w"]
    for nme, _ in WA_W_SHAPES:
        im["L1_attn_" + nme] = inputs["L1_attn_" + nme]
    im["L1_attn_bo"] = inputs["L1_attn_bo"]
    for nme in RN_CONST_NAMES + WA_CONST_NAMES:
        for p in ("_hi", "_lo"):
            im[nme + p] = _CONSTS[nme + p]
    return im


_PROGS = {}


def _prog(kind, in_tm, out_tm):
    key = (kind, in_tm, out_tm)
    if key not in _PROGS:
        if kind == "ffn":
            _PROGS[key] = build_ffn_prog(in_tm, out_tm)
        elif kind == "rn":
            _PROGS[key] = build_retnet_prog(in_tm, out_tm)
        else:
            _PROGS[key] = build_window_prog(in_tm, out_tm)
    return _PROGS[key]


def _ffn_im(inputs, layer):
    p = f"L{layer}_ffn_"
    im = {"ident_f32": _CONSTS["ident_f32"]}
    for nme in ("w1", "w2", "h1", "h2", "w3"):
        im[nme] = inputs[p + nme + "_w"]
        im[nme + "_s"] = inputs[p + nme + "_s"]
    return im


def _rn_im(inputs):
    im = {"ident_f32": _CONSTS["ident_f32"], "g_norm_w": inputs["L0_attn_g_norm_w"]}
    for nme in ("Wq", "Wk", "Wv", "Wg", "Wo"):
        im[nme] = inputs["L0_attn_" + nme]
    for nme in RN_CONST_NAMES:
        for pfx in ("_hi", "_lo"):
            im[nme + pfx] = _CONSTS[nme + pfx]
    return im


def _wa_im(inputs):
    im = {"ident_f32": _CONSTS["ident_f32"], "bo": inputs["L1_attn_bo"]}
    for nme in ("Wq", "Wk", "Wv", "Wo"):
        im[nme] = inputs["L1_attn_" + nme]
    for nme in WA_CONST_NAMES:
        for pfx in ("_hi", "_lo"):
            im[nme + pfx] = _CONSTS[nme + pfx]
    return im


def kernel(**inputs):
    x = np.ascontiguousarray(inputs["L0_attn_x"], dtype=np.float32)
    n = 8
    if "fused" not in _PROGS:
        _PROGS["fused"] = build_fused_prog()
    nc = _PROGS["fused"]
    wim = fused_im(inputs)
    in_maps = [dict(wim, x_in=np.ascontiguousarray(x[b])) for b in range(n)]
    res = run_bass_kernel_spmd(nc, in_maps, core_ids=list(range(n)))
    return np.stack([res.results[b]["x_out"] for b in range(n)], axis=0)


def kernel_unfused(**inputs):
    x = np.ascontiguousarray(inputs["L0_attn_x"], dtype=np.float32)
    n = 8
    shards = [np.ascontiguousarray(x[b]) for b in range(n)]

    seq = [
        ("ffn", _ffn_im(inputs, 0), True, False),
        ("rn", _rn_im(inputs), False, False),
        ("ffn", _ffn_im(inputs, 0), False, False),
        ("ffn", _ffn_im(inputs, 1), False, False),
        ("wa", _wa_im(inputs), False, False),
        ("ffn", _ffn_im(inputs, 1), False, True),
    ]
    cur = shards
    for (kind, wim, in_tm, out_tm) in seq:
        nc = _prog(kind, in_tm, out_tm)
        in_maps = [dict(wim, x_in=cur[b]) for b in range(n)]
        res = run_bass_kernel_spmd(nc, in_maps, core_ids=list(range(n)))
        cur = [np.ascontiguousarray(res.results[b]["x_out"]) for b in range(n)]
    return np.stack(cur, axis=0)
